# version 01
# baseline (speedup 1.0000x reference)
import numpy as np
import ml_dtypes
import concourse.bass as bass
import concourse.mybir as mybir
from concourse.bass_utils import run_bass_kernel_spmd
from contextlib import ExitStack

F32 = mybir.dt.float32
BF16 = mybir.dt.bfloat16
AF = mybir.ActivationFunctionType
ALU = mybir.AluOpType
AX = mybir.AxisListType.X

T, D, I, NH, HD = 4096, 1024, 2816, 16, 64
NT = T // 128
NC = D // 128
NI = I // 128
MAGIC = 12582912.0
DEBUG = False

ENGS = ("tensor", "vector", "scalar", "gpsimd", "sync")
DMA_KEYS = ("ld", "lg", "st")


class Prog:

    def __init__(self):
        self.streams = {e: [] for e in ENGS}
        self.cnt = {k: 0 for k in ("t", "v", "s", "g", "ld", "lg", "st")}
        self.pending = {e: [] for e in ENGS}

    def _emit(self, eng, fn, inc=None, waits=()):
        ws = {}
        for k, val in list(self.pending[eng]) + list(waits):
            if val and val > 0:
                ws[k] = max(ws.get(k, 0), val)
        self.pending[eng] = []
        self.streams[eng].append((fn, inc, sorted(ws.items())))
        if inc is not None:
            self.cnt[inc] += 16 if inc in DMA_KEYS else 1
            return self.cnt[inc]
        return None

    def t(self, fn, inc=None, waits=()):
        return self._emit("tensor", fn, inc, waits)

    def v(self, fn, inc=None, waits=()):
        return self._emit("vector", fn, inc, waits)

    def s(self, fn, inc=None, waits=()):
        return self._emit("scalar", fn, inc, waits)

    def g(self, fn, inc=None, waits=()):
        return self._emit("gpsimd", fn, inc, waits)

    def y(self, fn, inc=None, waits=()):
        return self._emit("sync", fn, inc, waits)

    def now(self):
        return dict(self.cnt)

    def barrier(self, engines=ENGS):
        snap = [(k, v) for k, v in self.cnt.items() if v > 0]
        for e in engines:
            self.pending[e].extend(snap)

    def build(self, nc):
        with ExitStack() as stk:
            sems = {k: stk.enter_context(nc.semaphore(f"sem_{k}")) for k in self.cnt}
            block = stk.enter_context(nc.Block())
            for eng_name in ENGS:
                stream = self.streams[eng_name]
                if not stream:
                    continue

                def make(stream_):
                    def run(eng):
                        for fn, inc, ws in stream_:
                            for k, val in ws:
                                eng.wait_ge(sems[k], val)
                            inst = fn(eng)
                            if inc is not None:
                                inst.then_inc(sems[inc], 16 if inc in DMA_KEYS else 1)
                    return run

                getattr(block, eng_name)(make(stream))


class Arena:
    def __init__(self, tensors):
        self.tensors = tensors
        self.off = {d: 0 for d in tensors}
        self.stack = []
        self.peak = {d: 0 for d in tensors}

    def push(self):
        self.stack.append(dict(self.off))

    def pop(self):
        self.off = self.stack.pop()

    def alloc(self, dtype, shape):
        cols = 1
        for s in shape[1:]:
            cols *= s
        esz = 2 if dtype == BF16 else 4
        o = self.off[dtype]
        o += (-(o * esz)) % 32 // esz
        self.off[dtype] = o + cols
        self.peak[dtype] = max(self.peak[dtype], self.off[dtype])
        tsr = self.tensors[dtype]
        assert self.off[dtype] <= tsr.shape[1], (
            f"arena overflow {dtype}: {self.off[dtype]} > {tsr.shape[1]}")
        ap = tsr[:][:, o:o + cols]
        if len(shape) > 2:
            names = " ".join(f"d{i}" for i in range(len(shape) - 1))
            ap = ap.rearrange(f"p ({names}) -> p {names}",
                              **{f"d{i}": shape[i + 1] for i in range(len(shape) - 1)})
        if shape[0] != 128:
            ap = ap[0:shape[0]]
        return ap


def emit_quant(p, eng_key, x_ap, out_ap, tmps, K, waits_first=(), inc_last=None):
    em = getattr(p, eng_key)
    NB = K // 16
    A, B = tmps["A"][:, :K], tmps["B"][:, :K]
    BM, SC = tmps["BM"][:, :NB], tmps["SC"][:, :NB]
    xv = x_ap.rearrange("p (b s) -> p b s", s=16)
    p.v(lambda e: e.tensor_reduce(BM, xv, axis=AX, op=ALU.max, apply_absolute_value=True),
        waits=waits_first)
    p.v(lambda e: e.tensor_scalar(SC, BM, 1.0 / 6.0, 1e-12, op0=ALU.mult, op1=ALU.max))
    p.spacer()
    vstat = p.v(lambda e: e.reciprocal(BM, SC), inc="v")
    if eng_key == "v":
        p.spacer()
    bmb = BM.broadcast_to([128, NB, 16])
    scb = SC.broadcast_to([128, NB, 16])
    tmps["_vstat"] = vstat
    em(lambda e: e.tensor_tensor(xv, xv, bmb, op=ALU.mult),
       waits=[("v", vstat)] if eng_key != "v" else ())
    em(lambda e: e.tensor_scalar(A, x_ap, 2.0, -2.0, op0=ALU.min, op1=ALU.max))
    em(lambda e: e.tensor_scalar(A, A, 2.0, MAGIC, op0=ALU.mult, op1=ALU.add))
    em(lambda e: e.tensor_scalar(A, A, MAGIC, 0.5, op0=ALU.subtract, op1=ALU.mult))
    em(lambda e: e.tensor_scalar(B, x_ap, 4.0, -4.0, op0=ALU.min, op1=ALU.max))
    em(lambda e: e.tensor_scalar(B, B, MAGIC, MAGIC, op0=ALU.add, op1=ALU.subtract))
    em(lambda e: e.tensor_tensor(A, A, B, op=ALU.add))
    em(lambda e: e.tensor_scalar(B, B, 2.0, -2.0, op0=ALU.min, op1=ALU.max))
    em(lambda e: e.tensor_tensor(A, A, B, op=ALU.subtract))
    em(lambda e: e.tensor_scalar(B, x_ap, 0.5, MAGIC, op0=ALU.mult, op1=ALU.add))
    em(lambda e: e.tensor_scalar(B, B, MAGIC, 2.0, op0=ALU.subtract, op1=ALU.mult))
    em(lambda e: e.tensor_tensor(A, A, B, op=ALU.add))
    em(lambda e: e.tensor_scalar(B, B, 4.0, -4.0, op0=ALU.min, op1=ALU.max))
    em(lambda e: e.tensor_tensor(A, A, B, op=ALU.subtract))
    av = A.rearrange("p (b s) -> p b s", s=16)
    ov = out_ap.rearrange("p (b s) -> p b s", s=16)
    return em(lambda e: e.tensor_tensor(ov, av, scb, op=ALU.mult), inc=inc_last)


def emit_wquant_transpose(p, w_src, n_rows, kk, dstT, wld, wqt, qtmp, psT, ident_sb, state):
    ntile = n_rows // 128
    nblk = kk // 128
    for r in range(ntile):
        lv = p.y(lambda e, r=r: e.dma_start(out=wld[:, 0:kk], in_=w_src[r * 128:(r + 1) * 128, :]),
                 inc="ld", waits=[("g", state["g"]), ("t", state["wt"]),
                                  ("v", qtmp.get("_vstat", 0))])
        state["g"] = emit_quant(p, "g", wld[:, 0:kk], wqt[:, 0:kk], qtmp, kk,
                                waits_first=[("ld", lv)], inc_last="g")
        for jb in range(0, nblk, 8):
            jn = min(8, nblk - jb)
            for j in range(jn):
                w = []
                if j == 0:
                    w = [("g", state["g"]), ("v", state["wv"])]
                tt = p.t(lambda e, j=j, jb=jb: e.transpose(
                    psT[:, j * 128:(j + 1) * 128], wqt[:, (jb + j) * 128:(jb + j + 1) * 128],
                    ident_sb), inc="t" if j == jn - 1 else None, waits=w)
            state["wt"] = tt
            dst = dstT[:, jb:jb + jn, r * 128:(r + 1) * 128]
            src = psT[:].rearrange("p (c q) -> p c q", q=128)[:, 0:jn, :]
            state["wv"] = p.v(lambda e, dst=dst, src=src: e.tensor_copy(dst, src),
                              inc="v", waits=[("t", tt)])


def build_attn_layer(layer):
    nc = bass.Bass("TRN2", target_bir_lowering=False)
    scale = {0: 0.125, 1: 0.03125, 2: 0.125}[layer]
    causal = layer == 2
    rotary = layer == 0
    add_xn = layer == 1

    x_in = nc.dram_tensor("x", [T, D], F32, kind="ExternalInput").ap()
    w_qkv = nc.dram_tensor("w_qkv", [D, 3 * D], F32, kind="ExternalInput").ap()
    b_pc = nc.dram_tensor("b_pc", [128, 24], F32, kind="ExternalInput").ap()
    bv_bc = nc.dram_tensor("bv_bc", [128, D], F32, kind="ExternalInput").ap()
    w_o = nc.dram_tensor("w_o", [D, D], F32, kind="ExternalInput").ap()
    bo_bc = nc.dram_tensor("bo_bc", [128, D], F32, kind="ExternalInput").ap()
    w_gate = nc.dram_tensor("w_gate", [I, D], F32, kind="ExternalInput").ap()
    w_up = nc.dram_tensor("w_up", [I, D], F32, kind="ExternalInput").ap()
    w_down = nc.dram_tensor("w_down", [D, I], F32, kind="ExternalInput").ap()
    ident = nc.dram_tensor("ident", [128, 128], BF16, kind="ExternalInput").ap()
    ones64 = nc.dram_tensor("ones64", [1, 64], BF16, kind="ExternalInput").ap()
    if rotary:
        perm_in = nc.dram_tensor("perm", [128, 128], BF16, kind="ExternalInput").ap()
        rotA = nc.dram_tensor("rotA", [128, T], BF16, kind="ExternalInput").ap()
        rotB = nc.dram_tensor("rotB", [128, T], BF16, kind="ExternalInput").ap()
    if causal:
        cmask = nc.dram_tensor("cmask", [4, 128, 512], BF16, kind="ExternalInput").ap()
    x_out = nc.dram_tensor("x_out", [T, D], F32, kind="ExternalOutput").ap()

    kind_dbg = "ExternalOutput" if DEBUG else "Internal"
    xn_h = nc.dram_tensor("xn_h", [T, D], F32, kind=kind_dbg).ap()
    qT_h = nc.dram_tensor("qT_h", [D, T], BF16, kind=kind_dbg).ap()
    kT_h = nc.dram_tensor("kT_h", [D, T], BF16, kind=kind_dbg).ap()
    vP_h = nc.dram_tensor("vP_h", [T, NH * 65], BF16, kind=kind_dbg).ap()
    hT_h = nc.dram_tensor("hT_h", [D, T], BF16, kind=kind_dbg).ap()
    s_h = nc.dram_tensor("s_h", [T, D], BF16, kind=kind_dbg).ap()
    hqT_h = nc.dram_tensor("hqT_h", [I, T], BF16, kind="Internal").ap()
    g_h = nc.dram_tensor("g_h", [T, D], BF16, kind=kind_dbg).ap() if DEBUG else None

    p = Prog()
    with ExitStack() as stk:
        BFA = stk.enter_context(nc.sbuf_tensor("bf_arena", [128, 75000], BF16))
        F32A = stk.enter_context(nc.sbuf_tensor("f32_arena", [128, 14000], F32))
        ar = Arena({BF16: BFA, F32: F32A})
        ps_big = stk.enter_context(nc.psum_tensor("ps_big", [128, 3072], F32))
        psT = stk.enter_context(nc.psum_tensor("psT", [128, 1024], BF16))

        ident_sb = ar.alloc(BF16, [128, 128])
        ones64_sb = ar.alloc(BF16, [1, 64])
        spc = ar.alloc(F32, [128, 128])
        p.v(lambda e: e.memset(spc, 0.0))
        p.spacer = lambda: p.v(lambda e: e.tensor_copy(spc[:, 0:64], spc[:, 64:128]))
        p.y(lambda e: e.dma_start(out=ident_sb, in_=ident), inc="ld")
        p.y(lambda e: e.dma_start(out=ones64_sb, in_=ones64), inc="ld")

        ar.push()
        xnT_sb = ar.alloc(BF16, [128, NC, T])

        ar.push()
        xf = [ar.alloc(F32, [128, D]) for _ in range(2)]
        sq = ar.alloc(F32, [128, D])
        xn32 = [ar.alloc(F32, [128, D]) for _ in range(2)]
        xnb = [ar.alloc(BF16, [128, D]) for _ in range(2)]
        st8 = ar.alloc(F32, [128, 8])
        nsum, sqs, negm, ex2, m2, var, lntmp, rstd = (st8[:, i:i + 1] for i in range(8))
        v_consumed = [0, 0]
        tr_done = [0, 0]
        st_xn = [0, 0]
        v_evac_done = 0
        for i in range(NT):
            bi = i % 2
            lv = p.y(lambda e, i=i, bi=bi: e.dma_start(out=xf[bi], in_=x_in[i * 128:(i + 1) * 128, :]),
                     inc="ld", waits=[("v", v_consumed[bi])])
            p.v(lambda e, bi=bi: e.tensor_tensor(sq, xf[bi], xf[bi], op=ALU.mult),
                waits=[("ld", lv)])
            p.v(lambda e, bi=bi: e.tensor_reduce(nsum, xf[bi], axis=AX, op=ALU.add, negate=True))
            p.v(lambda e: e.tensor_reduce(sqs, sq, axis=AX, op=ALU.add))
            p.v(lambda e: e.tensor_scalar_mul(negm, nsum, 1.0 / D))
            p.v(lambda e: e.tensor_scalar(ex2, sqs, 1.0 / D, 1e-5, op0=ALU.mult, op1=ALU.add))
            p.v(lambda e: e.tensor_tensor(m2, negm, negm, op=ALU.mult))
            p.spacer()
            vst = p.v(lambda e: e.tensor_tensor(var, ex2, m2, op=ALU.subtract), inc="v")
            p.s(lambda e: e.activation(lntmp, var, AF.Ln), waits=[("v", vst)])
            sst = p.s(lambda e: e.activation(rstd, lntmp, AF.Exp, scale=-0.5), inc="s")
            p.v(lambda e: e.tensor_tensor(negm, negm, rstd, op=ALU.mult), waits=[("s", sst)])
            p.spacer()
            p.v(lambda e, bi=bi: e.tensor_scalar(xn32[bi], xf[bi], rstd[:, 0:1], negm[:, 0:1],
                                                 op0=ALU.mult, op1=ALU.add),
                waits=[("st", st_xn[bi])])
            vdone = p.v(lambda e, bi=bi: e.tensor_copy(xnb[bi], xn32[bi]), inc="v",
                        waits=[("t", tr_done[bi])])
            v_consumed[bi] = vdone
            st_xn[bi] = p.y(lambda e, i=i, bi=bi: e.dma_start(
                out=xn_h[i * 128:(i + 1) * 128, :], in_=xn32[bi]),
                inc="st", waits=[("v", vdone)])
            for j in range(NC):
                w = [("v", vdone), ("v", v_evac_done)] if j == 0 else []
                last = p.t(lambda e, j=j, bi=bi: e.transpose(psT[:, j * 128:(j + 1) * 128],
                                                             xnb[bi][:, j * 128:(j + 1) * 128],
                                                             ident_sb),
                           inc="t" if j == NC - 1 else None, waits=w)
            tr_done[bi] = last
            dst = xnT_sb[:, :, i * 128:(i + 1) * 128]
            v_evac_done = p.v(lambda e, dst=dst: e.tensor_copy(
                dst, psT[:].rearrange("p (c q) -> p c q", q=128)),
                inc="v", waits=[("t", last)])
        ar.pop()
        p.barrier()

        ar.push()
        wq_sb = ar.alloc(BF16, [128, NC, 3 * D])
        bq_sb = ar.alloc(F32, [128, 24])
        bvb_sb = ar.alloc(F32, [128, D])
        qe_sb = ar.alloc(BF16, [128, T])
        vt_sb = ar.alloc(BF16, [128, NH * 65])
        t1_sb = ar.alloc(F32, [128, 512])
        t2_sb = ar.alloc(F32, [128, 512])
        swp_sb = [ar.alloc(BF16, [128, 512]) for _ in range(2)]
        if rotary:
            perm_sb = ar.alloc(BF16, [128, 128])
            rA_sb = ar.alloc(BF16, [128, T])
            rB_sb = ar.alloc(BF16, [128, T])
            p.y(lambda e: e.dma_start(out=perm_sb, in_=perm_in), inc="ld")
            p.y(lambda e: e.dma_start(out=rA_sb, in_=rotA), inc="ld")
            p.y(lambda e: e.dma_start(out=rB_sb, in_=rotB), inc="ld")
        p.g(lambda e: e.dma_start(out=wq_sb, in_=w_qkv.rearrange("(c p) e -> p c e", p=128)),
            inc="lg")
        p.y(lambda e: e.dma_start(out=bq_sb, in_=b_pc), inc="ld")
        p.y(lambda e: e.dma_start(out=bvb_sb, in_=bv_bc), inc="ld")
        p.barrier()

        for et in range(16):
            s_evac = [0, 0]
            v_rot = 0
            for nsl in range(8):
                pb = nsl % 2
                psq = ps_big[:, pb * 512:(pb + 1) * 512]
                for c in range(NC):
                    w = [("s", s_evac[pb])] if c == 0 else []
                    tmm = p.t(lambda e, c=c, et=et, nsl=nsl, psq=psq: e.matmul(
                        psq, wq_sb[:, c, et * 128:(et + 1) * 128],
                        xnT_sb[:, c, nsl * 512:(nsl + 1) * 512],
                        start=(c == 0), stop=(c == NC - 1)),
                        inc="t" if c == NC - 1 else None, waits=w)
                s_evac[pb] = p.s(lambda e, et=et, nsl=nsl, psq=psq: e.activation(
                    qe_sb[:, nsl * 512:(nsl + 1) * 512], psq, AF.Identity,
                    bias=bq_sb[:, et:et + 1]), inc="s", waits=[("t", tmm)])
            if rotary:
                v_sw = [0, 0]
                for nsl in range(8):
                    pb = nsl % 2
                    psw = ps_big[:, (2 + pb) * 512:(3 + pb) * 512]
                    tsw = p.t(lambda e, nsl=nsl, psw=psw: e.matmul(
                        psw, perm_sb, qe_sb[:, nsl * 512:(nsl + 1) * 512],
                        start=True, stop=True), inc="t",
                        waits=[("s", s_evac[0]), ("s", s_evac[1])] if nsl == 0 else
                              ([("v", v_sw[pb])] if v_sw[pb] else []))
                    ssw = p.s(lambda e, psw=psw, pb=pb: e.activation(swp_sb[pb], psw, AF.Identity),
                              inc="s", waits=[("t", tsw), ("v", v_sw[pb])])
                    sl = slice(nsl * 512, (nsl + 1) * 512)
                    p.v(lambda e, sl=sl: e.tensor_tensor(t1_sb, rA_sb[:, sl], qe_sb[:, sl],
                                                         op=ALU.mult), waits=[("s", ssw)])
                    p.v(lambda e, sl=sl, pb=pb: e.tensor_tensor(t2_sb, rB_sb[:, sl], swp_sb[pb],
                                                                op=ALU.mult))
                    v_sw[pb] = v_rot = p.v(lambda e, sl=sl: e.tensor_tensor(
                        qe_sb[:, sl], t1_sb, t2_sb, op=ALU.add), inc="v")
            dst_h = qT_h if et < 8 else kT_h
            row = (et % 8) * 128
            stw = [("v", v_rot)] if rotary else [("s", s_evac[0]), ("s", s_evac[1])]
            stv = p.y(lambda e, dst_h=dst_h, row=row: e.dma_start(
                out=dst_h[row:row + 128, :], in_=qe_sb), inc="st", waits=stw)
            p.pending["scalar"].append(("st", stv))
            if rotary:
                p.pending["tensor"].append(("st", stv))

        ovv = vt_sb.rearrange("p (h c) -> p h c", c=65)[:, :, 64:65]
        vmemset = p.v(lambda e: e.memset(ovv, 1.0), inc="v")
        p.pending["vector"].append(("v", vmemset))
        st_v = 0
        for ti in range(NT):
            v_evac = [0, 0]
            for vsl in range(2):
                psv = ps_big[:, vsl * 512:(vsl + 1) * 512]
                for c in range(NC):
                    w = [("v", v_evac[vsl]), ("st", st_v)] if c == 0 else []
                    tmm = p.t(lambda e, c=c, ti=ti, vsl=vsl, psv=psv: e.matmul(
                        psv, xnT_sb[:, c, ti * 128:(ti + 1) * 128],
                        wq_sb[:, c, 2 * D + vsl * 512:2 * D + (vsl + 1) * 512],
                        start=(c == 0), stop=(c == NC - 1)),
                        inc="t" if c == NC - 1 else None, waits=w)
                dst = vt_sb.rearrange("p (h c) -> p h c", c=65)[:, vsl * 8:(vsl + 1) * 8, 0:64]
                psv3 = psv.rearrange("p (h c) -> p h c", c=64)
                bvb3 = bvb_sb[:, vsl * 512:(vsl + 1) * 512].rearrange("p (h c) -> p h c", c=64)
                v_evac[vsl] = p.v(lambda e, psv3=psv3, bvb3=bvb3, dst=dst: e.tensor_tensor(
                    dst, psv3, bvb3, op=ALU.add), inc="v", waits=[("t", tmm)])
            st_v = p.y(lambda e, ti=ti: e.dma_start(out=vP_h[ti * 128:(ti + 1) * 128, :], in_=vt_sb),
                       inc="st", waits=[("v", v_evac[0]), ("v", v_evac[1])])
        ar.pop()
        ar.pop()
        p.barrier()

        ar.push()
        qh_sb = ar.alloc(BF16, [64, T])
        kh_sb = ar.alloc(BF16, [64, T])
        vh_sb = ar.alloc(BF16, [128, NT, 65])
        hT_sb = ar.alloc(BF16, [128, T])
        u_sb = [ar.alloc(BF16, [128, 512]) for _ in range(2)]
        numsb = ar.alloc(F32, [64, 512])
        rd_sb = ar.alloc(F32, [1, 512])
        rdb_sb = ar.alloc(BF16, [1, 512])
        if causal:
            cm_sb = ar.alloc(BF16, [128, 4, 512])
            p.y(lambda e: e.dma_start(out=cm_sb, in_=cmask.rearrange("r p q -> p r q")), inc="ld")
        ps_sc = [ps_big[:, 0:512], ps_big[:, 512:1024]]
        ps_num = ps_big[:, 1024:1536]
        ps_bc = ps_big[:, 1536:2048]
        st_hT = 0
        for h in range(NH):
            p.y(lambda e, h=h: e.dma_start(out=qh_sb, in_=qT_h[h * 64:(h + 1) * 64, :]), inc="ld")
            p.y(lambda e, h=h: e.dma_start(out=kh_sb, in_=kT_h[h * 64:(h + 1) * 64, :]), inc="ld")
            lq = p.y(lambda e, h=h: e.dma_start(
                out=vh_sb, in_=vP_h.rearrange("(t p) c -> p t c", p=128)[:, :, h * 65:(h + 1) * 65]),
                inc="ld")
            v_norm = 0
            s_u = [0, 0]
            v_u = [0, 0]
            for qb in range(8):
                t_av = [0, 0]
                kt_list = list(range(NT)) if not causal else list(range(min(NT, (qb + 1) * 4)))
                kt_first, kt_last = kt_list[0], kt_list[-1]
                for kt in kt_list:
                    bb = kt % 2
                    w = [("ld", lq)] if (qb == 0 and kt < 2) else []
                    if s_u[bb]:
                        w.append(("s", s_u[bb]))
                    if v_u[bb]:
                        w.append(("v", v_u[bb]))
                    tsc = p.t(lambda e, kt=kt, qb=qb, bb=bb: e.matmul(
                        ps_sc[bb], kh_sb[:, kt * 128:(kt + 1) * 128],
                        qh_sb[:, qb * 512:(qb + 1) * 512], start=True, stop=True),
                        inc="t", waits=w)
                    w = [("t", tsc)]
                    if t_av[bb]:
                        w.append(("t", t_av[bb]))
                    s_u[bb] = p.s(lambda e, bb=bb: e.activation(
                        u_sb[bb], ps_sc[bb], AF.Exp, scale=scale), inc="s", waits=w)
                    dep = ("s", s_u[bb])
                    if causal and kt * 128 >= qb * 512:
                        r = (kt * 128 - qb * 512) // 128
                        v_u[bb] = p.v(lambda e, bb=bb, r=r: e.tensor_tensor(
                            u_sb[bb], cm_sb[:, r, :], u_sb[bb], op=ALU.mult),
                            inc="v", waits=[("s", s_u[bb])])
                        dep = ("v", v_u[bb])
                    w = [dep]
                    if kt == kt_first and v_norm:
                        w.append(("v", v_norm))
                    t_av[bb] = p.t(lambda e, kt=kt, bb=bb, st=(kt == kt_first),
                                   sp=(kt == kt_last): e.matmul(
                        ps_num[0:65, :], vh_sb[:, kt, :], u_sb[bb], start=st, stop=sp),
                        inc="t", waits=w)
                p.v(lambda e: e.reciprocal(rd_sb, ps_num[64:65, :]),
                    waits=[("t", max(t_av))])
                p.v(lambda e: e.tensor_copy(rdb_sb, rd_sb))
                vv = p.v(lambda e: e.tensor_copy(numsb, ps_num[0:64, :]), inc="v")
                tbc = p.t(lambda e: e.matmul(ps_bc[0:64, :], ones64_sb, rdb_sb,
                                             start=True, stop=True), inc="t",
                          waits=[("v", vv)])
                w = [("t", tbc), ("st", st_hT)]
                v_norm = p.v(lambda e, qb=qb: e.tensor_tensor(
                    hT_sb[0:64, qb * 512:(qb + 1) * 512], numsb, ps_bc[0:64, :], op=ALU.mult),
                    inc="v", waits=w)
            st_hT = p.y(lambda e, h=h: e.dma_start(out=hT_h[h * 64:(h + 1) * 64, :],
                                                   in_=hT_sb[0:64]),
                        inc="st", waits=[("v", v_norm)])
            p.pending["sync"].append(("t", p.now()["t"]))
        ar.pop()
        p.barrier()

        ar.push()
        wo_sb = ar.alloc(BF16, [128, NC, D])
        bob_sb = ar.alloc(F32, [128, D])
        hTt_sb = [ar.alloc(BF16, [128, NC, 128]) for _ in range(2)]
        s_sb = ar.alloc(BF16, [128, D])
        xn_f = ar.alloc(F32, [128, D])
        t14 = ar.alloc(F32, [128, 512])
        p.g(lambda e: e.dma_start(out=wo_sb, in_=w_o.rearrange("(c p) e -> p c e", p=128)), inc="lg")
        p.y(lambda e: e.dma_start(out=bob_sb, in_=bo_bc), inc="ld")
        p.barrier()
        t_cons = [0, 0]
        st_s = 0
        v_ev_prev = 0
        for ti in range(NT):
            bi = ti % 2
            lv = p.y(lambda e, ti=ti, bi=bi: e.dma_start(
                out=hTt_sb[bi],
                in_=hT_h.rearrange("(c p) t -> p c t", p=128)[:, :, ti * 128:(ti + 1) * 128]),
                inc="ld", waits=[("t", t_cons[bi])])
            if add_xn:
                lv = p.y(lambda e, ti=ti: e.dma_start(out=xn_f, in_=xn_h[ti * 128:(ti + 1) * 128, :]),
                         inc="ld", waits=[("v", v_ev_prev)])
            v_ev = 0
            for esl in range(2):
                pso = ps_big[:, esl * 512:(esl + 1) * 512]
                for c in range(NC):
                    w = [("ld", lv), ("st", st_s), ("v", v_ev_prev)] if c == 0 else []
                    tmm = p.t(lambda e, c=c, esl=esl, bi=bi, pso=pso: e.matmul(
                        pso, hTt_sb[bi][:, c, :], wo_sb[:, c, esl * 512:(esl + 1) * 512],
                        start=(c == 0), stop=(c == NC - 1)),
                        inc="t" if c == NC - 1 else None, waits=w)
                t_cons[bi] = tmm
                sl = slice(esl * 512, (esl + 1) * 512)
                if add_xn:
                    p.v(lambda e, pso=pso, sl=sl: e.tensor_tensor(t14, pso, bob_sb[:, sl],
                                                                  op=ALU.add),
                        waits=[("t", tmm)])
                    v_ev = p.v(lambda e, sl=sl: e.tensor_tensor(s_sb[:, sl], t14, xn_f[:, sl],
                                                                op=ALU.add), inc="v")
                else:
                    v_ev = p.v(lambda e, pso=pso, sl=sl: e.tensor_tensor(
                        s_sb[:, sl], pso, bob_sb[:, sl], op=ALU.add), inc="v",
                        waits=[("t", tmm)])
            v_ev_prev = v_ev
            st_s = p.y(lambda e, ti=ti: e.dma_start(out=s_h[ti * 128:(ti + 1) * 128, :], in_=s_sb),
                       inc="st", waits=[("v", v_ev)])
        ar.pop()
        p.barrier()

        ar.push()
        wgT_sb = ar.alloc(BF16, [128, NC, I])
        wuT_sb = ar.alloc(BF16, [128, NC, I])
        wqt = ar.alloc(BF16, [128, I])
        wld = ar.alloc(F32, [128, I])
        qtmp = {"A": ar.alloc(F32, [128, I]), "B": ar.alloc(F32, [128, I]),
                "BM": ar.alloc(F32, [128, I // 16]), "SC": ar.alloc(F32, [128, I // 16])}
        wstate = {"g": 0, "wt": 0, "wv": 0}
        emit_wquant_transpose(p, w_gate, I, D, wgT_sb, wld, wqt, qtmp, psT, ident_sb, wstate)
        emit_wquant_transpose(p, w_up, I, D, wuT_sb, wld, wqt, qtmp, psT, ident_sb, wstate)
        p.barrier()

        xnq_f = ar.alloc(F32, [128, D])
        xq_sb = ar.alloc(BF16, [128, D])
        xqT_sb = ar.alloc(BF16, [128, D])
        h_f = ar.alloc(F32, [128, I])
        hq_sb = ar.alloc(BF16, [128, I])
        hqT_sb = ar.alloc(BF16, [128, NI, 128])
        GSL = [(i * 512, min(I, (i + 1) * 512)) for i in range(6)]
        for ti in range(NT):
            p.barrier()
            lv = p.y(lambda e, ti=ti: e.dma_start(out=xnq_f, in_=xn_h[ti * 128:(ti + 1) * 128, :]),
                     inc="ld")
            vq = emit_quant(p, "v", xnq_f, xq_sb, qtmp, D, waits_first=[("ld", lv)], inc_last="v")
            for j in range(NC):
                w = [("v", vq)] if j == 0 else []
                ttr = p.t(lambda e, j=j: e.transpose(psT[:, j * 128:(j + 1) * 128],
                                                     xq_sb[:, j * 128:(j + 1) * 128], ident_sb),
                          inc="t" if j == NC - 1 else None, waits=w)
            vxq = p.v(lambda e: e.tensor_copy(xqT_sb, psT[:]), inc="v", waits=[("t", ttr)])
            s_ev = 0
            v_h = 0
            for phase in range(2):
                wsb = wgT_sb if phase == 0 else wuT_sb
                for c in range(NC):
                    for isl, (a, b) in enumerate(GSL):
                        w = []
                        if c == 0 and isl == 0:
                            w = [("v", vxq)] if phase == 0 else [("v", v_h), ("s", s_ev)]
                        tmm = p.t(lambda e, c=c, a=a, b=b, wsb=wsb: e.matmul(
                            ps_big[:, a:b], xqT_sb[:, c * 128:(c + 1) * 128],
                            wsb[:, c, a:b], start=(c == 0), stop=(c == NC - 1)),
                            inc="t" if (c == NC - 1 and isl == len(GSL) - 1) else None, waits=w)
                if phase == 0:
                    for isl, (a, b) in enumerate(GSL):
                        s_ev = p.s(lambda e, a=a, b=b: e.activation(
                            h_f[:, a:b], ps_big[:, a:b], AF.Silu),
                            inc="s" if isl == len(GSL) - 1 else None,
                            waits=[("t", tmm)] if isl == 0 else [])
                else:
                    for isl, (a, b) in enumerate(GSL):
                        v_h = p.v(lambda e, a=a, b=b: e.tensor_tensor(
                            h_f[:, a:b], ps_big[:, a:b], h_f[:, a:b], op=ALU.mult),
                            inc="v" if isl == len(GSL) - 1 else None,
                            waits=[("t", tmm), ("s", s_ev)] if isl == 0 else [])
            vq2 = emit_quant(p, "v", h_f, hq_sb, qtmp, I, waits_first=[("v", v_h)], inc_last="v")
            v_hq = 0
            for jb in range(0, NI, 8):
                jn = min(8, NI - jb)
                for j in range(jn):
                    w = [("v", vq2)] if (jb == 0 and j == 0) else ([("v", v_hq)] if j == 0 else [])
                    ttr2 = p.t(lambda e, j=j, jb=jb: e.transpose(
                        psT[:, j * 128:(j + 1) * 128],
                        hq_sb[:, (jb + j) * 128:(jb + j + 1) * 128], ident_sb),
                        inc="t" if j == jn - 1 else None, waits=w)
                dst = hqT_sb[:, jb:jb + jn, :]
                src = psT[:].rearrange("p (c q) -> p c q", q=128)[:, 0:jn, :]
                v_hq = p.v(lambda e, dst=dst, src=src: e.tensor_copy(dst, src),
                           inc="v", waits=[("t", ttr2)])
            p.y(lambda e, ti=ti: e.dma_start(
                out=hqT_h.rearrange("(c p) t -> p c t", p=128)[:, :, ti * 128:(ti + 1) * 128],
                in_=hqT_sb), inc="st", waits=[("v", v_hq)])
        ar.pop()
        p.barrier()

        ar.push()
        wdT_sb = ar.alloc(BF16, [128, NI, D])
        wqt2 = ar.alloc(BF16, [128, I])
        wld2 = ar.alloc(F32, [128, I])
        qtmp2 = {"A": ar.alloc(F32, [128, I]), "B": ar.alloc(F32, [128, I]),
                 "BM": ar.alloc(F32, [128, I // 16]), "SC": ar.alloc(F32, [128, I // 16])}
        wstate = {"g": 0, "wt": 0, "wv": 0}
        emit_wquant_transpose(p, w_down, D, I, wdT_sb, wld2, wqt2, qtmp2, psT, ident_sb, wstate)
        p.barrier()
        hqt_t = [ar.alloc(BF16, [128, NI, 128]) for _ in range(2)]
        g_sb = ar.alloc(BF16, [128, D])
        s_tl = ar.alloc(BF16, [128, D])
        x_tl = ar.alloc(F32, [128, D])
        sg_f = ar.alloc(F32, [128, D])
        xo_f = ar.alloc(F32, [128, D])
        t_cons = [0, 0]
        v_done = 0
        s_gdone = 0
        st_x = 0
        for ti in range(NT):
            bi = ti % 2
            lv = p.y(lambda e, ti=ti, bi=bi: e.dma_start(
                out=hqt_t[bi],
                in_=hqT_h.rearrange("(c p) t -> p c t", p=128)[:, :, ti * 128:(ti + 1) * 128]),
                inc="ld", waits=[("t", t_cons[bi])])
            lv = p.y(lambda e, ti=ti: e.dma_start(out=s_tl, in_=s_h[ti * 128:(ti + 1) * 128, :]),
                     inc="ld", waits=[("v", v_done)])
            lv = p.y(lambda e, ti=ti: e.dma_start(out=x_tl, in_=x_in[ti * 128:(ti + 1) * 128, :]),
                     inc="ld")
            t_dn = 0
            for dsl in range(2):
                psd = ps_big[:, dsl * 512:(dsl + 1) * 512]
                for c in range(NI):
                    w = [("ld", lv), ("s", s_gdone), ("v", v_done)] if c == 0 else []
                    t_dn = p.t(lambda e, c=c, dsl=dsl, psd=psd, bi=bi: e.matmul(
                        psd, hqt_t[bi][:, c, :], wdT_sb[:, c, dsl * 512:(dsl + 1) * 512],
                        start=(c == 0), stop=(c == NI - 1)),
                        inc="t" if c == NI - 1 else None, waits=w)
                t_cons[bi] = t_dn
                s_gdone = p.s(lambda e, dsl=dsl, psd=psd: e.activation(
                    g_sb[:, dsl * 512:(dsl + 1) * 512], psd, AF.Silu),
                    inc="s", waits=[("t", t_dn)])
            if DEBUG:
                p.y(lambda e, ti=ti: e.dma_start(out=g_h[ti * 128:(ti + 1) * 128, :], in_=g_sb),
                    inc="st", waits=[("s", s_gdone)])
            p.v(lambda e: e.tensor_tensor(sg_f, s_tl, g_sb, op=ALU.mult),
                waits=[("s", s_gdone), ("st", st_x)])
            v_done = p.v(lambda e: e.tensor_tensor(xo_f, x_tl, sg_f, op=ALU.add), inc="v")
            st_x = p.y(lambda e, ti=ti: e.dma_start(out=x_out[ti * 128:(ti + 1) * 128, :], in_=xo_f),
                       inc="st", waits=[("v", v_done)])
        ar.pop()
        p.y(lambda e: e.nop(), waits=[("st", p.now()["st"])])

        p.build(nc)
    return nc


def build_l3():
    nc = bass.Bass("TRN2", target_bir_lowering=False)
    HID = 4 * D
    NHID = HID // 128

    x_in = nc.dram_tensor("x", [T, D], F32, kind="ExternalInput").ap()
    wq = nc.dram_tensor("wq", [D, D], F32, kind="ExternalInput").ap()
    wk = nc.dram_tensor("wk", [D, D], F32, kind="ExternalInput").ap()
    wv = nc.dram_tensor("wv", [D, D], F32, kind="ExternalInput").ap()
    wo = nc.dram_tensor("wo", [D, D], F32, kind="ExternalInput").ap()
    ln1g = nc.dram_tensor("ln1g", [128, D], F32, kind="ExternalInput").ap()
    ln1b = nc.dram_tensor("ln1b", [128, D], F32, kind="ExternalInput").ap()
    ln2g = nc.dram_tensor("ln2g", [128, D], F32, kind="ExternalInput").ap()
    ln2b = nc.dram_tensor("ln2b", [128, D], F32, kind="ExternalInput").ap()
    cw1 = nc.dram_tensor("cw1", [3, D, HID], F32, kind="ExternalInput").ap()
    cb1 = nc.dram_tensor("cb1", [3, 128, NHID], F32, kind="ExternalInput").ap()
    cw2 = nc.dram_tensor("cw2", [3, HID, D], F32, kind="ExternalInput").ap()
    cb2 = nc.dram_tensor("cb2", [3, 128, NC], F32, kind="ExternalInput").ap()
    wrT = nc.dram_tensor("wrT", [D, D], F32, kind="ExternalInput").ap()
    wkT = nc.dram_tensor("wkT", [D, D], F32, kind="ExternalInput").ap()
    wvT = nc.dram_tensor("wvT", [D, D], F32, kind="ExternalInput").ap()
    mur = nc.dram_tensor("mur", [128, D], F32, kind="ExternalInput").ap()
    muk = nc.dram_tensor("muk", [128, D], F32, kind="ExternalInput").ap()
    ident = nc.dram_tensor("ident", [128, 128], BF16, kind="ExternalInput").ap()
    trilm = nc.dram_tensor("trilm", [128, 128], BF16, kind="ExternalInput").ap()
    x_out = nc.dram_tensor("x_out", [T, D], F32, kind="ExternalOutput").ap()

    kind_dbg = "ExternalOutput" if DEBUG else "Internal"
    xn_h = nc.dram_tensor("xn_h", [T, D], F32, kind=kind_dbg).ap()
    qT_h = nc.dram_tensor("qT_h", [D, T], BF16, kind=kind_dbg).ap()
    kT_h = nc.dram_tensor("kT_h", [D, T], BF16, kind=kind_dbg).ap()
    ktok_h = nc.dram_tensor("ktok_h", [T, D], BF16, kind=kind_dbg).ap()
    vtok_h = nc.dram_tensor("vtok_h", [T, D], BF16, kind=kind_dbg).ap()
    yT_h = nc.dram_tensor("yT_h", [D, T], BF16, kind=kind_dbg).ap()
    x1_h = nc.dram_tensor("x1_h", [T, D], F32, kind=kind_dbg).ap()
    hT_hs = [nc.dram_tensor(f"h{i}T_h", [D, T], BF16, kind=kind_dbg).ap() for i in range(4)]
    g3_h = nc.dram_tensor("g3_h", [T, D], BF16, kind=kind_dbg).ap() if DEBUG else None

    p = Prog()
    with ExitStack() as stk:
        BFA = stk.enter_context(nc.sbuf_tensor("bf_arena", [128, 73000], BF16))
        F32A = stk.enter_context(nc.sbuf_tensor("f32_arena", [128, 15000], F32))
        ar = Arena({BF16: BFA, F32: F32A})
        ps_big = stk.enter_context(nc.psum_tensor("ps_big", [128, 3072], F32))
        psT = stk.enter_context(nc.psum_tensor("psT", [128, 1024], BF16))

        ident_sb = ar.alloc(BF16, [128, 128])
        tril_sb = ar.alloc(BF16, [128, 128])
        spc = ar.alloc(F32, [128, 128])
        p.v(lambda e: e.memset(spc, 0.0))
        p.spacer = lambda: p.v(lambda e: e.tensor_copy(spc[:, 0:64], spc[:, 64:128]))
        p.y(lambda e: e.dma_start(out=ident_sb, in_=ident), inc="ld")
        p.y(lambda e: e.dma_start(out=tril_sb, in_=trilm), inc="ld")

        def emit_ln_tokens(src_getter, dst_h, g_bc, b_bc, make_T, hT_dst, src_store_h=None):
            xfb = [ar.alloc(F32, [128, D]) for _ in range(2)]
            sqb = ar.alloc(F32, [128, D])
            on32 = [ar.alloc(F32, [128, D]) for _ in range(2)]
            onb = [ar.alloc(BF16, [128, D]) for _ in range(2)]
            st8 = ar.alloc(F32, [128, 8])
            nsum, sqs, negm, ex2, m2, var, lntmp, rstd = (st8[:, i:i + 1] for i in range(8))
            v_consumed = [0, 0]
            tr_done = [0, 0]
            st_n = [0, 0]
            st_s = [0, 0]
            v_evac_done = 0
            for i in range(NT):
                bi = i % 2
                fn, ws = src_getter(i, xfb[bi], [("v", v_consumed[bi]), ("st", st_s[bi])])
                lv = p.y(fn, inc="ld", waits=ws)
                p.v(lambda e, bi=bi: e.tensor_tensor(sqb, xfb[bi], xfb[bi], op=ALU.mult),
                    waits=[("ld", lv)])
                p.v(lambda e, bi=bi: e.tensor_reduce(nsum, xfb[bi], axis=AX, op=ALU.add,
                                                     negate=True))
                p.v(lambda e: e.tensor_reduce(sqs, sqb, axis=AX, op=ALU.add))
                p.v(lambda e: e.tensor_scalar_mul(negm, nsum, 1.0 / D))
                p.v(lambda e: e.tensor_scalar(ex2, sqs, 1.0 / D, 1e-5, op0=ALU.mult, op1=ALU.add))
                p.v(lambda e: e.tensor_tensor(m2, negm, negm, op=ALU.mult))
                p.spacer()
                vst = p.v(lambda e: e.tensor_tensor(var, ex2, m2, op=ALU.subtract), inc="v")
                p.s(lambda e: e.activation(lntmp, var, AF.Ln), waits=[("v", vst)])
                sst = p.s(lambda e: e.activation(rstd, lntmp, AF.Exp, scale=-0.5), inc="s")
                p.v(lambda e: e.tensor_tensor(negm, negm, rstd, op=ALU.mult), waits=[("s", sst)])
                p.spacer()
                p.v(lambda e, bi=bi: e.tensor_scalar(on32[bi], xfb[bi], rstd[:, 0:1], negm[:, 0:1],
                                                     op0=ALU.mult, op1=ALU.add),
                    waits=[("st", st_n[bi])])
                if g_bc is not None:
                    p.v(lambda e, bi=bi: e.tensor_tensor(on32[bi], on32[bi], g_bc, op=ALU.mult))
                    p.v(lambda e, bi=bi: e.tensor_tensor(on32[bi], on32[bi], b_bc, op=ALU.add))
                vdone = p.v(lambda e, bi=bi: e.tensor_copy(onb[bi], on32[bi]), inc="v",
                            waits=[("t", tr_done[bi])])
                v_consumed[bi] = vdone
                if src_store_h is not None:
                    st_s[bi] = p.y(lambda e, i=i, bi=bi: e.dma_start(
                        out=src_store_h[i * 128:(i + 1) * 128, :], in_=xfb[bi]),
                        inc="st", waits=[("v", vdone)])
                if dst_h is not None:
                    st_n[bi] = p.y(lambda e, i=i, bi=bi: e.dma_start(
                        out=dst_h[i * 128:(i + 1) * 128, :], in_=on32[bi]),
                        inc="st", waits=[("v", vdone)])
                if make_T:
                    for j in range(NC):
                        w = [("v", vdone), ("v", v_evac_done)] if j == 0 else []
                        last = p.t(lambda e, j=j, bi=bi: e.transpose(
                            psT[:, j * 128:(j + 1) * 128], onb[bi][:, j * 128:(j + 1) * 128],
                            ident_sb), inc="t" if j == NC - 1 else None, waits=w)
                    tr_done[bi] = last
                    dst = hT_dst[:, :, i * 128:(i + 1) * 128]
                    v_evac_done = p.v(lambda e, dst=dst: e.tensor_copy(
                        dst, psT[:].rearrange("p (c q) -> p c q", q=128)),
                        inc="v", waits=[("t", last)])

        ar.push()
        emit_ln_tokens(lambda i, dst, ws: ((lambda e, i=i, dst=dst: e.dma_start(
            out=dst, in_=x_in[i * 128:(i + 1) * 128, :])), ws), xn_h, None, None, False, None)
        ar.pop()
        p.barrier()

        ar.push()
        h1T_sb = ar.alloc(BF16, [128, NC, T])
        ar.push()
        l1g_sb = ar.alloc(F32, [128, D])
        l1b_sb = ar.alloc(F32, [128, D])
        p.y(lambda e: e.dma_start(out=l1g_sb, in_=ln1g), inc="ld")
        lvg = p.y(lambda e: e.dma_start(out=l1b_sb, in_=ln1b), inc="ld")
        p.pending["vector"].append(("ld", lvg))
        emit_ln_tokens(lambda i, dst, ws: ((lambda e, i=i, dst=dst: e.dma_start(
            out=dst, in_=xn_h[i * 128:(i + 1) * 128, :])), ws), None, l1g_sb, l1b_sb,
            True, h1T_sb)
        ar.pop()
        p.barrier()

        ar.push()
        wp_sb = ar.alloc(BF16, [128, NC, 3 * D])
        qe_sb = ar.alloc(BF16, [128, T])
        kv_sb = ar.alloc(BF16, [128, D])
        p.g(lambda e: e.dma_start(out=wp_sb[:, :, 0:D], in_=wq.rearrange("(c p) e -> p c e", p=128)),
            inc="lg")
        p.g(lambda e: e.dma_start(out=wp_sb[:, :, D:2 * D],
                                  in_=wk.rearrange("(c p) e -> p c e", p=128)), inc="lg")
        p.g(lambda e: e.dma_start(out=wp_sb[:, :, 2 * D:3 * D],
                                  in_=wv.rearrange("(c p) e -> p c e", p=128)), inc="lg")
        p.barrier()
        for et in range(16):
            off = (et % 8) * 128 + (et // 8) * D
            s_evac = [0, 0]
            for nsl in range(8):
                pb = nsl % 2
                psq = ps_big[:, pb * 512:(pb + 1) * 512]
                for c in range(NC):
                    w = [("s", s_evac[pb])] if c == 0 else []
                    tmm = p.t(lambda e, c=c, off=off, nsl=nsl, psq=psq: e.matmul(
                        psq, wp_sb[:, c, off:off + 128],
                        h1T_sb[:, c, nsl * 512:(nsl + 1) * 512],
                        start=(c == 0), stop=(c == NC - 1)),
                        inc="t" if c == NC - 1 else None, waits=w)
                s_evac[pb] = p.s(lambda e, nsl=nsl, psq=psq: e.activation(
                    qe_sb[:, nsl * 512:(nsl + 1) * 512], psq, AF.Identity),
                    inc="s", waits=[("t", tmm)])
            dst_h = qT_h if et < 8 else kT_h
            row = (et % 8) * 128
            stv = p.y(lambda e, dst_h=dst_h, row=row: e.dma_start(
                out=dst_h[row:row + 128, :], in_=qe_sb), inc="st",
                waits=[("s", s_evac[0]), ("s", s_evac[1])])
            p.pending["scalar"].append(("st", stv))
        st_v = 0
        v_evac = [0, 0]
        for part, dst_h in [(1, ktok_h), (2, vtok_h)]:
            for ti in range(NT):
                for vsl in range(2):
                    psv = ps_big[:, vsl * 512:(vsl + 1) * 512]
                    for c in range(NC):
                        w = [("v", v_evac[vsl]), ("st", st_v)] if c == 0 else []
                        tmm = p.t(lambda e, c=c, ti=ti, vsl=vsl, psv=psv, part=part: e.matmul(
                            psv, h1T_sb[:, c, ti * 128:(ti + 1) * 128],
                            wp_sb[:, c, part * D + vsl * 512:part * D + (vsl + 1) * 512],
                            start=(c == 0), stop=(c == NC - 1)),
                            inc="t" if c == NC - 1 else None, waits=w)
                    v_evac[vsl] = p.v(lambda e, psv=psv, vsl=vsl: e.tensor_copy(
                        kv_sb[:, vsl * 512:(vsl + 1) * 512], psv), inc="v", waits=[("t", tmm)])
                st_v = p.y(lambda e, ti=ti, dst_h=dst_h: e.dma_start(
                    out=dst_h[ti * 128:(ti + 1) * 128, :], in_=kv_sb),
                    inc="st", waits=[("v", v_evac[0]), ("v", v_evac[1])])
        ar.pop()
        ar.pop()
        p.barrier()

        ar.push()
        qh_sb = ar.alloc(BF16, [64, T])
        kh_sb = ar.alloc(BF16, [64, T])
        ktk_sb = ar.alloc(BF16, [128, NT, 64])
        vtk_sb = ar.alloc(BF16, [128, NT, 64])
        msb = ar.alloc(BF16, [64, 64])
        ssb = ar.alloc(BF16, [128, 128])
        ysb = ar.alloc(BF16, [128, 64])
        yT_sb = ar.alloc(BF16, [64, T])
        ps_y = ps_big[:, 0:512]
        ps_s = ps_big[:, 512:1024]
        ps_M = ps_big[:, 1024:1536]
        for h in range(NH):
            p.y(lambda e, h=h: e.dma_start(out=qh_sb, in_=qT_h[h * 64:(h + 1) * 64, :]), inc="ld")
            p.y(lambda e, h=h: e.dma_start(out=kh_sb, in_=kT_h[h * 64:(h + 1) * 64, :]), inc="ld")
            p.y(lambda e, h=h: e.dma_start(
                out=ktk_sb, in_=ktok_h.rearrange("(t p) c -> p t c", p=128)[:, :, h * 64:(h + 1) * 64]),
                inc="ld")
            lq = p.y(lambda e, h=h: e.dma_start(
                out=vtk_sb, in_=vtok_h.rearrange("(t p) c -> p t c", p=128)[:, :, h * 64:(h + 1) * 64]),
                inc="ld")
            t_mup = 0
            v_mcp = 0
            t_yev = 0
            v_yev = 0
            s_yev = 0
            v_ss = 0
            t_intra_prev = 0
            for ci in range(NT):
                w0 = [("ld", lq)] if ci == 0 else []
                if ci > 0:
                    v_mcp = p.v(lambda e: e.tensor_copy(msb, ps_M[0:64, 0:64]), inc="v",
                                waits=[("t", t_mup)])
                    p.t(lambda e, ci=ci: e.matmul(
                        ps_y[0:128, 0:64], qh_sb[:, ci * 128:(ci + 1) * 128], msb,
                        start=True, stop=False), inc="t",
                        waits=[("v", v_mcp), ("s", s_yev)])
                tss = p.t(lambda e, ci=ci: e.matmul(
                    ps_s[:, 0:128], kh_sb[:, ci * 128:(ci + 1) * 128],
                    qh_sb[:, ci * 128:(ci + 1) * 128], start=True, stop=True), inc="t",
                    waits=w0 + [("v", v_ss)])
                v_ss = p.v(lambda e: e.tensor_tensor(ssb, tril_sb, ps_s[:, 0:128], op=ALU.mult),
                           inc="v", waits=[("t", tss), ("t", t_intra_prev)])
                t_intra_prev = p.t(lambda e, ci=ci: e.matmul(
                    ps_y[0:128, 0:64], ssb, vtk_sb[:, ci, :],
                    start=(ci == 0), stop=True), inc="t",
                    waits=[("v", v_ss)] + ([("s", s_yev)] if ci == 0 else []))
                t_mup = p.t(lambda e, ci=ci: e.matmul(
                    ps_M[0:64, 0:64], ktk_sb[:, ci, :], vtk_sb[:, ci, :],
                    start=(ci == 0), stop=(ci == NT - 1)), inc="t",
                    waits=[("v", v_mcp)] if ci > 0 else [])
                s_yev = p.s(lambda e: e.activation(ysb, ps_y[0:128, 0:64], AF.Identity),
                            inc="s", waits=[("t", t_intra_prev), ("t", t_yev)])
                jj = ci % 8
                t_yev = p.t(lambda e, jj=jj: e.transpose(
                    psT[0:64, jj * 128:(jj + 1) * 128], ysb, ident_sb), inc="t",
                    waits=[("s", s_yev)] + ([("v", v_yev)] if jj == 0 else []))
                if jj == 7:
                    dst = yT_sb[:, (ci - 7) * 128:(ci + 1) * 128]
                    v_yev = p.v(lambda e, dst=dst: e.tensor_copy(dst, psT[0:64, :]),
                                inc="v", waits=[("t", t_yev)])
            st_yT = p.y(lambda e, h=h: e.dma_start(out=yT_h[h * 64:(h + 1) * 64, :], in_=yT_sb),
                        inc="st", waits=[("v", v_yev)])
            p.pending["sync"].append(("t", p.now()["t"]))
            p.pending["vector"].append(("st", st_yT))
        ar.pop()
        p.barrier()

        ar.push()
        h2T_sb = ar.alloc(BF16, [128, NC, T])
        ar.push()
        wo_sb = ar.alloc(BF16, [128, NC, D])
        yt_t = [ar.alloc(BF16, [128, NC, 128]) for _ in range(2)]
        l2g_sb = ar.alloc(F32, [128, D])
        l2b_sb = ar.alloc(F32, [128, D])
        xnl_sb = ar.alloc(F32, [128, D])
        p.g(lambda e: e.dma_start(out=wo_sb, in_=wo.rearrange("(c p) e -> p c e", p=128)), inc="lg")
        p.y(lambda e: e.dma_start(out=l2g_sb, in_=ln2g), inc="ld")
        lvg = p.y(lambda e: e.dma_start(out=l2b_sb, in_=ln2b), inc="ld")
        p.pending["vector"].append(("ld", lvg))
        p.barrier()
        t_cons = [0, 0]

        def src_x1(i, dst, ws):
            bi = i % 2
            p.y(lambda e, i=i, bi=bi: e.dma_start(
                out=yt_t[bi], in_=yT_h.rearrange("(c p) t -> p c t", p=128)[:, :, i * 128:(i + 1) * 128]),
                inc="ld", waits=[("t", t_cons[bi])])
            lv2 = p.y(lambda e, i=i: e.dma_start(out=xnl_sb, in_=xn_h[i * 128:(i + 1) * 128, :]),
                      inc="ld", waits=ws)
            v_ev = 0
            for esl in range(2):
                pso = ps_big[:, esl * 512:(esl + 1) * 512]
                for c in range(NC):
                    w = [("ld", lv2), ("v", v_ev)] if c == 0 else []
                    tmm = p.t(lambda e, c=c, esl=esl, bi=bi, pso=pso: e.matmul(
                        pso, yt_t[bi][:, c, :], wo_sb[:, c, esl * 512:(esl + 1) * 512],
                        start=(c == 0), stop=(c == NC - 1)),
                        inc="t" if c == NC - 1 else None, waits=w)
                t_cons[bi] = tmm
                sl = slice(esl * 512, (esl + 1) * 512)
                v_ev = p.v(lambda e, pso=pso, sl=sl, dst=dst: e.tensor_tensor(
                    dst[:, sl], pso, xnl_sb[:, sl], op=ALU.add), inc="v", waits=[("t", tmm)])
            return (lambda e: e.nop()), [("v", v_ev)]

        emit_ln_tokens(src_x1, None, l2g_sb, l2b_sb, True, h2T_sb, src_store_h=x1_h)
        ar.pop()
        p.barrier()
        p.y(lambda e: e.dma_start(out=hT_hs[0].rearrange("(c p) t -> p c t", p=128), in_=h2T_sb),
            inc="st")
        ar.pop()
        p.barrier()

        ar.push()
        w1_sb = ar.alloc(BF16, [128, NC, HID])
        w2_sb = ar.alloc(BF16, [128, NHID, D])
        b1_sb = ar.alloc(F32, [128, NHID])
        b2_sb = ar.alloc(F32, [128, NC])
        TB = 256
        NTB = T // TB
        hblk = ar.alloc(BF16, [128, NC, TB])
        asb = [ar.alloc(BF16, [128, TB]) for _ in range(2)]
        outev = ar.alloc(BF16, [128, NC, TB])
        st_o = 0
        v_oev = 0
        s_aev = [0, 0]
        t_out2 = 0
        for lvl in range(3):
            src_hT = hT_hs[lvl]
            dst_hT = hT_hs[lvl + 1]
            p.g(lambda e, lvl=lvl: e.dma_start(
                out=w1_sb, in_=cw1[lvl].rearrange("(c p) e -> p c e", p=128)), inc="lg",
                waits=[("t", p.now()["t"])])
            p.g(lambda e, lvl=lvl: e.dma_start(
                out=w2_sb, in_=cw2[lvl].rearrange("(c p) e -> p c e", p=128)), inc="lg")
            p.y(lambda e, lvl=lvl: e.dma_start(out=b1_sb, in_=cb1[lvl]), inc="ld")
            p.y(lambda e, lvl=lvl: e.dma_start(out=b2_sb, in_=cb2[lvl]), inc="ld")
            p.barrier()
            for tb in range(NTB):
                tsl = slice(tb * TB, (tb + 1) * TB)
                lv = p.y(lambda e, tsl=tsl, src_hT=src_hT: e.dma_start(
                    out=hblk, in_=src_hT.rearrange("(c p) t -> p c t", p=128)[:, :, tsl]),
                    inc="ld", waits=[("t", p.now()["t"])])
                for ht in range(NHID):
                    hb = ht % 2
                    pa = ps_big[:, 2048 + hb * 512: 2048 + hb * 512 + TB]
                    for c in range(NC):
                        w = []
                        if c == 0:
                            w = [("ld", lv), ("s", s_aev[hb])]
                        tmm = p.t(lambda e, c=c, ht=ht, pa=pa: e.matmul(
                            pa, w1_sb[:, c, ht * 128:(ht + 1) * 128], hblk[:, c, :],
                            start=(c == 0), stop=(c == NC - 1)),
                            inc="t" if c == NC - 1 else None, waits=w)
                    w = [("t", tmm)]
                    if t_out2:
                        w.append(("t", t_out2))
                    s_aev[hb] = p.s(lambda e, ht=ht, pa=pa, hb=hb: e.activation(
                        asb[hb], pa, AF.Gelu_apprx_tanh, bias=b1_sb[:, ht:ht + 1]),
                        inc="s", waits=w)
                    for dt in range(NC):
                        po = ps_big[:, dt * 256:(dt + 1) * 256]
                        w = []
                        if dt == 0:
                            w = [("s", s_aev[hb])]
                            if ht == 0:
                                w.append(("v", v_oev))
                        t_out2 = p.t(lambda e, ht=ht, dt=dt, po=po, hb=hb: e.matmul(
                            po, w2_sb[:, ht, dt * 128:(dt + 1) * 128], asb[hb],
                            start=(ht == 0 and dt % 2 == 0), stop=(ht == NHID - 1)),
                            inc="t" if dt == NC - 1 else None, waits=w)
                for dt in range(NC):
                    po = ps_big[:, dt * 256:(dt + 1) * 256]
                    v_oev = p.v(lambda e, dt=dt, po=po: e.tensor_scalar(
                        outev[:, dt, :], po, b2_sb[:, dt:dt + 1], None, op0=ALU.add),
                        inc="v" if dt == NC - 1 else None,
                        waits=[("t", t_out2), ("st", st_o)] if dt == 0 else [])
                st_o = p.y(lambda e, tsl=tsl, dst_hT=dst_hT: e.dma_start(
                    out=dst_hT.rearrange("(c p) t -> p c t", p=128)[:, :, tsl],
                    in_=outev), inc="st", waits=[("v", v_oev)])
            p.barrier()
        ar.pop()
        p.barrier()

        ar.push()
        wr_sb = ar.alloc(BF16, [128, NC, D])
        wk2_sb = ar.alloc(BF16, [128, NC, D])
        wv2_sb = ar.alloc(BF16, [128, NC, D])
        mur_sb = ar.alloc(F32, [128, D])
        muk_sb = ar.alloc(F32, [128, D])
        omur_sb = ar.alloc(F32, [128, D])
        omuk_sb = ar.alloc(F32, [128, D])
        xn_t = ar.alloc(F32, [128, D])
        xp_t = ar.alloc(F32, [128, D])
        xr_t = ar.alloc(F32, [128, D])
        xk_t = ar.alloc(F32, [128, D])
        tmp_t = ar.alloc(F32, [128, D])
        mixb = ar.alloc(BF16, [128, D])
        mixT = ar.alloc(BF16, [128, D])
        r_f = xr_t
        krel_f = xk_t
        kact = ar.alloc(BF16, [128, D])
        kaT = ar.alloc(BF16, [128, D])
        rk_f = tmp_t
        g_sb = ar.alloc(BF16, [128, D])
        h3t = [ar.alloc(BF16, [128, NC, 128]) for _ in range(2)]
        h3tok = ar.alloc(F32, [128, D])
        s_f = ar.alloc(F32, [128, D])
        s_bf = ar.alloc(BF16, [128, D])
        x_tl = ar.alloc(F32, [128, D])
        sg_f = xn_t
        xo_f = xp_t
        p.g(lambda e: e.dma_start(out=wr_sb, in_=wrT.rearrange("(c p) e -> p c e", p=128)), inc="lg")
        p.g(lambda e: e.dma_start(out=wk2_sb, in_=wkT.rearrange("(c p) e -> p c e", p=128)), inc="lg")
        p.g(lambda e: e.dma_start(out=wv2_sb, in_=wvT.rearrange("(c p) e -> p c e", p=128)), inc="lg")
        p.y(lambda e: e.dma_start(out=mur_sb, in_=mur), inc="ld")
        lvm = p.y(lambda e: e.dma_start(out=muk_sb, in_=muk), inc="ld")
        p.barrier()
        p.v(lambda e: e.tensor_scalar(omur_sb, mur_sb, -1.0, 1.0, op0=ALU.mult, op1=ALU.add),
            waits=[("ld", lvm)])
        p.v(lambda e: e.tensor_scalar(omuk_sb, muk_sb, -1.0, 1.0, op0=ALU.mult, op1=ALU.add))
        p.barrier()
        for ti in range(NT):
            p.barrier()
            p.y(lambda e, ti=ti: e.dma_start(out=xn_t, in_=xn_h[ti * 128:(ti + 1) * 128, :]),
                inc="ld")
            if ti == 0:
                p.v(lambda e: e.memset(xp_t[0:1, :], 0.0))
                p.y(lambda e: e.dma_start(out=xp_t[1:128, :], in_=xn_h[0:127, :]), inc="ld")
            else:
                p.y(lambda e, ti=ti: e.dma_start(
                    out=xp_t, in_=xn_h[ti * 128 - 1:(ti + 1) * 128 - 1, :]), inc="ld")
            p.y(lambda e, ti=ti: e.dma_start(
                out=h3t[ti % 2], in_=hT_hs[3].rearrange("(c p) t -> p c t", p=128)[:, :, ti * 128:(ti + 1) * 128]),
                inc="ld")
            p.y(lambda e, ti=ti: e.dma_start(out=x_tl, in_=x_in[ti * 128:(ti + 1) * 128, :]),
                inc="ld")
            lv = p.y(lambda e, ti=ti: e.dma_start(out=s_f, in_=x1_h[ti * 128:(ti + 1) * 128, :]),
                     inc="ld")
            p.pending["vector"].append(("ld", lv))
            p.pending["tensor"].append(("ld", lv))
            p.v(lambda e: e.tensor_tensor(xr_t, mur_sb, xn_t, op=ALU.mult))
            p.v(lambda e: e.tensor_tensor(tmp_t, omur_sb, xp_t, op=ALU.mult))
            p.v(lambda e: e.tensor_tensor(xr_t, xr_t, tmp_t, op=ALU.add))
            p.v(lambda e: e.tensor_tensor(xk_t, muk_sb, xn_t, op=ALU.mult))
            p.v(lambda e: e.tensor_tensor(tmp_t, omuk_sb, xp_t, op=ALU.mult))
            p.v(lambda e: e.tensor_tensor(xk_t, xk_t, tmp_t, op=ALU.add))
            s_used = 0
            for src, wsb, dst, act in [(xr_t, wr_sb, r_f, AF.Sigmoid),
                                       (xk_t, wk2_sb, krel_f, AF.Relu)]:
                vq = p.v(lambda e, src=src: e.tensor_copy(mixb, src), inc="v")
                for j in range(NC):
                    w = [("v", vq)] if j == 0 else []
                    ttr = p.t(lambda e, j=j: e.transpose(
                        psT[:, j * 128:(j + 1) * 128], mixb[:, j * 128:(j + 1) * 128], ident_sb),
                        inc="t" if j == NC - 1 else None, waits=w)
                vx = p.v(lambda e: e.tensor_copy(mixT, psT[:]), inc="v", waits=[("t", ttr)])
                for esl in range(2):
                    pso = ps_big[:, 1024 + esl * 512:1024 + (esl + 1) * 512]
                    for c in range(NC):
                        w = [("v", vx), ("s", s_used)] if c == 0 else []
                        tmm = p.t(lambda e, c=c, esl=esl, wsb=wsb, pso=pso: e.matmul(
                            pso, mixT[:, c * 128:(c + 1) * 128],
                            wsb[:, c, esl * 512:(esl + 1) * 512],
                            start=(c == 0), stop=(c == NC - 1)),
                            inc="t" if c == NC - 1 else None, waits=w)
                    s_used = p.s(lambda e, esl=esl, dst=dst, act=act, pso=pso: e.activation(
                        dst[:, esl * 512:(esl + 1) * 512], pso, act),
                        inc="s", waits=[("t", tmm)])
            vq = p.v(lambda e: e.tensor_tensor(kact, krel_f, krel_f, op=ALU.mult), inc="v",
                     waits=[("s", s_used)])
            for j in range(NC):
                w = [("v", vq)] if j == 0 else []
                ttr = p.t(lambda e, j=j: e.transpose(
                    psT[:, j * 128:(j + 1) * 128], kact[:, j * 128:(j + 1) * 128], ident_sb),
                    inc="t" if j == NC - 1 else None, waits=w)
            vx = p.v(lambda e: e.tensor_copy(kaT, psT[:]), inc="v", waits=[("t", ttr)])
            v_used = 0
            for esl in range(2):
                pso = ps_big[:, 1024 + esl * 512:1024 + (esl + 1) * 512]
                for c in range(NC):
                    w = [("v", vx), ("s", s_used)] if c == 0 else []
                    tmm = p.t(lambda e, c=c, esl=esl, pso=pso: e.matmul(
                        pso, kaT[:, c * 128:(c + 1) * 128], wv2_sb[:, c, esl * 512:(esl + 1) * 512],
                        start=(c == 0), stop=(c == NC - 1)),
                        inc="t" if c == NC - 1 else None, waits=w)
                v_used = p.v(lambda e, esl=esl, pso=pso: e.tensor_tensor(
                    rk_f[:, esl * 512:(esl + 1) * 512], pso, r_f[:, esl * 512:(esl + 1) * 512],
                    op=ALU.mult), inc="v", waits=[("t", tmm)])
            s_g = p.s(lambda e: e.activation(g_sb, rk_f, AF.Silu), inc="s", waits=[("v", v_used)])
            for j in range(NC):
                ttr = p.t(lambda e, j=j, ti=ti: e.transpose(
                    psT[:, j * 128:(j + 1) * 128], h3t[ti % 2][:, j, :], ident_sb),
                    inc="t" if j == NC - 1 else None)
            p.v(lambda e: e.tensor_copy(h3tok, psT[:]), waits=[("t", ttr)])
            p.v(lambda e: e.tensor_tensor(s_f, s_f, h3tok, op=ALU.add))
            p.v(lambda e: e.tensor_copy(s_bf, s_f))
            p.v(lambda e: e.tensor_tensor(sg_f, s_bf, g_sb, op=ALU.mult), waits=[("s", s_g)])
            v_done = p.v(lambda e: e.tensor_tensor(xo_f, x_tl, sg_f, op=ALU.add), inc="v")
            if DEBUG:
                p.y(lambda e, ti=ti: e.dma_start(out=g3_h[ti * 128:(ti + 1) * 128, :], in_=g_sb),
                    inc="st", waits=[("s", s_g)])
            p.y(lambda e, ti=ti: e.dma_start(out=x_out[ti * 128:(ti + 1) * 128, :], in_=xo_f),
                inc="st", waits=[("v", v_done)])
        ar.pop()
        p.y(lambda e: e.nop(), waits=[("st", p.now()["st"])])
        p.build(nc)
    return nc


_CONST_CACHE = {}


def host_consts():
    if _CONST_CACHE:
        return _CONST_CACHE
    c = {}
    c["ident"] = np.eye(128, dtype=ml_dtypes.bfloat16)
    c["ones64"] = np.ones((1, 64), dtype=ml_dtypes.bfloat16)
    freqs = (np.float32(10000.0) ** (np.arange(0, HD, 2.0, dtype=np.float32) /
                                     np.float32(HD))).astype(np.float32)
    t = np.arange(T, dtype=np.float32)
    ang = (t[:, None] * freqs[None, :]).astype(np.float32)
    cos = np.cos(ang.astype(np.float64)).astype(np.float32)
    sin = np.sin(ang.astype(np.float64)).astype(np.float32)
    A = np.empty((128, T), np.float32)
    B = np.empty((128, T), np.float32)
    for pm in range(128):
        j = pm % 32
        A[pm] = cos[:, j]
        B[pm] = sin[:, j] if (pm % 64) >= 32 else -sin[:, j]
    c["rotA"] = A.astype(ml_dtypes.bfloat16)
    c["rotB"] = B.astype(ml_dtypes.bfloat16)
    P = np.zeros((128, 128), np.float32)
    for pm in range(128):
        base = (pm // 64) * 64
        off = pm % 64
        src = base + ((off + 32) % 64)
        P[src, pm] = 1.0
    c["perm"] = P.astype(ml_dtypes.bfloat16)
    cm = np.zeros((4, 128, 512), np.float32)
    for r in range(4):
        for kk in range(128):
            cm[r, kk, kk + r * 128:] = 1.0
    c["cmask"] = cm.astype(ml_dtypes.bfloat16)
    tr = np.zeros((128, 128), np.float32)
    for e in range(128):
        tr[e, e:] = 1.0
    c["trilm"] = tr.astype(ml_dtypes.bfloat16)
    _CONST_CACHE.update(c)
    return c


def rotary_perm_idx():
    idx = np.arange(3 * D)
    for part in range(2):
        for h in range(NH):
            base = part * D + h * HD
            ev = base + np.arange(0, HD, 2)
            od = base + np.arange(1, HD, 2)
            idx[base:base + 32] = ev
            idx[base + 32:base + 64] = od
    return idx


def layer_inputs(layer, ins):
    c = host_consts()
    d = {"ident": c["ident"], "ones64": c["ones64"]}
    if layer == 0:
        idx = rotary_perm_idx()
        w = np.ascontiguousarray(ins["L0_attn_qkv_w"][:, idx], dtype=np.float32)
        b = np.ascontiguousarray(np.asarray(ins["L0_attn_qkv_b"], dtype=np.float32)[idx])
        d["w_o"] = np.ascontiguousarray(ins["L0_attn_out_w"], dtype=np.float32)
        bo = np.asarray(ins["L0_attn_out_b"], dtype=np.float32)
        d["perm"] = c["perm"]
        d["rotA"] = c["rotA"]
        d["rotB"] = c["rotB"]
        pref = "L0"
    elif layer == 1:
        w = np.ascontiguousarray(np.asarray(ins["L1_attn_w_qkv"], dtype=np.float32).T)
        b = np.asarray(ins["L1_attn_b_qkv"], dtype=np.float32)
        d["w_o"] = np.ascontiguousarray(np.asarray(ins["L1_attn_w_o"], dtype=np.float32).T)
        bo = np.asarray(ins["L1_attn_b_o"], dtype=np.float32)
        pref = "L1"
    else:
        w = np.ascontiguousarray(np.asarray(ins["L2_attn_w_in"], dtype=np.float32).T)
        b = np.asarray(ins["L2_attn_b_in"], dtype=np.float32)
        d["w_o"] = np.ascontiguousarray(np.asarray(ins["L2_attn_w_out"], dtype=np.float32).T)
        bo = np.asarray(ins["L2_attn_b_out"], dtype=np.float32)
        d["cmask"] = c["cmask"]
        pref = "L2"
    d["w_qkv"] = w
    d["b_pc"] = np.ascontiguousarray(b.reshape(24, 128).T)
    d["bv_bc"] = np.ascontiguousarray(np.broadcast_to(b[2 * D:], (128, D)))
    d["bo_bc"] = np.ascontiguousarray(np.broadcast_to(bo, (128, D)))
    d["w_gate"] = np.ascontiguousarray(ins[f"{pref}_ffn_w_gate"], dtype=np.float32)
    d["w_up"] = np.ascontiguousarray(ins[f"{pref}_ffn_w_up"], dtype=np.float32)
    d["w_down"] = np.ascontiguousarray(ins[f"{pref}_ffn_w_down"], dtype=np.float32)
    return d


def l3_inputs(ins):
    c = host_consts()
    f = np.float32
    d = {"ident": c["ident"], "trilm": c["trilm"]}
    for k, src in [("wq", "L3_attn_wq"), ("wk", "L3_attn_wk"), ("wv", "L3_attn_wv"),
                   ("wo", "L3_attn_wo")]:
        d[k] = np.ascontiguousarray(np.asarray(ins[src], dtype=f))
    for k, src in [("ln1g", "L3_attn_ln1_g"), ("ln1b", "L3_attn_ln1_b"),
                   ("ln2g", "L3_attn_ln2_g"), ("ln2b", "L3_attn_ln2_b")]:
        d[k] = np.ascontiguousarray(np.broadcast_to(np.asarray(ins[src], dtype=f), (128, D)))
    d["cw1"] = np.ascontiguousarray(np.asarray(ins["L3_attn_cms_w1"], dtype=f))
    d["cw2"] = np.ascontiguousarray(np.asarray(ins["L3_attn_cms_w2"], dtype=f))
    b1 = np.asarray(ins["L3_attn_cms_b1"], dtype=f)
    b2 = np.asarray(ins["L3_attn_cms_b2"], dtype=f)
    d["cb1"] = np.ascontiguousarray(np.stack([b1[l].reshape(32, 128).T for l in range(3)]))
    d["cb2"] = np.ascontiguousarray(np.stack([b2[l].reshape(8, 128).T for l in range(3)]))
    for k, src in [("wrT", "L3_ffn_W_r"), ("wkT", "L3_ffn_W_k"), ("wvT", "L3_ffn_W_v")]:
        d[k] = np.ascontiguousarray(np.asarray(ins[src], dtype=f).T)
    d["mur"] = np.ascontiguousarray(np.broadcast_to(np.asarray(ins["L3_ffn_mu_r"], dtype=f), (128, D)))
    d["muk"] = np.ascontiguousarray(np.broadcast_to(np.asarray(ins["L3_ffn_mu_k"], dtype=f), (128, D)))
    return d


def kernel(**ins):
    x = np.ascontiguousarray(np.asarray(ins["L0_attn_x"], dtype=np.float32))
    B = x.shape[0]
    shards = [np.ascontiguousarray(x[b]) for b in range(B)]
    for layer in range(4):
        nc = build_attn_layer(layer) if layer < 3 else build_l3()
        lid = layer_inputs(layer, ins) if layer < 3 else l3_inputs(ins)
        in_maps = [dict(lid, x=shards[b]) for b in range(B)]
        res = run_bass_kernel_spmd(nc, in_maps, core_ids=list(range(B)))
        shards = [np.asarray(res.results[b]["x_out"]) for b in range(B)]
    return np.stack(shards).astype(np.float32)


# version 02
# speedup vs baseline: 1.1811x; 1.1811x over previous
import numpy as np
import ml_dtypes
import concourse.bass as bass
import concourse.mybir as mybir
from concourse.bass_utils import run_bass_kernel_spmd
from contextlib import ExitStack

F32 = mybir.dt.float32
BF16 = mybir.dt.bfloat16
AF = mybir.ActivationFunctionType
ALU = mybir.AluOpType
AX = mybir.AxisListType.X

T, D, I, NH, HD = 4096, 1024, 2816, 16, 64
NT = T // 128
NC = D // 128
NI = I // 128
MAGIC = 12582912.0
DEBUG = False
LAST_PEAK = {}

ENGS = ("tensor", "vector", "scalar", "gpsimd", "sync")
DMA_KEYS = ("ld", "lg", "st")


class Prog:

    def __init__(self):
        self.streams = {e: [] for e in ENGS}
        self.cnt = {k: 0 for k in ("t", "v", "s", "g", "ld", "lg", "st")}
        self.pending = {e: [] for e in ENGS}

    def _emit(self, eng, fn, inc=None, waits=()):
        ws = {}
        for k, val in list(self.pending[eng]) + list(waits):
            if val and val > 0:
                ws[k] = max(ws.get(k, 0), val)
        self.pending[eng] = []
        self.streams[eng].append((fn, inc, sorted(ws.items())))
        if inc is not None:
            self.cnt[inc] += 16 if inc in DMA_KEYS else 1
            return self.cnt[inc]
        return None

    def t(self, fn, inc=None, waits=()):
        return self._emit("tensor", fn, inc, waits)

    def v(self, fn, inc=None, waits=()):
        return self._emit("vector", fn, inc, waits)

    def s(self, fn, inc=None, waits=()):
        return self._emit("scalar", fn, inc, waits)

    def g(self, fn, inc=None, waits=()):
        return self._emit("gpsimd", fn, inc, waits)

    def y(self, fn, inc=None, waits=()):
        return self._emit("sync", fn, inc, waits)

    def now(self):
        return dict(self.cnt)

    def barrier(self, engines=ENGS):
        snap = [(k, v) for k, v in self.cnt.items() if v > 0]
        for e in engines:
            self.pending[e].extend(snap)

    def build(self, nc):
        with ExitStack() as stk:
            sems = {k: stk.enter_context(nc.semaphore(f"sem_{k}")) for k in self.cnt}
            block = stk.enter_context(nc.Block())
            for eng_name in ENGS:
                stream = self.streams[eng_name]
                if not stream:
                    continue

                def make(stream_):
                    def run(eng):
                        for fn, inc, ws in stream_:
                            for k, val in ws:
                                eng.wait_ge(sems[k], val)
                            inst = fn(eng)
                            if inc is not None:
                                inst.then_inc(sems[inc], 16 if inc in DMA_KEYS else 1)
                    return run

                getattr(block, eng_name)(make(stream))


class Arena:
    def __init__(self, tensors):
        self.tensors = tensors
        self.off = {d: 0 for d in tensors}
        self.stack = []
        self.peak = {d: 0 for d in tensors}

    def push(self):
        self.stack.append(dict(self.off))

    def pop(self):
        self.off = self.stack.pop()

    def alloc(self, dtype, shape):
        cols = 1
        for s in shape[1:]:
            cols *= s
        esz = 2 if dtype == BF16 else 4
        o = self.off[dtype]
        o += (-(o * esz)) % 32 // esz
        self.off[dtype] = o + cols
        self.peak[dtype] = max(self.peak[dtype], self.off[dtype])
        tsr = self.tensors[dtype]
        assert self.off[dtype] <= tsr.shape[1], (
            f"arena overflow {dtype}: {self.off[dtype]} > {tsr.shape[1]}")
        ap = tsr[:][:, o:o + cols]
        if len(shape) > 2:
            names = " ".join(f"d{i}" for i in range(len(shape) - 1))
            ap = ap.rearrange(f"p ({names}) -> p {names}",
                              **{f"d{i}": shape[i + 1] for i in range(len(shape) - 1)})
        if shape[0] != 128:
            ap = ap[0:shape[0]]
        return ap


def emit_quant(p, eng_key, x_ap, out_ap, tmps, K, waits_first=(), inc_last=None):
    em = getattr(p, eng_key)
    NB = K // 16
    A, B = tmps["A"][:, :K], tmps["B"][:, :K]
    BM, SC = tmps["BM"][:, :NB], tmps["SC"][:, :NB]
    xv = x_ap.rearrange("p (b s) -> p b s", s=16)
    p.v(lambda e: e.tensor_reduce(BM, xv, axis=AX, op=ALU.max, apply_absolute_value=True),
        waits=waits_first)
    p.v(lambda e: e.tensor_scalar(SC, BM, 1.0 / 6.0, 1e-12, op0=ALU.mult, op1=ALU.max))
    p.spacer()
    vstat = p.v(lambda e: e.reciprocal(BM, SC), inc="v")
    if eng_key == "v":
        p.spacer()
    bmb = BM.broadcast_to([128, NB, 16])
    scb = SC.broadcast_to([128, NB, 16])
    tmps["_vstat"] = vstat
    em(lambda e: e.tensor_tensor(xv, xv, bmb, op=ALU.mult),
       waits=[("v", vstat)] if eng_key != "v" else ())
    em(lambda e: e.tensor_scalar(A, x_ap, 2.0, -2.0, op0=ALU.min, op1=ALU.max))
    em(lambda e: e.tensor_scalar(A, A, 2.0, MAGIC, op0=ALU.mult, op1=ALU.add))
    em(lambda e: e.tensor_scalar(A, A, MAGIC, 0.5, op0=ALU.subtract, op1=ALU.mult))
    em(lambda e: e.tensor_scalar(B, x_ap, 4.0, -4.0, op0=ALU.min, op1=ALU.max))
    em(lambda e: e.tensor_scalar(B, B, MAGIC, MAGIC, op0=ALU.add, op1=ALU.subtract))
    em(lambda e: e.tensor_tensor(A, A, B, op=ALU.add))
    em(lambda e: e.tensor_scalar(B, B, 2.0, -2.0, op0=ALU.min, op1=ALU.max))
    em(lambda e: e.tensor_tensor(A, A, B, op=ALU.subtract))
    em(lambda e: e.tensor_scalar(B, x_ap, 0.5, MAGIC, op0=ALU.mult, op1=ALU.add))
    em(lambda e: e.tensor_scalar(B, B, MAGIC, 2.0, op0=ALU.subtract, op1=ALU.mult))
    em(lambda e: e.tensor_tensor(A, A, B, op=ALU.add))
    em(lambda e: e.tensor_scalar(B, B, 4.0, -4.0, op0=ALU.min, op1=ALU.max))
    em(lambda e: e.tensor_tensor(A, A, B, op=ALU.subtract))
    av = A.rearrange("p (b s) -> p b s", s=16)
    ov = out_ap.rearrange("p (b s) -> p b s", s=16)
    return em(lambda e: e.tensor_tensor(ov, av, scb, op=ALU.mult), inc=inc_last)


def emit_wquant_transpose(p, w_src, n_rows, kk, dstT, wld, wqt, qtmp, psT, ident_sb, state):
    ntile = n_rows // 128
    nblk = kk // 128
    for r in range(ntile):
        lv = p.y(lambda e, r=r: e.dma_start(out=wld[:, 0:kk], in_=w_src[r * 128:(r + 1) * 128, :]),
                 inc="ld", waits=[("g", state["g"]), ("t", state["wt"]),
                                  ("v", qtmp.get("_vstat", 0))])
        state["g"] = emit_quant(p, "g", wld[:, 0:kk], wqt[:, 0:kk], qtmp, kk,
                                waits_first=[("ld", lv)], inc_last="g")
        for jb in range(0, nblk, 8):
            jn = min(8, nblk - jb)
            for j in range(jn):
                w = []
                if j == 0:
                    w = [("g", state["g"]), ("v", state["wv"])]
                tt = p.t(lambda e, j=j, jb=jb: e.transpose(
                    psT[:, j * 128:(j + 1) * 128], wqt[:, (jb + j) * 128:(jb + j + 1) * 128],
                    ident_sb), inc="t" if j == jn - 1 else None, waits=w)
            state["wt"] = tt
            dst = dstT[:, jb:jb + jn, r * 128:(r + 1) * 128]
            src = psT[:].rearrange("p (c q) -> p c q", q=128)[:, 0:jn, :]
            state["wv"] = p.v(lambda e, dst=dst, src=src: e.tensor_copy(dst, src),
                              inc="v", waits=[("t", tt)])


def emit_attn_layer(nc, p, ar, ps_big, psT, ident_sb, ones64_sb, layer, pfx, x_in, x_out):
    scale = {0: 0.125, 1: 0.03125, 2: 0.125}[layer]
    causal = layer == 2
    rotary = layer == 0
    add_xn = layer == 1
    dt = lambda name, *a, **k: nc.dram_tensor(pfx + name, *a, **k)
    w_qkv = dt("w_qkv", [D, 3 * D], F32, kind="ExternalInput").ap()
    b_pc = dt("b_pc", [128, 24], F32, kind="ExternalInput").ap()
    bv_bc = dt("bv_bc", [128, D], F32, kind="ExternalInput").ap()
    w_o = dt("w_o", [D, D], F32, kind="ExternalInput").ap()
    bo_bc = dt("bo_bc", [128, D], F32, kind="ExternalInput").ap()
    w_gate = dt("w_gate", [I, D], F32, kind="ExternalInput").ap()
    w_up = dt("w_up", [I, D], F32, kind="ExternalInput").ap()
    w_down = dt("w_down", [D, I], F32, kind="ExternalInput").ap()
    if rotary:
        perm_in = dt("perm", [128, 128], BF16, kind="ExternalInput").ap()
        rotA = dt("rotA", [128, T], BF16, kind="ExternalInput").ap()
        rotB = dt("rotB", [128, T], BF16, kind="ExternalInput").ap()
    if causal:
        cmask = dt("cmask", [4, 128, 512], BF16, kind="ExternalInput").ap()

    kind_dbg = "ExternalOutput" if DEBUG else "Internal"
    xn_h = dt("xn_h", [T, D], F32, kind=kind_dbg).ap()
    qT_h = dt("qT_h", [D, T], BF16, kind=kind_dbg).ap()
    kT_h = dt("kT_h", [D, T], BF16, kind=kind_dbg).ap()
    vP_h = dt("vP_h", [T, NH * 65], BF16, kind=kind_dbg).ap()
    hT_h = dt("hT_h", [D, T], BF16, kind=kind_dbg).ap()
    s_h = dt("s_h", [T, D], BF16, kind=kind_dbg).ap()
    hqT_h = dt("hqT_h", [I, T], BF16, kind="Internal").ap()
    g_h = dt("g_h", [T, D], BF16, kind=kind_dbg).ap() if DEBUG else None

    if True:
        ar.push()
        xnT_sb = ar.alloc(BF16, [128, NC, T])

        ar.push()
        xf = [ar.alloc(F32, [128, D]) for _ in range(2)]
        sq = ar.alloc(F32, [128, D])
        xn32 = [ar.alloc(F32, [128, D]) for _ in range(2)]
        xnb = [ar.alloc(BF16, [128, D]) for _ in range(2)]
        st8 = ar.alloc(F32, [128, 8])
        nsum, sqs, negm, ex2, m2, var, lntmp, rstd = (st8[:, i:i + 1] for i in range(8))
        v_consumed = [0, 0]
        tr_done = [0, 0]
        st_xn = [0, 0]
        v_evac_done = 0
        for i in range(NT):
            bi = i % 2
            lv = p.y(lambda e, i=i, bi=bi: e.dma_start(out=xf[bi], in_=x_in[i * 128:(i + 1) * 128, :]),
                     inc="ld", waits=[("v", v_consumed[bi])])
            p.v(lambda e, bi=bi: e.tensor_tensor(sq, xf[bi], xf[bi], op=ALU.mult),
                waits=[("ld", lv)])
            p.v(lambda e, bi=bi: e.tensor_reduce(nsum, xf[bi], axis=AX, op=ALU.add, negate=True))
            p.v(lambda e: e.tensor_reduce(sqs, sq, axis=AX, op=ALU.add))
            p.v(lambda e: e.tensor_scalar_mul(negm, nsum, 1.0 / D))
            p.v(lambda e: e.tensor_scalar(ex2, sqs, 1.0 / D, 1e-5, op0=ALU.mult, op1=ALU.add))
            p.v(lambda e: e.tensor_tensor(m2, negm, negm, op=ALU.mult))
            p.spacer()
            vst = p.v(lambda e: e.tensor_tensor(var, ex2, m2, op=ALU.subtract), inc="v")
            p.s(lambda e: e.activation(lntmp, var, AF.Ln), waits=[("v", vst)])
            sst = p.s(lambda e: e.activation(rstd, lntmp, AF.Exp, scale=-0.5), inc="s")
            p.v(lambda e: e.tensor_tensor(negm, negm, rstd, op=ALU.mult), waits=[("s", sst)])
            p.spacer()
            p.v(lambda e, bi=bi: e.tensor_scalar(xn32[bi], xf[bi], rstd[:, 0:1], negm[:, 0:1],
                                                 op0=ALU.mult, op1=ALU.add),
                waits=[("st", st_xn[bi])])
            vdone = p.v(lambda e, bi=bi: e.tensor_copy(xnb[bi], xn32[bi]), inc="v",
                        waits=[("t", tr_done[bi])])
            v_consumed[bi] = vdone
            st_xn[bi] = p.y(lambda e, i=i, bi=bi: e.dma_start(
                out=xn_h[i * 128:(i + 1) * 128, :], in_=xn32[bi]),
                inc="st", waits=[("v", vdone)])
            for j in range(NC):
                w = [("v", vdone), ("v", v_evac_done)] if j == 0 else []
                last = p.t(lambda e, j=j, bi=bi: e.transpose(psT[:, j * 128:(j + 1) * 128],
                                                             xnb[bi][:, j * 128:(j + 1) * 128],
                                                             ident_sb),
                           inc="t" if j == NC - 1 else None, waits=w)
            tr_done[bi] = last
            dst = xnT_sb[:, :, i * 128:(i + 1) * 128]
            v_evac_done = p.v(lambda e, dst=dst: e.tensor_copy(
                dst, psT[:].rearrange("p (c q) -> p c q", q=128)),
                inc="v", waits=[("t", last)])
        ar.pop()
        p.barrier()

        ar.push()
        wq_sb = ar.alloc(BF16, [128, NC, 3 * D])
        bq_sb = ar.alloc(F32, [128, 24])
        bvb_sb = ar.alloc(F32, [128, D])
        qe_sb = ar.alloc(BF16, [128, T])
        vt_sb = ar.alloc(BF16, [128, NH * 65])
        t1_sb = ar.alloc(F32, [128, 512])
        t2_sb = ar.alloc(F32, [128, 512])
        swp_sb = [ar.alloc(BF16, [128, 512]) for _ in range(2)]
        if rotary:
            perm_sb = ar.alloc(BF16, [128, 128])
            rA_sb = ar.alloc(BF16, [128, T])
            rB_sb = ar.alloc(BF16, [128, T])
            p.y(lambda e: e.dma_start(out=perm_sb, in_=perm_in), inc="ld")
            p.y(lambda e: e.dma_start(out=rA_sb, in_=rotA), inc="ld")
            p.y(lambda e: e.dma_start(out=rB_sb, in_=rotB), inc="ld")
        p.g(lambda e: e.dma_start(out=wq_sb, in_=w_qkv.rearrange("(c p) e -> p c e", p=128)),
            inc="lg")
        p.y(lambda e: e.dma_start(out=bq_sb, in_=b_pc), inc="ld")
        p.y(lambda e: e.dma_start(out=bvb_sb, in_=bv_bc), inc="ld")
        p.barrier()

        for et in range(16):
            s_evac = [0, 0]
            v_rot = 0
            for nsl in range(8):
                pb = nsl % 2
                psq = ps_big[:, pb * 512:(pb + 1) * 512]
                for c in range(NC):
                    w = [("s", s_evac[pb])] if c == 0 else []
                    tmm = p.t(lambda e, c=c, et=et, nsl=nsl, psq=psq: e.matmul(
                        psq, wq_sb[:, c, et * 128:(et + 1) * 128],
                        xnT_sb[:, c, nsl * 512:(nsl + 1) * 512],
                        start=(c == 0), stop=(c == NC - 1)),
                        inc="t" if c == NC - 1 else None, waits=w)
                s_evac[pb] = p.s(lambda e, et=et, nsl=nsl, psq=psq: e.activation(
                    qe_sb[:, nsl * 512:(nsl + 1) * 512], psq, AF.Identity,
                    bias=bq_sb[:, et:et + 1]), inc="s", waits=[("t", tmm)])
            if rotary:
                v_sw = [0, 0]
                for nsl in range(8):
                    pb = nsl % 2
                    psw = ps_big[:, (2 + pb) * 512:(3 + pb) * 512]
                    tsw = p.t(lambda e, nsl=nsl, psw=psw: e.matmul(
                        psw, perm_sb, qe_sb[:, nsl * 512:(nsl + 1) * 512],
                        start=True, stop=True), inc="t",
                        waits=[("s", s_evac[0]), ("s", s_evac[1])] if nsl == 0 else
                              ([("v", v_sw[pb])] if v_sw[pb] else []))
                    ssw = p.s(lambda e, psw=psw, pb=pb: e.activation(swp_sb[pb], psw, AF.Identity),
                              inc="s", waits=[("t", tsw), ("v", v_sw[pb])])
                    sl = slice(nsl * 512, (nsl + 1) * 512)
                    p.v(lambda e, sl=sl: e.tensor_tensor(t1_sb, rA_sb[:, sl], qe_sb[:, sl],
                                                         op=ALU.mult), waits=[("s", ssw)])
                    p.v(lambda e, sl=sl, pb=pb: e.tensor_tensor(t2_sb, rB_sb[:, sl], swp_sb[pb],
                                                                op=ALU.mult))
                    v_sw[pb] = v_rot = p.v(lambda e, sl=sl: e.tensor_tensor(
                        qe_sb[:, sl], t1_sb, t2_sb, op=ALU.add), inc="v")
            dst_h = qT_h if et < 8 else kT_h
            row = (et % 8) * 128
            stw = [("v", v_rot)] if rotary else [("s", s_evac[0]), ("s", s_evac[1])]
            stv = p.y(lambda e, dst_h=dst_h, row=row: e.dma_start(
                out=dst_h[row:row + 128, :], in_=qe_sb), inc="st", waits=stw)
            p.pending["scalar"].append(("st", stv))
            if rotary:
                p.pending["tensor"].append(("st", stv))

        ovv = vt_sb.rearrange("p (h c) -> p h c", c=65)[:, :, 64:65]
        vmemset = p.v(lambda e: e.memset(ovv, 1.0), inc="v")
        p.pending["vector"].append(("v", vmemset))
        st_v = 0
        for ti in range(NT):
            v_evac = [0, 0]
            for vsl in range(2):
                psv = ps_big[:, vsl * 512:(vsl + 1) * 512]
                for c in range(NC):
                    w = [("v", v_evac[vsl]), ("st", st_v)] if c == 0 else []
                    tmm = p.t(lambda e, c=c, ti=ti, vsl=vsl, psv=psv: e.matmul(
                        psv, xnT_sb[:, c, ti * 128:(ti + 1) * 128],
                        wq_sb[:, c, 2 * D + vsl * 512:2 * D + (vsl + 1) * 512],
                        start=(c == 0), stop=(c == NC - 1)),
                        inc="t" if c == NC - 1 else None, waits=w)
                dst = vt_sb.rearrange("p (h c) -> p h c", c=65)[:, vsl * 8:(vsl + 1) * 8, 0:64]
                psv3 = psv.rearrange("p (h c) -> p h c", c=64)
                bvb3 = bvb_sb[:, vsl * 512:(vsl + 1) * 512].rearrange("p (h c) -> p h c", c=64)
                v_evac[vsl] = p.v(lambda e, psv3=psv3, bvb3=bvb3, dst=dst: e.tensor_tensor(
                    dst, psv3, bvb3, op=ALU.add), inc="v", waits=[("t", tmm)])
            st_v = p.y(lambda e, ti=ti: e.dma_start(out=vP_h[ti * 128:(ti + 1) * 128, :], in_=vt_sb),
                       inc="st", waits=[("v", v_evac[0]), ("v", v_evac[1])])
        ar.pop()
        ar.pop()
        p.barrier()

        ar.push()
        qh_sb = ar.alloc(BF16, [64, T])
        kh_sb = ar.alloc(BF16, [64, T])
        vh_sb = ar.alloc(BF16, [128, NT, 65])
        hT_sb = ar.alloc(BF16, [128, T])
        u_sb = [ar.alloc(BF16, [128, 512]) for _ in range(2)]
        numsb = ar.alloc(F32, [64, 512])
        rd_sb = ar.alloc(F32, [1, 512])
        rdb_sb = ar.alloc(BF16, [1, 512])
        if causal:
            cm_sb = ar.alloc(BF16, [128, 4, 512])
            p.y(lambda e: e.dma_start(out=cm_sb, in_=cmask.rearrange("r p q -> p r q")), inc="ld")
        ps_sc = [ps_big[:, 0:512], ps_big[:, 512:1024]]
        ps_num = ps_big[:, 1024:1536]
        ps_bc = ps_big[:, 1536:2048]
        st_hT = 0
        for h in range(NH):
            p.y(lambda e, h=h: e.dma_start(out=qh_sb, in_=qT_h[h * 64:(h + 1) * 64, :]), inc="ld")
            p.y(lambda e, h=h: e.dma_start(out=kh_sb, in_=kT_h[h * 64:(h + 1) * 64, :]), inc="ld")
            lq = p.y(lambda e, h=h: e.dma_start(
                out=vh_sb, in_=vP_h.rearrange("(t p) c -> p t c", p=128)[:, :, h * 65:(h + 1) * 65]),
                inc="ld")
            v_norm = 0
            s_u = [0, 0]
            v_u = [0, 0]
            for qb in range(8):
                t_av = [0, 0]
                kt_list = list(range(NT)) if not causal else list(range(min(NT, (qb + 1) * 4)))
                kt_first, kt_last = kt_list[0], kt_list[-1]
                for kt in kt_list:
                    bb = kt % 2
                    w = [("ld", lq)] if (qb == 0 and kt < 2) else []
                    if s_u[bb]:
                        w.append(("s", s_u[bb]))
                    if v_u[bb]:
                        w.append(("v", v_u[bb]))
                    tsc = p.t(lambda e, kt=kt, qb=qb, bb=bb: e.matmul(
                        ps_sc[bb], kh_sb[:, kt * 128:(kt + 1) * 128],
                        qh_sb[:, qb * 512:(qb + 1) * 512], start=True, stop=True),
                        inc="t", waits=w)
                    w = [("t", tsc)]
                    if t_av[bb]:
                        w.append(("t", t_av[bb]))
                    s_u[bb] = p.s(lambda e, bb=bb: e.activation(
                        u_sb[bb], ps_sc[bb], AF.Exp, scale=scale), inc="s", waits=w)
                    dep = ("s", s_u[bb])
                    if causal and kt * 128 >= qb * 512:
                        r = (kt * 128 - qb * 512) // 128
                        v_u[bb] = p.v(lambda e, bb=bb, r=r: e.tensor_tensor(
                            u_sb[bb], cm_sb[:, r, :], u_sb[bb], op=ALU.mult),
                            inc="v", waits=[("s", s_u[bb])])
                        dep = ("v", v_u[bb])
                    w = [dep]
                    if kt == kt_first and v_norm:
                        w.append(("v", v_norm))
                    t_av[bb] = p.t(lambda e, kt=kt, bb=bb, st=(kt == kt_first),
                                   sp=(kt == kt_last): e.matmul(
                        ps_num[0:65, :], vh_sb[:, kt, :], u_sb[bb], start=st, stop=sp),
                        inc="t", waits=w)
                p.v(lambda e: e.reciprocal(rd_sb, ps_num[64:65, :]),
                    waits=[("t", max(t_av))])
                p.v(lambda e: e.tensor_copy(rdb_sb, rd_sb))
                vv = p.v(lambda e: e.tensor_copy(numsb, ps_num[0:64, :]), inc="v")
                tbc = p.t(lambda e: e.matmul(ps_bc[0:64, :], ones64_sb, rdb_sb,
                                             start=True, stop=True), inc="t",
                          waits=[("v", vv)])
                w = [("t", tbc), ("st", st_hT)]
                v_norm = p.v(lambda e, qb=qb: e.tensor_tensor(
                    hT_sb[0:64, qb * 512:(qb + 1) * 512], numsb, ps_bc[0:64, :], op=ALU.mult),
                    inc="v", waits=w)
            st_hT = p.y(lambda e, h=h: e.dma_start(out=hT_h[h * 64:(h + 1) * 64, :],
                                                   in_=hT_sb[0:64]),
                        inc="st", waits=[("v", v_norm)])
            p.pending["sync"].append(("t", p.now()["t"]))
        ar.pop()
        p.barrier()

        ar.push()
        wo_sb = ar.alloc(BF16, [128, NC, D])
        bob_sb = ar.alloc(F32, [128, D])
        hTt_sb = [ar.alloc(BF16, [128, NC, 128]) for _ in range(2)]
        s_sb = ar.alloc(BF16, [128, D])
        xn_f = ar.alloc(F32, [128, D])
        t14 = ar.alloc(F32, [128, 512])
        p.g(lambda e: e.dma_start(out=wo_sb, in_=w_o.rearrange("(c p) e -> p c e", p=128)), inc="lg")
        p.y(lambda e: e.dma_start(out=bob_sb, in_=bo_bc), inc="ld")
        p.barrier()
        t_cons = [0, 0]
        st_s = 0
        v_ev_prev = 0
        for ti in range(NT):
            bi = ti % 2
            lv = p.y(lambda e, ti=ti, bi=bi: e.dma_start(
                out=hTt_sb[bi],
                in_=hT_h.rearrange("(c p) t -> p c t", p=128)[:, :, ti * 128:(ti + 1) * 128]),
                inc="ld", waits=[("t", t_cons[bi])])
            if add_xn:
                lv = p.y(lambda e, ti=ti: e.dma_start(out=xn_f, in_=xn_h[ti * 128:(ti + 1) * 128, :]),
                         inc="ld", waits=[("v", v_ev_prev)])
            v_ev = 0
            for esl in range(2):
                pso = ps_big[:, esl * 512:(esl + 1) * 512]
                for c in range(NC):
                    w = [("ld", lv), ("st", st_s), ("v", v_ev_prev)] if c == 0 else []
                    tmm = p.t(lambda e, c=c, esl=esl, bi=bi, pso=pso: e.matmul(
                        pso, hTt_sb[bi][:, c, :], wo_sb[:, c, esl * 512:(esl + 1) * 512],
                        start=(c == 0), stop=(c == NC - 1)),
                        inc="t" if c == NC - 1 else None, waits=w)
                t_cons[bi] = tmm
                sl = slice(esl * 512, (esl + 1) * 512)
                if add_xn:
                    p.v(lambda e, pso=pso, sl=sl: e.tensor_tensor(t14, pso, bob_sb[:, sl],
                                                                  op=ALU.add),
                        waits=[("t", tmm)])
                    v_ev = p.v(lambda e, sl=sl: e.tensor_tensor(s_sb[:, sl], t14, xn_f[:, sl],
                                                                op=ALU.add), inc="v")
                else:
                    v_ev = p.v(lambda e, pso=pso, sl=sl: e.tensor_tensor(
                        s_sb[:, sl], pso, bob_sb[:, sl], op=ALU.add), inc="v",
                        waits=[("t", tmm)])
            v_ev_prev = v_ev
            st_s = p.y(lambda e, ti=ti: e.dma_start(out=s_h[ti * 128:(ti + 1) * 128, :], in_=s_sb),
                       inc="st", waits=[("v", v_ev)])
        ar.pop()
        p.barrier()

        ar.push()
        wgT_sb = ar.alloc(BF16, [128, NC, I])
        wuT_sb = ar.alloc(BF16, [128, NC, I])
        wqt = ar.alloc(BF16, [128, I])
        wld = ar.alloc(F32, [128, I])
        qtmp = {"A": ar.alloc(F32, [128, I]), "B": ar.alloc(F32, [128, I]),
                "BM": ar.alloc(F32, [128, I // 16]), "SC": ar.alloc(F32, [128, I // 16])}
        wstate = {"g": 0, "wt": 0, "wv": 0}
        emit_wquant_transpose(p, w_gate, I, D, wgT_sb, wld, wqt, qtmp, psT, ident_sb, wstate)
        emit_wquant_transpose(p, w_up, I, D, wuT_sb, wld, wqt, qtmp, psT, ident_sb, wstate)
        p.barrier()

        xnq_f = ar.alloc(F32, [128, D])
        xq_sb = ar.alloc(BF16, [128, D])
        xqT_sb = ar.alloc(BF16, [128, D])
        h_f = ar.alloc(F32, [128, I])
        hq_sb = ar.alloc(BF16, [128, I])
        hqT_sb = ar.alloc(BF16, [128, NI, 128])
        GSL = [(i * 512, min(I, (i + 1) * 512)) for i in range(6)]
        for ti in range(NT):
            p.barrier()
            lv = p.y(lambda e, ti=ti: e.dma_start(out=xnq_f, in_=xn_h[ti * 128:(ti + 1) * 128, :]),
                     inc="ld")
            vq = emit_quant(p, "v", xnq_f, xq_sb, qtmp, D, waits_first=[("ld", lv)], inc_last="v")
            for j in range(NC):
                w = [("v", vq)] if j == 0 else []
                ttr = p.t(lambda e, j=j: e.transpose(psT[:, j * 128:(j + 1) * 128],
                                                     xq_sb[:, j * 128:(j + 1) * 128], ident_sb),
                          inc="t" if j == NC - 1 else None, waits=w)
            vxq = p.v(lambda e: e.tensor_copy(xqT_sb, psT[:]), inc="v", waits=[("t", ttr)])
            s_ev = 0
            v_h = 0
            for phase in range(2):
                wsb = wgT_sb if phase == 0 else wuT_sb
                for c in range(NC):
                    for isl, (a, b) in enumerate(GSL):
                        w = []
                        if c == 0 and isl == 0:
                            w = [("v", vxq)] if phase == 0 else [("v", v_h), ("s", s_ev)]
                        tmm = p.t(lambda e, c=c, a=a, b=b, wsb=wsb: e.matmul(
                            ps_big[:, a:b], xqT_sb[:, c * 128:(c + 1) * 128],
                            wsb[:, c, a:b], start=(c == 0), stop=(c == NC - 1)),
                            inc="t" if (c == NC - 1 and isl == len(GSL) - 1) else None, waits=w)
                if phase == 0:
                    for isl, (a, b) in enumerate(GSL):
                        s_ev = p.s(lambda e, a=a, b=b: e.activation(
                            h_f[:, a:b], ps_big[:, a:b], AF.Silu),
                            inc="s" if isl == len(GSL) - 1 else None,
                            waits=[("t", tmm)] if isl == 0 else [])
                else:
                    for isl, (a, b) in enumerate(GSL):
                        v_h = p.v(lambda e, a=a, b=b: e.tensor_tensor(
                            h_f[:, a:b], ps_big[:, a:b], h_f[:, a:b], op=ALU.mult),
                            inc="v" if isl == len(GSL) - 1 else None,
                            waits=[("t", tmm), ("s", s_ev)] if isl == 0 else [])
            vq2 = emit_quant(p, "v", h_f, hq_sb, qtmp, I, waits_first=[("v", v_h)], inc_last="v")
            v_hq = 0
            for jb in range(0, NI, 8):
                jn = min(8, NI - jb)
                for j in range(jn):
                    w = [("v", vq2)] if (jb == 0 and j == 0) else ([("v", v_hq)] if j == 0 else [])
                    ttr2 = p.t(lambda e, j=j, jb=jb: e.transpose(
                        psT[:, j * 128:(j + 1) * 128],
                        hq_sb[:, (jb + j) * 128:(jb + j + 1) * 128], ident_sb),
                        inc="t" if j == jn - 1 else None, waits=w)
                dst = hqT_sb[:, jb:jb + jn, :]
                src = psT[:].rearrange("p (c q) -> p c q", q=128)[:, 0:jn, :]
                v_hq = p.v(lambda e, dst=dst, src=src: e.tensor_copy(dst, src),
                           inc="v", waits=[("t", ttr2)])
            p.y(lambda e, ti=ti: e.dma_start(
                out=hqT_h.rearrange("(c p) t -> p c t", p=128)[:, :, ti * 128:(ti + 1) * 128],
                in_=hqT_sb), inc="st", waits=[("v", v_hq)])
        ar.pop()
        p.barrier()

        ar.push()
        wdT_sb = ar.alloc(BF16, [128, NI, D])
        wqt2 = ar.alloc(BF16, [128, I])
        wld2 = ar.alloc(F32, [128, I])
        qtmp2 = {"A": ar.alloc(F32, [128, I]), "B": ar.alloc(F32, [128, I]),
                 "BM": ar.alloc(F32, [128, I // 16]), "SC": ar.alloc(F32, [128, I // 16])}
        wstate = {"g": 0, "wt": 0, "wv": 0}
        emit_wquant_transpose(p, w_down, D, I, wdT_sb, wld2, wqt2, qtmp2, psT, ident_sb, wstate)
        p.barrier()
        hqt_t = [ar.alloc(BF16, [128, NI, 128]) for _ in range(2)]
        g_sb = ar.alloc(BF16, [128, D])
        s_tl = ar.alloc(BF16, [128, D])
        x_tl = ar.alloc(F32, [128, D])
        sg_f = ar.alloc(F32, [128, D])
        xo_f = ar.alloc(F32, [128, D])
        t_cons = [0, 0]
        v_done = 0
        s_gdone = 0
        st_x = 0
        for ti in range(NT):
            bi = ti % 2
            lv = p.y(lambda e, ti=ti, bi=bi: e.dma_start(
                out=hqt_t[bi],
                in_=hqT_h.rearrange("(c p) t -> p c t", p=128)[:, :, ti * 128:(ti + 1) * 128]),
                inc="ld", waits=[("t", t_cons[bi])])
            lv = p.y(lambda e, ti=ti: e.dma_start(out=s_tl, in_=s_h[ti * 128:(ti + 1) * 128, :]),
                     inc="ld", waits=[("v", v_done)])
            lv = p.y(lambda e, ti=ti: e.dma_start(out=x_tl, in_=x_in[ti * 128:(ti + 1) * 128, :]),
                     inc="ld")
            t_dn = 0
            for dsl in range(2):
                psd = ps_big[:, dsl * 512:(dsl + 1) * 512]
                for c in range(NI):
                    w = [("ld", lv), ("s", s_gdone), ("v", v_done)] if c == 0 else []
                    t_dn = p.t(lambda e, c=c, dsl=dsl, psd=psd, bi=bi: e.matmul(
                        psd, hqt_t[bi][:, c, :], wdT_sb[:, c, dsl * 512:(dsl + 1) * 512],
                        start=(c == 0), stop=(c == NI - 1)),
                        inc="t" if c == NI - 1 else None, waits=w)
                t_cons[bi] = t_dn
                s_gdone = p.s(lambda e, dsl=dsl, psd=psd: e.activation(
                    g_sb[:, dsl * 512:(dsl + 1) * 512], psd, AF.Silu),
                    inc="s", waits=[("t", t_dn)])
            if DEBUG:
                p.y(lambda e, ti=ti: e.dma_start(out=g_h[ti * 128:(ti + 1) * 128, :], in_=g_sb),
                    inc="st", waits=[("s", s_gdone)])
            p.v(lambda e: e.tensor_tensor(sg_f, s_tl, g_sb, op=ALU.mult),
                waits=[("s", s_gdone), ("st", st_x)])
            v_done = p.v(lambda e: e.tensor_tensor(xo_f, x_tl, sg_f, op=ALU.add), inc="v")
            st_x = p.y(lambda e, ti=ti: e.dma_start(out=x_out[ti * 128:(ti + 1) * 128, :], in_=xo_f),
                       inc="st", waits=[("v", v_done)])
        ar.pop()
        LAST_PEAK.update({str(k): v for k, v in ar.peak.items()})


def emit_l3(nc, p, ar, ps_big, psT, ident_sb, pfx, x_in, x_out):
    HID = 4 * D
    NHID = HID // 128
    dt = lambda name, *a, **k: nc.dram_tensor(pfx + name, *a, **k)
    wq = dt("wq", [D, D], F32, kind="ExternalInput").ap()
    wk = dt("wk", [D, D], F32, kind="ExternalInput").ap()
    wv = dt("wv", [D, D], F32, kind="ExternalInput").ap()
    wo = dt("wo", [D, D], F32, kind="ExternalInput").ap()
    ln1g = dt("ln1g", [128, D], F32, kind="ExternalInput").ap()
    ln1b = dt("ln1b", [128, D], F32, kind="ExternalInput").ap()
    ln2g = dt("ln2g", [128, D], F32, kind="ExternalInput").ap()
    ln2b = dt("ln2b", [128, D], F32, kind="ExternalInput").ap()
    cw1 = dt("cw1", [3, D, HID], F32, kind="ExternalInput").ap()
    cb1 = dt("cb1", [3, 128, NHID], F32, kind="ExternalInput").ap()
    cw2 = dt("cw2", [3, HID, D], F32, kind="ExternalInput").ap()
    cb2 = dt("cb2", [3, 128, NC], F32, kind="ExternalInput").ap()
    wrT = dt("wrT", [D, D], F32, kind="ExternalInput").ap()
    wkT = dt("wkT", [D, D], F32, kind="ExternalInput").ap()
    wvT = dt("wvT", [D, D], F32, kind="ExternalInput").ap()
    mur = dt("mur", [128, D], F32, kind="ExternalInput").ap()
    muk = dt("muk", [128, D], F32, kind="ExternalInput").ap()
    trilm = dt("trilm", [128, 128], BF16, kind="ExternalInput").ap()

    kind_dbg = "ExternalOutput" if DEBUG else "Internal"
    xn_h = dt("xn_h", [T, D], F32, kind=kind_dbg).ap()
    qT_h = dt("qT_h", [D, T], BF16, kind=kind_dbg).ap()
    kT_h = dt("kT_h", [D, T], BF16, kind=kind_dbg).ap()
    ktok_h = dt("ktok_h", [T, D], BF16, kind=kind_dbg).ap()
    vtok_h = dt("vtok_h", [T, D], BF16, kind=kind_dbg).ap()
    yT_h = dt("yT_h", [D, T], BF16, kind=kind_dbg).ap()
    x1_h = dt("x1_h", [T, D], F32, kind=kind_dbg).ap()
    hT_hs = [dt(f"h{i}T_h", [D, T], BF16, kind=kind_dbg).ap() for i in range(4)]
    g3_h = dt("g3_h", [T, D], BF16, kind=kind_dbg).ap() if DEBUG else None

    if True:
        ar.push()
        tril_sb = ar.alloc(BF16, [128, 128])
        p.y(lambda e: e.dma_start(out=tril_sb, in_=trilm), inc="ld")

        def emit_ln_tokens(src_getter, dst_h, g_bc, b_bc, make_T, hT_dst, src_store_h=None):
            xfb = [ar.alloc(F32, [128, D]) for _ in range(2)]
            sqb = ar.alloc(F32, [128, D])
            on32 = [ar.alloc(F32, [128, D]) for _ in range(2)]
            onb = [ar.alloc(BF16, [128, D]) for _ in range(2)]
            st8 = ar.alloc(F32, [128, 8])
            nsum, sqs, negm, ex2, m2, var, lntmp, rstd = (st8[:, i:i + 1] for i in range(8))
            v_consumed = [0, 0]
            tr_done = [0, 0]
            st_n = [0, 0]
            st_s = [0, 0]
            v_evac_done = 0
            for i in range(NT):
                bi = i % 2
                fn, ws = src_getter(i, xfb[bi], [("v", v_consumed[bi]), ("st", st_s[bi])])
                lv = p.y(fn, inc="ld", waits=ws)
                p.v(lambda e, bi=bi: e.tensor_tensor(sqb, xfb[bi], xfb[bi], op=ALU.mult),
                    waits=[("ld", lv)])
                p.v(lambda e, bi=bi: e.tensor_reduce(nsum, xfb[bi], axis=AX, op=ALU.add,
                                                     negate=True))
                p.v(lambda e: e.tensor_reduce(sqs, sqb, axis=AX, op=ALU.add))
                p.v(lambda e: e.tensor_scalar_mul(negm, nsum, 1.0 / D))
                p.v(lambda e: e.tensor_scalar(ex2, sqs, 1.0 / D, 1e-5, op0=ALU.mult, op1=ALU.add))
                p.v(lambda e: e.tensor_tensor(m2, negm, negm, op=ALU.mult))
                p.spacer()
                vst = p.v(lambda e: e.tensor_tensor(var, ex2, m2, op=ALU.subtract), inc="v")
                p.s(lambda e: e.activation(lntmp, var, AF.Ln), waits=[("v", vst)])
                sst = p.s(lambda e: e.activation(rstd, lntmp, AF.Exp, scale=-0.5), inc="s")
                p.v(lambda e: e.tensor_tensor(negm, negm, rstd, op=ALU.mult), waits=[("s", sst)])
                p.spacer()
                p.v(lambda e, bi=bi: e.tensor_scalar(on32[bi], xfb[bi], rstd[:, 0:1], negm[:, 0:1],
                                                     op0=ALU.mult, op1=ALU.add),
                    waits=[("st", st_n[bi])])
                if g_bc is not None:
                    p.v(lambda e, bi=bi: e.tensor_tensor(on32[bi], on32[bi], g_bc, op=ALU.mult))
                    p.v(lambda e, bi=bi: e.tensor_tensor(on32[bi], on32[bi], b_bc, op=ALU.add))
                vdone = p.v(lambda e, bi=bi: e.tensor_copy(onb[bi], on32[bi]), inc="v",
                            waits=[("t", tr_done[bi])])
                v_consumed[bi] = vdone
                if src_store_h is not None:
                    st_s[bi] = p.y(lambda e, i=i, bi=bi: e.dma_start(
                        out=src_store_h[i * 128:(i + 1) * 128, :], in_=xfb[bi]),
                        inc="st", waits=[("v", vdone)])
                if dst_h is not None:
                    st_n[bi] = p.y(lambda e, i=i, bi=bi: e.dma_start(
                        out=dst_h[i * 128:(i + 1) * 128, :], in_=on32[bi]),
                        inc="st", waits=[("v", vdone)])
                if make_T:
                    for j in range(NC):
                        w = [("v", vdone), ("v", v_evac_done)] if j == 0 else []
                        last = p.t(lambda e, j=j, bi=bi: e.transpose(
                            psT[:, j * 128:(j + 1) * 128], onb[bi][:, j * 128:(j + 1) * 128],
                            ident_sb), inc="t" if j == NC - 1 else None, waits=w)
                    tr_done[bi] = last
                    dst = hT_dst[:, :, i * 128:(i + 1) * 128]
                    v_evac_done = p.v(lambda e, dst=dst: e.tensor_copy(
                        dst, psT[:].rearrange("p (c q) -> p c q", q=128)),
                        inc="v", waits=[("t", last)])

        ar.push()
        emit_ln_tokens(lambda i, dst, ws: ((lambda e, i=i, dst=dst: e.dma_start(
            out=dst, in_=x_in[i * 128:(i + 1) * 128, :])), ws), xn_h, None, None, False, None)
        ar.pop()
        p.barrier()

        ar.push()
        h1T_sb = ar.alloc(BF16, [128, NC, T])
        ar.push()
        l1g_sb = ar.alloc(F32, [128, D])
        l1b_sb = ar.alloc(F32, [128, D])
        p.y(lambda e: e.dma_start(out=l1g_sb, in_=ln1g), inc="ld")
        lvg = p.y(lambda e: e.dma_start(out=l1b_sb, in_=ln1b), inc="ld")
        p.pending["vector"].append(("ld", lvg))
        emit_ln_tokens(lambda i, dst, ws: ((lambda e, i=i, dst=dst: e.dma_start(
            out=dst, in_=xn_h[i * 128:(i + 1) * 128, :])), ws), None, l1g_sb, l1b_sb,
            True, h1T_sb)
        ar.pop()
        p.barrier()

        ar.push()
        wp_sb = ar.alloc(BF16, [128, NC, 3 * D])
        qe_sb = ar.alloc(BF16, [128, T])
        kv_sb = ar.alloc(BF16, [128, D])
        p.g(lambda e: e.dma_start(out=wp_sb[:, :, 0:D], in_=wq.rearrange("(c p) e -> p c e", p=128)),
            inc="lg")
        p.g(lambda e: e.dma_start(out=wp_sb[:, :, D:2 * D],
                                  in_=wk.rearrange("(c p) e -> p c e", p=128)), inc="lg")
        p.g(lambda e: e.dma_start(out=wp_sb[:, :, 2 * D:3 * D],
                                  in_=wv.rearrange("(c p) e -> p c e", p=128)), inc="lg")
        p.barrier()
        for et in range(16):
            off = (et % 8) * 128 + (et // 8) * D
            s_evac = [0, 0]
            for nsl in range(8):
                pb = nsl % 2
                psq = ps_big[:, pb * 512:(pb + 1) * 512]
                for c in range(NC):
                    w = [("s", s_evac[pb])] if c == 0 else []
                    tmm = p.t(lambda e, c=c, off=off, nsl=nsl, psq=psq: e.matmul(
                        psq, wp_sb[:, c, off:off + 128],
                        h1T_sb[:, c, nsl * 512:(nsl + 1) * 512],
                        start=(c == 0), stop=(c == NC - 1)),
                        inc="t" if c == NC - 1 else None, waits=w)
                s_evac[pb] = p.s(lambda e, nsl=nsl, psq=psq: e.activation(
                    qe_sb[:, nsl * 512:(nsl + 1) * 512], psq, AF.Identity),
                    inc="s", waits=[("t", tmm)])
            dst_h = qT_h if et < 8 else kT_h
            row = (et % 8) * 128
            stv = p.y(lambda e, dst_h=dst_h, row=row: e.dma_start(
                out=dst_h[row:row + 128, :], in_=qe_sb), inc="st",
                waits=[("s", s_evac[0]), ("s", s_evac[1])])
            p.pending["scalar"].append(("st", stv))
        st_v = 0
        v_evac = [0, 0]
        for part, dst_h in [(1, ktok_h), (2, vtok_h)]:
            for ti in range(NT):
                for vsl in range(2):
                    psv = ps_big[:, vsl * 512:(vsl + 1) * 512]
                    for c in range(NC):
                        w = [("v", v_evac[vsl]), ("st", st_v)] if c == 0 else []
                        tmm = p.t(lambda e, c=c, ti=ti, vsl=vsl, psv=psv, part=part: e.matmul(
                            psv, h1T_sb[:, c, ti * 128:(ti + 1) * 128],
                            wp_sb[:, c, part * D + vsl * 512:part * D + (vsl + 1) * 512],
                            start=(c == 0), stop=(c == NC - 1)),
                            inc="t" if c == NC - 1 else None, waits=w)
                    v_evac[vsl] = p.v(lambda e, psv=psv, vsl=vsl: e.tensor_copy(
                        kv_sb[:, vsl * 512:(vsl + 1) * 512], psv), inc="v", waits=[("t", tmm)])
                st_v = p.y(lambda e, ti=ti, dst_h=dst_h: e.dma_start(
                    out=dst_h[ti * 128:(ti + 1) * 128, :], in_=kv_sb),
                    inc="st", waits=[("v", v_evac[0]), ("v", v_evac[1])])
        ar.pop()
        ar.pop()
        p.barrier()

        ar.push()
        qh_sb = ar.alloc(BF16, [64, T])
        kh_sb = ar.alloc(BF16, [64, T])
        ktk_sb = ar.alloc(BF16, [128, NT, 64])
        vtk_sb = ar.alloc(BF16, [128, NT, 64])
        msb = ar.alloc(BF16, [64, 64])
        ssb = ar.alloc(BF16, [128, 128])
        ysb = ar.alloc(BF16, [128, 64])
        yT_sb = ar.alloc(BF16, [64, T])
        ps_y = ps_big[:, 0:512]
        ps_s = ps_big[:, 512:1024]
        ps_M = ps_big[:, 1024:1536]
        for h in range(NH):
            p.y(lambda e, h=h: e.dma_start(out=qh_sb, in_=qT_h[h * 64:(h + 1) * 64, :]), inc="ld")
            p.y(lambda e, h=h: e.dma_start(out=kh_sb, in_=kT_h[h * 64:(h + 1) * 64, :]), inc="ld")
            p.y(lambda e, h=h: e.dma_start(
                out=ktk_sb, in_=ktok_h.rearrange("(t p) c -> p t c", p=128)[:, :, h * 64:(h + 1) * 64]),
                inc="ld")
            lq = p.y(lambda e, h=h: e.dma_start(
                out=vtk_sb, in_=vtok_h.rearrange("(t p) c -> p t c", p=128)[:, :, h * 64:(h + 1) * 64]),
                inc="ld")
            t_mup = 0
            v_mcp = 0
            t_yev = 0
            v_yev = 0
            s_yev = 0
            v_ss = 0
            t_intra_prev = 0
            for ci in range(NT):
                w0 = [("ld", lq)] if ci == 0 else []
                if ci > 0:
                    v_mcp = p.v(lambda e: e.tensor_copy(msb, ps_M[0:64, 0:64]), inc="v",
                                waits=[("t", t_mup)])
                    p.t(lambda e, ci=ci: e.matmul(
                        ps_y[0:128, 0:64], qh_sb[:, ci * 128:(ci + 1) * 128], msb,
                        start=True, stop=False), inc="t",
                        waits=[("v", v_mcp), ("s", s_yev)])
                tss = p.t(lambda e, ci=ci: e.matmul(
                    ps_s[:, 0:128], kh_sb[:, ci * 128:(ci + 1) * 128],
                    qh_sb[:, ci * 128:(ci + 1) * 128], start=True, stop=True), inc="t",
                    waits=w0 + [("v", v_ss)])
                v_ss = p.v(lambda e: e.tensor_tensor(ssb, tril_sb, ps_s[:, 0:128], op=ALU.mult),
                           inc="v", waits=[("t", tss), ("t", t_intra_prev)])
                t_intra_prev = p.t(lambda e, ci=ci: e.matmul(
                    ps_y[0:128, 0:64], ssb, vtk_sb[:, ci, :],
                    start=(ci == 0), stop=True), inc="t",
                    waits=[("v", v_ss)] + ([("s", s_yev)] if ci == 0 else []))
                t_mup = p.t(lambda e, ci=ci: e.matmul(
                    ps_M[0:64, 0:64], ktk_sb[:, ci, :], vtk_sb[:, ci, :],
                    start=(ci == 0), stop=(ci == NT - 1)), inc="t",
                    waits=[("v", v_mcp)] if ci > 0 else [])
                s_yev = p.s(lambda e: e.activation(ysb, ps_y[0:128, 0:64], AF.Identity),
                            inc="s", waits=[("t", t_intra_prev), ("t", t_yev)])
                jj = ci % 8
                t_yev = p.t(lambda e, jj=jj: e.transpose(
                    psT[0:64, jj * 128:(jj + 1) * 128], ysb, ident_sb), inc="t",
                    waits=[("s", s_yev)] + ([("v", v_yev)] if jj == 0 else []))
                if jj == 7:
                    dst = yT_sb[:, (ci - 7) * 128:(ci + 1) * 128]
                    v_yev = p.v(lambda e, dst=dst: e.tensor_copy(dst, psT[0:64, :]),
                                inc="v", waits=[("t", t_yev)])
            st_yT = p.y(lambda e, h=h: e.dma_start(out=yT_h[h * 64:(h + 1) * 64, :], in_=yT_sb),
                        inc="st", waits=[("v", v_yev)])
            p.pending["sync"].append(("t", p.now()["t"]))
            p.pending["vector"].append(("st", st_yT))
        ar.pop()
        p.barrier()

        ar.push()
        h2T_sb = ar.alloc(BF16, [128, NC, T])
        ar.push()
        wo_sb = ar.alloc(BF16, [128, NC, D])
        yt_t = [ar.alloc(BF16, [128, NC, 128]) for _ in range(2)]
        l2g_sb = ar.alloc(F32, [128, D])
        l2b_sb = ar.alloc(F32, [128, D])
        xnl_sb = ar.alloc(F32, [128, D])
        p.g(lambda e: e.dma_start(out=wo_sb, in_=wo.rearrange("(c p) e -> p c e", p=128)), inc="lg")
        p.y(lambda e: e.dma_start(out=l2g_sb, in_=ln2g), inc="ld")
        lvg = p.y(lambda e: e.dma_start(out=l2b_sb, in_=ln2b), inc="ld")
        p.pending["vector"].append(("ld", lvg))
        p.barrier()
        t_cons = [0, 0]

        def src_x1(i, dst, ws):
            bi = i % 2
            p.y(lambda e, i=i, bi=bi: e.dma_start(
                out=yt_t[bi], in_=yT_h.rearrange("(c p) t -> p c t", p=128)[:, :, i * 128:(i + 1) * 128]),
                inc="ld", waits=[("t", t_cons[bi])])
            lv2 = p.y(lambda e, i=i: e.dma_start(out=xnl_sb, in_=xn_h[i * 128:(i + 1) * 128, :]),
                      inc="ld", waits=ws)
            v_ev = 0
            for esl in range(2):
                pso = ps_big[:, esl * 512:(esl + 1) * 512]
                for c in range(NC):
                    w = [("ld", lv2), ("v", v_ev)] if c == 0 else []
                    tmm = p.t(lambda e, c=c, esl=esl, bi=bi, pso=pso: e.matmul(
                        pso, yt_t[bi][:, c, :], wo_sb[:, c, esl * 512:(esl + 1) * 512],
                        start=(c == 0), stop=(c == NC - 1)),
                        inc="t" if c == NC - 1 else None, waits=w)
                t_cons[bi] = tmm
                sl = slice(esl * 512, (esl + 1) * 512)
                v_ev = p.v(lambda e, pso=pso, sl=sl, dst=dst: e.tensor_tensor(
                    dst[:, sl], pso, xnl_sb[:, sl], op=ALU.add), inc="v", waits=[("t", tmm)])
            return (lambda e: e.nop()), [("v", v_ev)]

        emit_ln_tokens(src_x1, None, l2g_sb, l2b_sb, True, h2T_sb, src_store_h=x1_h)
        ar.pop()
        p.barrier()
        p.y(lambda e: e.dma_start(out=hT_hs[0].rearrange("(c p) t -> p c t", p=128), in_=h2T_sb),
            inc="st")
        ar.pop()
        p.barrier()

        ar.push()
        w1_sb = ar.alloc(BF16, [128, NC, HID])
        w2_sb = ar.alloc(BF16, [128, NHID, D])
        b1_sb = ar.alloc(F32, [128, NHID])
        b2_sb = ar.alloc(F32, [128, NC])
        TB = 256
        NTB = T // TB
        hblk = ar.alloc(BF16, [128, NC, TB])
        asb = [ar.alloc(BF16, [128, TB]) for _ in range(2)]
        outev = ar.alloc(BF16, [128, NC, TB])
        st_o = 0
        v_oev = 0
        s_aev = [0, 0]
        t_out2 = 0
        for lvl in range(3):
            src_hT = hT_hs[lvl]
            dst_hT = hT_hs[lvl + 1]
            p.g(lambda e, lvl=lvl: e.dma_start(
                out=w1_sb, in_=cw1[lvl].rearrange("(c p) e -> p c e", p=128)), inc="lg",
                waits=[("t", p.now()["t"])])
            p.g(lambda e, lvl=lvl: e.dma_start(
                out=w2_sb, in_=cw2[lvl].rearrange("(c p) e -> p c e", p=128)), inc="lg")
            p.y(lambda e, lvl=lvl: e.dma_start(out=b1_sb, in_=cb1[lvl]), inc="ld")
            p.y(lambda e, lvl=lvl: e.dma_start(out=b2_sb, in_=cb2[lvl]), inc="ld")
            p.barrier()
            for tb in range(NTB):
                tsl = slice(tb * TB, (tb + 1) * TB)
                lv = p.y(lambda e, tsl=tsl, src_hT=src_hT: e.dma_start(
                    out=hblk, in_=src_hT.rearrange("(c p) t -> p c t", p=128)[:, :, tsl]),
                    inc="ld", waits=[("t", p.now()["t"])])
                for ht in range(NHID):
                    hb = ht % 2
                    pa = ps_big[:, 2048 + hb * 512: 2048 + hb * 512 + TB]
                    for c in range(NC):
                        w = []
                        if c == 0:
                            w = [("ld", lv), ("s", s_aev[hb])]
                        tmm = p.t(lambda e, c=c, ht=ht, pa=pa: e.matmul(
                            pa, w1_sb[:, c, ht * 128:(ht + 1) * 128], hblk[:, c, :],
                            start=(c == 0), stop=(c == NC - 1)),
                            inc="t" if c == NC - 1 else None, waits=w)
                    w = [("t", tmm)]
                    if t_out2:
                        w.append(("t", t_out2))
                    s_aev[hb] = p.s(lambda e, ht=ht, pa=pa, hb=hb: e.activation(
                        asb[hb], pa, AF.Gelu_apprx_tanh, bias=b1_sb[:, ht:ht + 1]),
                        inc="s", waits=w)
                    for dt in range(NC):
                        po = ps_big[:, dt * 256:(dt + 1) * 256]
                        w = []
                        if dt == 0:
                            w = [("s", s_aev[hb])]
                            if ht == 0:
                                w.append(("v", v_oev))
                        t_out2 = p.t(lambda e, ht=ht, dt=dt, po=po, hb=hb: e.matmul(
                            po, w2_sb[:, ht, dt * 128:(dt + 1) * 128], asb[hb],
                            start=(ht == 0 and dt % 2 == 0), stop=(ht == NHID - 1)),
                            inc="t" if dt == NC - 1 else None, waits=w)
                for dt in range(NC):
                    po = ps_big[:, dt * 256:(dt + 1) * 256]
                    v_oev = p.v(lambda e, dt=dt, po=po: e.tensor_scalar(
                        outev[:, dt, :], po, b2_sb[:, dt:dt + 1], None, op0=ALU.add),
                        inc="v" if dt == NC - 1 else None,
                        waits=[("t", t_out2), ("st", st_o)] if dt == 0 else [])
                st_o = p.y(lambda e, tsl=tsl, dst_hT=dst_hT: e.dma_start(
                    out=dst_hT.rearrange("(c p) t -> p c t", p=128)[:, :, tsl],
                    in_=outev), inc="st", waits=[("v", v_oev)])
            p.barrier()
        ar.pop()
        p.barrier()

        ar.push()
        wr_sb = ar.alloc(BF16, [128, NC, D])
        wk2_sb = ar.alloc(BF16, [128, NC, D])
        wv2_sb = ar.alloc(BF16, [128, NC, D])
        mur_sb = ar.alloc(F32, [128, D])
        muk_sb = ar.alloc(F32, [128, D])
        omur_sb = ar.alloc(F32, [128, D])
        omuk_sb = ar.alloc(F32, [128, D])
        xn_t = ar.alloc(F32, [128, D])
        xp_t = ar.alloc(F32, [128, D])
        xr_t = ar.alloc(F32, [128, D])
        xk_t = ar.alloc(F32, [128, D])
        tmp_t = ar.alloc(F32, [128, D])
        mixb = ar.alloc(BF16, [128, D])
        mixT = ar.alloc(BF16, [128, D])
        r_f = xr_t
        krel_f = xk_t
        kact = ar.alloc(BF16, [128, D])
        kaT = ar.alloc(BF16, [128, D])
        rk_f = tmp_t
        g_sb = ar.alloc(BF16, [128, D])
        h3t = [ar.alloc(BF16, [128, NC, 128]) for _ in range(2)]
        h3tok = ar.alloc(F32, [128, D])
        s_f = ar.alloc(F32, [128, D])
        s_bf = ar.alloc(BF16, [128, D])
        x_tl = ar.alloc(F32, [128, D])
        sg_f = xn_t
        xo_f = xp_t
        p.g(lambda e: e.dma_start(out=wr_sb, in_=wrT.rearrange("(c p) e -> p c e", p=128)), inc="lg")
        p.g(lambda e: e.dma_start(out=wk2_sb, in_=wkT.rearrange("(c p) e -> p c e", p=128)), inc="lg")
        p.g(lambda e: e.dma_start(out=wv2_sb, in_=wvT.rearrange("(c p) e -> p c e", p=128)), inc="lg")
        p.y(lambda e: e.dma_start(out=mur_sb, in_=mur), inc="ld")
        lvm = p.y(lambda e: e.dma_start(out=muk_sb, in_=muk), inc="ld")
        p.barrier()
        p.v(lambda e: e.tensor_scalar(omur_sb, mur_sb, -1.0, 1.0, op0=ALU.mult, op1=ALU.add),
            waits=[("ld", lvm)])
        p.v(lambda e: e.tensor_scalar(omuk_sb, muk_sb, -1.0, 1.0, op0=ALU.mult, op1=ALU.add))
        p.barrier()
        for ti in range(NT):
            p.barrier()
            p.y(lambda e, ti=ti: e.dma_start(out=xn_t, in_=xn_h[ti * 128:(ti + 1) * 128, :]),
                inc="ld")
            if ti == 0:
                p.v(lambda e: e.memset(xp_t[0:1, :], 0.0))
                p.y(lambda e: e.dma_start(out=xp_t[1:128, :], in_=xn_h[0:127, :]), inc="ld")
            else:
                p.y(lambda e, ti=ti: e.dma_start(
                    out=xp_t, in_=xn_h[ti * 128 - 1:(ti + 1) * 128 - 1, :]), inc="ld")
            p.y(lambda e, ti=ti: e.dma_start(
                out=h3t[ti % 2], in_=hT_hs[3].rearrange("(c p) t -> p c t", p=128)[:, :, ti * 128:(ti + 1) * 128]),
                inc="ld")
            p.y(lambda e, ti=ti: e.dma_start(out=x_tl, in_=x_in[ti * 128:(ti + 1) * 128, :]),
                inc="ld")
            lv = p.y(lambda e, ti=ti: e.dma_start(out=s_f, in_=x1_h[ti * 128:(ti + 1) * 128, :]),
                     inc="ld")
            p.pending["vector"].append(("ld", lv))
            p.pending["tensor"].append(("ld", lv))
            p.v(lambda e: e.tensor_tensor(xr_t, mur_sb, xn_t, op=ALU.mult))
            p.v(lambda e: e.tensor_tensor(tmp_t, omur_sb, xp_t, op=ALU.mult))
            p.v(lambda e: e.tensor_tensor(xr_t, xr_t, tmp_t, op=ALU.add))
            p.v(lambda e: e.tensor_tensor(xk_t, muk_sb, xn_t, op=ALU.mult))
            p.v(lambda e: e.tensor_tensor(tmp_t, omuk_sb, xp_t, op=ALU.mult))
            p.v(lambda e: e.tensor_tensor(xk_t, xk_t, tmp_t, op=ALU.add))
            s_used = 0
            for src, wsb, dst, act in [(xr_t, wr_sb, r_f, AF.Sigmoid),
                                       (xk_t, wk2_sb, krel_f, AF.Relu)]:
                vq = p.v(lambda e, src=src: e.tensor_copy(mixb, src), inc="v")
                for j in range(NC):
                    w = [("v", vq)] if j == 0 else []
                    ttr = p.t(lambda e, j=j: e.transpose(
                        psT[:, j * 128:(j + 1) * 128], mixb[:, j * 128:(j + 1) * 128], ident_sb),
                        inc="t" if j == NC - 1 else None, waits=w)
                vx = p.v(lambda e: e.tensor_copy(mixT, psT[:]), inc="v", waits=[("t", ttr)])
                for esl in range(2):
                    pso = ps_big[:, 1024 + esl * 512:1024 + (esl + 1) * 512]
                    for c in range(NC):
                        w = [("v", vx), ("s", s_used)] if c == 0 else []
                        tmm = p.t(lambda e, c=c, esl=esl, wsb=wsb, pso=pso: e.matmul(
                            pso, mixT[:, c * 128:(c + 1) * 128],
                            wsb[:, c, esl * 512:(esl + 1) * 512],
                            start=(c == 0), stop=(c == NC - 1)),
                            inc="t" if c == NC - 1 else None, waits=w)
                    s_used = p.s(lambda e, esl=esl, dst=dst, act=act, pso=pso: e.activation(
                        dst[:, esl * 512:(esl + 1) * 512], pso, act),
                        inc="s", waits=[("t", tmm)])
            vq = p.v(lambda e: e.tensor_tensor(kact, krel_f, krel_f, op=ALU.mult), inc="v",
                     waits=[("s", s_used)])
            for j in range(NC):
                w = [("v", vq)] if j == 0 else []
                ttr = p.t(lambda e, j=j: e.transpose(
                    psT[:, j * 128:(j + 1) * 128], kact[:, j * 128:(j + 1) * 128], ident_sb),
                    inc="t" if j == NC - 1 else None, waits=w)
            vx = p.v(lambda e: e.tensor_copy(kaT, psT[:]), inc="v", waits=[("t", ttr)])
            v_used = 0
            for esl in range(2):
                pso = ps_big[:, 1024 + esl * 512:1024 + (esl + 1) * 512]
                for c in range(NC):
                    w = [("v", vx), ("s", s_used)] if c == 0 else []
                    tmm = p.t(lambda e, c=c, esl=esl, pso=pso: e.matmul(
                        pso, kaT[:, c * 128:(c + 1) * 128], wv2_sb[:, c, esl * 512:(esl + 1) * 512],
                        start=(c == 0), stop=(c == NC - 1)),
                        inc="t" if c == NC - 1 else None, waits=w)
                v_used = p.v(lambda e, esl=esl, pso=pso: e.tensor_tensor(
                    rk_f[:, esl * 512:(esl + 1) * 512], pso, r_f[:, esl * 512:(esl + 1) * 512],
                    op=ALU.mult), inc="v", waits=[("t", tmm)])
            s_g = p.s(lambda e: e.activation(g_sb, rk_f, AF.Silu), inc="s", waits=[("v", v_used)])
            for j in range(NC):
                ttr = p.t(lambda e, j=j, ti=ti: e.transpose(
                    psT[:, j * 128:(j + 1) * 128], h3t[ti % 2][:, j, :], ident_sb),
                    inc="t" if j == NC - 1 else None)
            p.v(lambda e: e.tensor_copy(h3tok, psT[:]), waits=[("t", ttr)])
            p.v(lambda e: e.tensor_tensor(s_f, s_f, h3tok, op=ALU.add))
            p.v(lambda e: e.tensor_copy(s_bf, s_f))
            p.v(lambda e: e.tensor_tensor(sg_f, s_bf, g_sb, op=ALU.mult), waits=[("s", s_g)])
            v_done = p.v(lambda e: e.tensor_tensor(xo_f, x_tl, sg_f, op=ALU.add), inc="v")
            if DEBUG:
                p.y(lambda e, ti=ti: e.dma_start(out=g3_h[ti * 128:(ti + 1) * 128, :], in_=g_sb),
                    inc="st", waits=[("s", s_g)])
            p.y(lambda e, ti=ti: e.dma_start(out=x_out[ti * 128:(ti + 1) * 128, :], in_=xo_f),
                inc="st", waits=[("v", v_done)])
        ar.pop()
        ar.pop()
        LAST_PEAK.update({str(k): v for k, v in ar.peak.items()})


def _setup_ctx(nc, p, bf_cols=72200, f32_cols=13000):
    stk = ExitStack()
    BFA = stk.enter_context(nc.sbuf_tensor("bf_arena", [128, bf_cols], BF16))
    F32A = stk.enter_context(nc.sbuf_tensor("f32_arena", [128, f32_cols], F32))
    ar = Arena({BF16: BFA, F32: F32A})
    ps_big = stk.enter_context(nc.psum_tensor("ps_big", [128, 3072], F32))
    psT = stk.enter_context(nc.psum_tensor("psT", [128, 1024], BF16))
    ident_in = nc.dram_tensor("ident", [128, 128], BF16, kind="ExternalInput").ap()
    ones64_in = nc.dram_tensor("ones64", [1, 64], BF16, kind="ExternalInput").ap()
    ident_sb = ar.alloc(BF16, [128, 128])
    ones64_sb = ar.alloc(BF16, [1, 64])
    spc = ar.alloc(F32, [128, 128])
    p.v(lambda e: e.memset(spc, 0.0))
    p.spacer = lambda: p.v(lambda e: e.tensor_copy(spc[:, 0:64], spc[:, 64:128]))
    p.y(lambda e: e.dma_start(out=ident_sb, in_=ident_in), inc="ld")
    p.y(lambda e: e.dma_start(out=ones64_sb, in_=ones64_in), inc="ld")
    return stk, ar, ps_big, psT, ident_sb, ones64_sb


def build_megakernel():
    nc = bass.Bass("TRN2", target_bir_lowering=False)
    p = Prog()
    x0 = nc.dram_tensor("x", [T, D], F32, kind="ExternalInput").ap()
    xo = nc.dram_tensor("x_out", [T, D], F32, kind="ExternalOutput").ap()
    xa = nc.dram_tensor("xmid_a", [T, D], F32, kind="Internal").ap()
    xb = nc.dram_tensor("xmid_b", [T, D], F32, kind="Internal").ap()
    stk, ar, ps_big, psT, ident_sb, ones64_sb = _setup_ctx(nc, p)
    with stk:
        chain = [(0, x0, xa), (1, xa, xb), (2, xb, xa), (3, xa, xo)]
        for layer, xi, xout in chain:
            if layer < 3:
                emit_attn_layer(nc, p, ar, ps_big, psT, ident_sb, ones64_sb,
                                layer, f"l{layer}_", xi, xout)
            else:
                emit_l3(nc, p, ar, ps_big, psT, ident_sb, f"l{layer}_", xi, xout)
            p.barrier()
        p.y(lambda e: e.nop(), waits=[("st", p.now()["st"])])
        p.build(nc)
    return nc


def build_attn_layer(layer):
    nc = bass.Bass("TRN2", target_bir_lowering=False)
    p = Prog()
    x0 = nc.dram_tensor("x", [T, D], F32, kind="ExternalInput").ap()
    xo = nc.dram_tensor("x_out", [T, D], F32, kind="ExternalOutput").ap()
    stk, ar, ps_big, psT, ident_sb, ones64_sb = _setup_ctx(nc, p)
    with stk:
        emit_attn_layer(nc, p, ar, ps_big, psT, ident_sb, ones64_sb, layer, "", x0, xo)
        p.y(lambda e: e.nop(), waits=[("st", p.now()["st"])])
        p.build(nc)
    return nc


def build_l3():
    nc = bass.Bass("TRN2", target_bir_lowering=False)
    p = Prog()
    x0 = nc.dram_tensor("x", [T, D], F32, kind="ExternalInput").ap()
    xo = nc.dram_tensor("x_out", [T, D], F32, kind="ExternalOutput").ap()
    stk, ar, ps_big, psT, ident_sb, ones64_sb = _setup_ctx(nc, p)
    with stk:
        emit_l3(nc, p, ar, ps_big, psT, ident_sb, "", x0, xo)
        p.y(lambda e: e.nop(), waits=[("st", p.now()["st"])])
        p.build(nc)
    return nc


_CONST_CACHE = {}


def host_consts():
    if _CONST_CACHE:
        return _CONST_CACHE
    c = {}
    c["ident"] = np.eye(128, dtype=ml_dtypes.bfloat16)
    c["ones64"] = np.ones((1, 64), dtype=ml_dtypes.bfloat16)
    freqs = (np.float32(10000.0) ** (np.arange(0, HD, 2.0, dtype=np.float32) /
                                     np.float32(HD))).astype(np.float32)
    t = np.arange(T, dtype=np.float32)
    ang = (t[:, None] * freqs[None, :]).astype(np.float32)
    cos = np.cos(ang.astype(np.float64)).astype(np.float32)
    sin = np.sin(ang.astype(np.float64)).astype(np.float32)
    A = np.empty((128, T), np.float32)
    B = np.empty((128, T), np.float32)
    for pm in range(128):
        j = pm % 32
        A[pm] = cos[:, j]
        B[pm] = sin[:, j] if (pm % 64) >= 32 else -sin[:, j]
    c["rotA"] = A.astype(ml_dtypes.bfloat16)
    c["rotB"] = B.astype(ml_dtypes.bfloat16)
    P = np.zeros((128, 128), np.float32)
    for pm in range(128):
        base = (pm // 64) * 64
        off = pm % 64
        src = base + ((off + 32) % 64)
        P[src, pm] = 1.0
    c["perm"] = P.astype(ml_dtypes.bfloat16)
    cm = np.zeros((4, 128, 512), np.float32)
    for r in range(4):
        for kk in range(128):
            cm[r, kk, kk + r * 128:] = 1.0
    c["cmask"] = cm.astype(ml_dtypes.bfloat16)
    tr = np.zeros((128, 128), np.float32)
    for e in range(128):
        tr[e, e:] = 1.0
    c["trilm"] = tr.astype(ml_dtypes.bfloat16)
    _CONST_CACHE.update(c)
    return c


def rotary_perm_idx():
    idx = np.arange(3 * D)
    for part in range(2):
        for h in range(NH):
            base = part * D + h * HD
            ev = base + np.arange(0, HD, 2)
            od = base + np.arange(1, HD, 2)
            idx[base:base + 32] = ev
            idx[base + 32:base + 64] = od
    return idx


def layer_inputs(layer, ins):
    c = host_consts()
    d = {}
    if layer == 0:
        idx = rotary_perm_idx()
        w = np.ascontiguousarray(ins["L0_attn_qkv_w"][:, idx], dtype=np.float32)
        b = np.ascontiguousarray(np.asarray(ins["L0_attn_qkv_b"], dtype=np.float32)[idx])
        d["w_o"] = np.ascontiguousarray(ins["L0_attn_out_w"], dtype=np.float32)
        bo = np.asarray(ins["L0_attn_out_b"], dtype=np.float32)
        d["perm"] = c["perm"]
        d["rotA"] = c["rotA"]
        d["rotB"] = c["rotB"]
        pref = "L0"
    elif layer == 1:
        w = np.ascontiguousarray(np.asarray(ins["L1_attn_w_qkv"], dtype=np.float32).T)
        b = np.asarray(ins["L1_attn_b_qkv"], dtype=np.float32)
        d["w_o"] = np.ascontiguousarray(np.asarray(ins["L1_attn_w_o"], dtype=np.float32).T)
        bo = np.asarray(ins["L1_attn_b_o"], dtype=np.float32)
        pref = "L1"
    else:
        w = np.ascontiguousarray(np.asarray(ins["L2_attn_w_in"], dtype=np.float32).T)
        b = np.asarray(ins["L2_attn_b_in"], dtype=np.float32)
        d["w_o"] = np.ascontiguousarray(np.asarray(ins["L2_attn_w_out"], dtype=np.float32).T)
        bo = np.asarray(ins["L2_attn_b_out"], dtype=np.float32)
        d["cmask"] = c["cmask"]
        pref = "L2"
    d["w_qkv"] = w
    d["b_pc"] = np.ascontiguousarray(b.reshape(24, 128).T)
    d["bv_bc"] = np.ascontiguousarray(np.broadcast_to(b[2 * D:], (128, D)))
    d["bo_bc"] = np.ascontiguousarray(np.broadcast_to(bo, (128, D)))
    d["w_gate"] = np.ascontiguousarray(ins[f"{pref}_ffn_w_gate"], dtype=np.float32)
    d["w_up"] = np.ascontiguousarray(ins[f"{pref}_ffn_w_up"], dtype=np.float32)
    d["w_down"] = np.ascontiguousarray(ins[f"{pref}_ffn_w_down"], dtype=np.float32)
    return d


def l3_inputs(ins):
    c = host_consts()
    f = np.float32
    d = {"trilm": c["trilm"]}
    for k, src in [("wq", "L3_attn_wq"), ("wk", "L3_attn_wk"), ("wv", "L3_attn_wv"),
                   ("wo", "L3_attn_wo")]:
        d[k] = np.ascontiguousarray(np.asarray(ins[src], dtype=f))
    for k, src in [("ln1g", "L3_attn_ln1_g"), ("ln1b", "L3_attn_ln1_b"),
                   ("ln2g", "L3_attn_ln2_g"), ("ln2b", "L3_attn_ln2_b")]:
        d[k] = np.ascontiguousarray(np.broadcast_to(np.asarray(ins[src], dtype=f), (128, D)))
    d["cw1"] = np.ascontiguousarray(np.asarray(ins["L3_attn_cms_w1"], dtype=f))
    d["cw2"] = np.ascontiguousarray(np.asarray(ins["L3_attn_cms_w2"], dtype=f))
    b1 = np.asarray(ins["L3_attn_cms_b1"], dtype=f)
    b2 = np.asarray(ins["L3_attn_cms_b2"], dtype=f)
    d["cb1"] = np.ascontiguousarray(np.stack([b1[l].reshape(32, 128).T for l in range(3)]))
    d["cb2"] = np.ascontiguousarray(np.stack([b2[l].reshape(8, 128).T for l in range(3)]))
    for k, src in [("wrT", "L3_ffn_W_r"), ("wkT", "L3_ffn_W_k"), ("wvT", "L3_ffn_W_v")]:
        d[k] = np.ascontiguousarray(np.asarray(ins[src], dtype=f).T)
    d["mur"] = np.ascontiguousarray(np.broadcast_to(np.asarray(ins["L3_ffn_mu_r"], dtype=f), (128, D)))
    d["muk"] = np.ascontiguousarray(np.broadcast_to(np.asarray(ins["L3_ffn_mu_k"], dtype=f), (128, D)))
    return d


def kernel(**ins):
    x = np.ascontiguousarray(np.asarray(ins["L0_attn_x"], dtype=np.float32))
    B = x.shape[0]
    c = host_consts()
    nc = build_megakernel()
    base = {"ident": c["ident"], "ones64": c["ones64"]}
    for layer in range(4):
        lid = layer_inputs(layer, ins) if layer < 3 else l3_inputs(ins)
        for k, v in lid.items():
            base[f"l{layer}_{k}"] = v
    in_maps = [dict(base, x=np.ascontiguousarray(x[b])) for b in range(B)]
    res = run_bass_kernel_spmd(nc, in_maps, core_ids=list(range(B)))
    return np.stack([np.asarray(res.results[b]["x_out"]) for b in range(B)]).astype(np.float32)


# version 04
# speedup vs baseline: 1.2864x; 1.0892x over previous
import numpy as np
import ml_dtypes
import concourse.bass as bass
import concourse.mybir as mybir
from concourse.bass_utils import run_bass_kernel_spmd
from contextlib import ExitStack

F32 = mybir.dt.float32
BF16 = mybir.dt.bfloat16
AF = mybir.ActivationFunctionType
ALU = mybir.AluOpType
AX = mybir.AxisListType.X

T, D, I, NH, HD = 4096, 1024, 2816, 16, 64
NT = T // 128
NC = D // 128
NI = I // 128
MAGIC = 12582912.0
DEBUG = False
STOP_STAGE = 99
DRAIN_S1 = 3
S3_HEADS = NH if False else 16
LAST_PEAK = {}

ENGS = ("tensor", "vector", "scalar", "gpsimd", "sync")
DMA_KEYS = ("ld", "lg", "st")


class Prog:

    def __init__(self):
        self.streams = {e: [] for e in ENGS}
        self.cnt = {k: 0 for k in ("t", "v", "s", "g", "w", "ld", "lg", "st")}
        self.pending = {e: [] for e in ENGS}

    def _emit(self, eng, fn, inc=None, waits=()):
        ws = {}
        for k, val in list(self.pending[eng]) + list(waits):
            if val and val > 0:
                ws[k] = max(ws.get(k, 0), val)
        self.pending[eng] = []
        self.streams[eng].append((fn, inc, sorted(ws.items())))
        if inc is not None:
            self.cnt[inc] += 16 if inc in DMA_KEYS else 1
            return self.cnt[inc]
        return None

    def t(self, fn, inc=None, waits=()):
        return self._emit("tensor", fn, inc, waits)

    def v(self, fn, inc=None, waits=()):
        return self._emit("vector", fn, inc, waits)

    def s(self, fn, inc=None, waits=()):
        return self._emit("scalar", fn, inc, waits)

    def g(self, fn, inc=None, waits=()):
        return self._emit("gpsimd", fn, inc, waits)

    def y(self, fn, inc=None, waits=()):
        return self._emit("sync", fn, inc, waits)

    def now(self):
        return dict(self.cnt)

    def barrier(self, engines=ENGS):
        snap = [(k, v) for k, v in self.cnt.items() if v > 0]
        for e in engines:
            self.pending[e].extend(snap)

    def build(self, nc):
        with ExitStack() as stk:
            sems = {k: stk.enter_context(nc.semaphore(f"sem_{k}")) for k in self.cnt}
            block = stk.enter_context(nc.Block())
            for eng_name in ENGS:
                stream = self.streams[eng_name]
                if not stream:
                    continue

                def make(stream_):
                    def run(eng):
                        for fn, inc, ws in stream_:
                            for k, val in ws:
                                eng.wait_ge(sems[k], val)
                            inst = fn(eng)
                            if inc is not None:
                                inst.then_inc(sems[inc], 16 if inc in DMA_KEYS else 1)
                    return run

                getattr(block, eng_name)(make(stream))


class Arena:
    def __init__(self, tensors):
        self.tensors = tensors
        self.off = {d: 0 for d in tensors}
        self.stack = []
        self.peak = {d: 0 for d in tensors}

    def push(self):
        self.stack.append(dict(self.off))

    def pop(self):
        self.off = self.stack.pop()

    def alloc(self, dtype, shape):
        cols = 1
        for s in shape[1:]:
            cols *= s
        esz = 2 if dtype == BF16 else 4
        o = self.off[dtype]
        o += (-(o * esz)) % 32 // esz
        self.off[dtype] = o + cols
        self.peak[dtype] = max(self.peak[dtype], self.off[dtype])
        tsr = self.tensors[dtype]
        assert self.off[dtype] <= tsr.shape[1], (
            f"arena overflow {dtype}: {self.off[dtype]} > {tsr.shape[1]}")
        ap = tsr[:][:, o:o + cols]
        if len(shape) > 2:
            names = " ".join(f"d{i}" for i in range(len(shape) - 1))
            ap = ap.rearrange(f"p ({names}) -> p {names}",
                              **{f"d{i}": shape[i + 1] for i in range(len(shape) - 1)})
        if shape[0] != 128:
            ap = ap[0:shape[0]]
        return ap


def emit_quant_stats(p, x_ap, BM, SC, K, inc_key="v", waits_first=()):
    NB = K // 16
    BM, SC = BM[:, :NB], SC[:, :NB]
    xv = x_ap.rearrange("p (b s) -> p b s", s=16)
    p.v(lambda e: e.tensor_reduce(BM, xv, axis=AX, op=ALU.max, apply_absolute_value=True),
        waits=waits_first)
    p.v(lambda e: e.tensor_scalar(SC, BM, 1.0 / 6.0, 1e-12, op0=ALU.mult, op1=ALU.max))
    p.spacer()
    return p.v(lambda e: e.reciprocal(BM, SC), inc=inc_key)


def emit_quant_bulk(p, eng_key, x_ap, out_ap, A, B, BM, SC, K, stat_wait,
                    waits_first=(), inc_last=None):
    em = getattr(p, eng_key)
    NB = K // 16
    A, B = A[:, :K], B[:, :K]
    BM, SC = BM[:, :NB], SC[:, :NB]
    xv = x_ap.rearrange("p (b s) -> p b s", s=16)
    bmb = BM.broadcast_to([128, NB, 16])
    scb = SC.broadcast_to([128, NB, 16])
    em(lambda e: e.tensor_tensor(xv, xv, bmb, op=ALU.mult),
       waits=list(waits_first) + [stat_wait])
    em(lambda e: e.tensor_scalar(A, x_ap, 2.0, -2.0, op0=ALU.min, op1=ALU.max))
    em(lambda e: e.tensor_scalar(A, A, 2.0, MAGIC, op0=ALU.mult, op1=ALU.add))
    em(lambda e: e.tensor_scalar(A, A, MAGIC, 0.5, op0=ALU.subtract, op1=ALU.mult))
    em(lambda e: e.tensor_scalar(B, x_ap, 4.0, -4.0, op0=ALU.min, op1=ALU.max))
    em(lambda e: e.tensor_scalar(B, B, MAGIC, MAGIC, op0=ALU.add, op1=ALU.subtract))
    em(lambda e: e.tensor_tensor(A, A, B, op=ALU.add))
    em(lambda e: e.tensor_scalar(B, B, 2.0, -2.0, op0=ALU.min, op1=ALU.max))
    em(lambda e: e.tensor_tensor(A, A, B, op=ALU.subtract))
    em(lambda e: e.tensor_scalar(B, x_ap, 0.5, MAGIC, op0=ALU.mult, op1=ALU.add))
    em(lambda e: e.tensor_scalar(B, B, MAGIC, 2.0, op0=ALU.subtract, op1=ALU.mult))
    em(lambda e: e.tensor_tensor(A, A, B, op=ALU.add))
    em(lambda e: e.tensor_scalar(B, B, 4.0, -4.0, op0=ALU.min, op1=ALU.max))
    em(lambda e: e.tensor_tensor(A, A, B, op=ALU.subtract))
    av = A.rearrange("p (b s) -> p b s", s=16)
    ov = out_ap.rearrange("p (b s) -> p b s", s=16)
    return em(lambda e: e.tensor_tensor(ov, av, scb, op=ALU.mult), inc=inc_last)


def emit_quant(p, eng_key, x_ap, out_ap, tmps, K, waits_first=(), inc_last=None):
    emit_quant_stats(p, x_ap, tmps["BM"], tmps["SC"], K, inc_key="v",
                     waits_first=waits_first)
    if eng_key == "v":
        p.spacer()
        sw = ("v", 0)
    else:
        sw = ("v", p.now()["v"])
    return emit_quant_bulk(p, eng_key, x_ap, out_ap, tmps["A"], tmps["B"],
                           tmps["BM"], tmps["SC"], K, sw,
                           waits_first=(), inc_last=inc_last)


NRING = 4


def plan_wquant(nc, p, pfx, w_list, bufs):
    stats_q = []
    dsts = []
    w_base = p.now()["w"]
    assert bufs.get("chunk0", 0) == 0
    chunk = 0
    g_done = bufs.setdefault("g_done", [0, 0])
    g_ring = bufs.setdefault("g_ring", [0] * NRING)
    st_wq = bufs.get("st_wq", 0)
    for w_src, n_rows, kk, dname in w_list:
        dst_h = nc.dram_tensor(pfx + dname, [n_rows, kk], BF16,
                               kind="ExternalOutput" if DEBUG else "Internal").ap()
        dsts.append(dst_h)
        for r in range(n_rows // 128):
            for c0 in range(0, kk, 1024):
                w = min(1024, kk - c0)
                bi = chunk % 2
                sl = chunk % NRING
                lv = p.g(lambda e, w_src=w_src, r=r, c0=c0, w=w, bi=bi: e.dma_start(
                    out=bufs["wld"][bi][:, 0:w],
                    in_=w_src[r * 128:(r + 1) * 128, c0:c0 + w]),
                    inc="lg", waits=[("g", g_done[bi])])
                stats_q.append((bufs["wld"][bi][:, 0:w], w, lv, sl, g_ring[sl]))
                g_done[bi] = emit_quant_bulk(
                    p, "g", bufs["wld"][bi][:, 0:w], bufs["wqt"][:, 0:w],
                    bufs["A"], bufs["B"], bufs["BMr"][sl], bufs["SCr"][sl],
                    w, ("w", w_base + chunk + 1),
                    waits_first=[("lg", lv), ("lg", st_wq)], inc_last="g")
                g_ring[sl] = g_done[bi]
                st_wq = p.g(lambda e, dst_h=dst_h, r=r, c0=c0, w=w: e.dma_start(
                    out=dst_h[r * 128:(r + 1) * 128, c0:c0 + w], in_=bufs["wqt"][:, 0:w]),
                    inc="lg", waits=[("g", g_done[bi])])
                chunk += 1
    bufs["chunk0"] = chunk
    bufs["st_wq"] = st_wq
    return stats_q, dsts


def drain_wstats(p, bufs, stats_q, n):
    for _ in range(min(n, len(stats_q))):
        x_ap, w, lv, sl, g_reuse = stats_q.pop(0)
        emit_quant_stats(p, x_ap, bufs["BMr"][sl], bufs["SCr"][sl], w, inc_key="w",
                         waits_first=[("lg", lv), ("g", g_reuse)])


def emit_attn_layer(nc, p, ar, ps_big, psT, ident_sb, ones64_sb, layer, pfx, x_in, x_out):
    scale = {0: 0.125, 1: 0.03125, 2: 0.125}[layer]
    causal = layer == 2
    rotary = layer == 0
    add_xn = layer == 1
    dt = lambda name, *a, **k: nc.dram_tensor(pfx + name, *a, **k)

    w_qkv = dt("w_qkv", [D, 3 * D], F32, kind="ExternalInput").ap()
    b_pc = dt("b_pc", [128, 24], F32, kind="ExternalInput").ap()
    bv_bc = dt("bv_bc", [128, D], F32, kind="ExternalInput").ap()
    w_o = dt("w_o", [D, D], F32, kind="ExternalInput").ap()
    bo_bc = dt("bo_bc", [128, D], F32, kind="ExternalInput").ap()
    w_gate = dt("w_gate", [I, D], F32, kind="ExternalInput").ap()
    w_up = dt("w_up", [I, D], F32, kind="ExternalInput").ap()
    w_down = dt("w_down", [D, I], F32, kind="ExternalInput").ap()
    if rotary:
        rotA = dt("rotA64", [64, T], BF16, kind="ExternalInput").ap()
        rotB = dt("rotB64", [64, T], BF16, kind="ExternalInput").ap()
    if causal:
        cmask = dt("cmask", [4, 128, 512], BF16, kind="ExternalInput").ap()

    kind_dbg = "ExternalOutput" if DEBUG else "Internal"
    xn_h = dt("xn_h", [T, D], F32, kind=kind_dbg).ap()
    qT_h = dt("qT_h", [D, T], BF16, kind=kind_dbg).ap()
    kT_h = dt("kT_h", [D, T], BF16, kind=kind_dbg).ap()
    vP_h = dt("vP_h", [T, NH * 65], BF16, kind=kind_dbg).ap()
    hT_h = dt("hT_h", [D, T], BF16, kind=kind_dbg).ap()
    s_h = dt("s_h", [T, D], BF16, kind=kind_dbg).ap()
    hqT_h = dt("hqT_h", [I, T], BF16, kind=kind_dbg).ap()
    g_h = dt("g_h", [T, D], BF16, kind=kind_dbg).ap() if DEBUG else None

    ar.push()
    wbufs = {
        "wld": [ar.alloc(F32, [128, 1024]) for _ in range(2)],
        "wqt": ar.alloc(BF16, [128, 1024]),
        "A": ar.alloc(F32, [128, 1024]),
        "B": ar.alloc(F32, [128, 1024]),
        "BMr": [ar.alloc(F32, [128, 64]) for _ in range(NRING)],
        "SCr": [ar.alloc(F32, [128, 64]) for _ in range(NRING)],
    }
    ar.push()
    wq_sb = ar.alloc(BF16, [128, NC, 3 * D])
    lg_wq = p.g(lambda e: e.dma_start(out=wq_sb, in_=w_qkv.rearrange("(c p) e -> p c e", p=128)),
                inc="lg")
    stats_q, (wgq_h, wuq_h, wdq_h) = plan_wquant(
        nc, p, pfx, [(w_gate, I, D, "wgq"), (w_up, I, D, "wuq"), (w_down, D, I, "wdq")],
        wbufs)
    wq_g_done = p.now()["g"]
    wq_lg_done = p.now()["lg"]

    ar.push()
    xnT_sb = ar.alloc(BF16, [128, NC, T])
    ar.push()
    xf = [ar.alloc(F32, [128, D]) for _ in range(2)]
    sq = ar.alloc(F32, [128, D])
    xn32 = [ar.alloc(F32, [128, D]) for _ in range(2)]
    xnb = [ar.alloc(BF16, [128, D]) for _ in range(2)]
    st8 = ar.alloc(F32, [128, 8])
    nsum, sqs, negm, ex2, m2, var, lntmp, rstd = (st8[:, i:i + 1] for i in range(8))
    v_consumed = [0, 0]
    tr_done = [0, 0]
    st_xn = [0, 0]
    v_evac_done = 0
    for i in range(NT):
        bi = i % 2
        lv = p.y(lambda e, i=i, bi=bi: e.dma_start(out=xf[bi], in_=x_in[i * 128:(i + 1) * 128, :]),
                 inc="ld", waits=[("v", v_consumed[bi])])
        p.v(lambda e, bi=bi: e.tensor_tensor(sq, xf[bi], xf[bi], op=ALU.mult),
            waits=[("ld", lv)])
        p.v(lambda e, bi=bi: e.tensor_reduce(nsum, xf[bi], axis=AX, op=ALU.add, negate=True))
        p.v(lambda e: e.tensor_reduce(sqs, sq, axis=AX, op=ALU.add))
        p.v(lambda e: e.tensor_scalar_mul(negm, nsum, 1.0 / D))
        p.v(lambda e: e.tensor_scalar(ex2, sqs, 1.0 / D, 1e-5, op0=ALU.mult, op1=ALU.add))
        p.v(lambda e: e.tensor_tensor(m2, negm, negm, op=ALU.mult))
        p.spacer()
        vst = p.v(lambda e: e.tensor_tensor(var, ex2, m2, op=ALU.subtract), inc="v")
        p.s(lambda e: e.activation(lntmp, var, AF.Ln), waits=[("v", vst)])
        sst = p.s(lambda e: e.activation(rstd, lntmp, AF.Exp, scale=-0.5), inc="s")
        p.v(lambda e: e.tensor_tensor(negm, negm, rstd, op=ALU.mult), waits=[("s", sst)])
        p.spacer()
        p.v(lambda e, bi=bi: e.tensor_scalar(xn32[bi], xf[bi], rstd[:, 0:1], negm[:, 0:1],
                                             op0=ALU.mult, op1=ALU.add),
            waits=[("st", st_xn[bi])])
        vdone = p.v(lambda e, bi=bi: e.tensor_copy(xnb[bi], xn32[bi]), inc="v",
                    waits=[("t", tr_done[bi])])
        v_consumed[bi] = vdone
        st_xn[bi] = p.y(lambda e, i=i, bi=bi: e.dma_start(
            out=xn_h[i * 128:(i + 1) * 128, :], in_=xn32[bi]),
            inc="st", waits=[("v", vdone)])
        for j in range(NC):
            w = [("v", vdone), ("v", v_evac_done)] if j == 0 else []
            last = p.t(lambda e, j=j, bi=bi: e.transpose(psT[:, j * 128:(j + 1) * 128],
                                                         xnb[bi][:, j * 128:(j + 1) * 128],
                                                         ident_sb),
                       inc="t" if j == NC - 1 else None, waits=w)
        tr_done[bi] = last
        dst = xnT_sb[:, :, i * 128:(i + 1) * 128]
        v_evac_done = p.v(lambda e, dst=dst: e.tensor_copy(
            dst, psT[:].rearrange("p (c q) -> p c q", q=128)),
            inc="v", waits=[("t", last)])
        drain_wstats(p, wbufs, stats_q, DRAIN_S1)
    ar.pop()
    p.barrier()

    if STOP_STAGE <= 1:
        return
    ar.push()
    bq_sb = ar.alloc(F32, [128, 24])
    bvb_sb = ar.alloc(F32, [128, D])
    qe_sb = ar.alloc(BF16, [128, T])
    vt_sb = ar.alloc(BF16, [128, NH * 65])
    p.y(lambda e: e.dma_start(out=bq_sb, in_=b_pc), inc="ld")
    lv_b = p.y(lambda e: e.dma_start(out=bvb_sb, in_=bv_bc), inc="ld")
    p.pending["tensor"].append(("lg", lg_wq))
    p.pending["scalar"].append(("ld", lv_b))
    p.pending["vector"].append(("ld", lv_b))
    p.barrier(engines=("tensor",))
    for et in range(16):
        s_evac = [0, 0]
        for nsl in range(8):
            pb = nsl % 2
            psq = ps_big[:, pb * 512:(pb + 1) * 512]
            for c in range(NC):
                w = [("s", s_evac[pb])] if c == 0 else []
                tmm = p.t(lambda e, c=c, et=et, nsl=nsl, psq=psq: e.matmul(
                    psq, wq_sb[:, c, et * 128:(et + 1) * 128],
                    xnT_sb[:, c, nsl * 512:(nsl + 1) * 512],
                    start=(c == 0), stop=(c == NC - 1)),
                    inc="t" if c == NC - 1 else None, waits=w)
            s_evac[pb] = p.s(lambda e, et=et, nsl=nsl, psq=psq: e.activation(
                qe_sb[:, nsl * 512:(nsl + 1) * 512], psq, AF.Identity,
                bias=bq_sb[:, et:et + 1]), inc="s", waits=[("t", tmm)])
        dst_h = qT_h if et < 8 else kT_h
        row = (et % 8) * 128
        stv = p.y(lambda e, dst_h=dst_h, row=row: e.dma_start(
            out=dst_h[row:row + 128, :], in_=qe_sb), inc="st",
            waits=[("s", s_evac[0]), ("s", s_evac[1])])
        p.pending["scalar"].append(("st", stv))
        drain_wstats(p, wbufs, stats_q, 1)
    ovv = vt_sb.rearrange("p (h c) -> p h c", c=65)[:, :, 64:65]
    vmemset = p.v(lambda e: e.memset(ovv, 1.0), inc="v")
    p.pending["vector"].append(("v", vmemset))
    st_v = 0
    v_evac = [0, 0]
    for ti in range(NT):
        for vsl in range(2):
            psv = ps_big[:, vsl * 512:(vsl + 1) * 512]
            for c in range(NC):
                w = [("v", v_evac[vsl]), ("st", st_v)] if c == 0 else []
                tmm = p.t(lambda e, c=c, ti=ti, vsl=vsl, psv=psv: e.matmul(
                    psv, xnT_sb[:, c, ti * 128:(ti + 1) * 128],
                    wq_sb[:, c, 2 * D + vsl * 512:2 * D + (vsl + 1) * 512],
                    start=(c == 0), stop=(c == NC - 1)),
                    inc="t" if c == NC - 1 else None, waits=w)
            dst2 = vt_sb.rearrange("p (h c) -> p h c", c=65)[:, vsl * 8:(vsl + 1) * 8, 0:64]
            psv3 = psv.rearrange("p (h c) -> p h c", c=64)
            bvb3 = bvb_sb[:, vsl * 512:(vsl + 1) * 512].rearrange("p (h c) -> p h c", c=64)
            v_evac[vsl] = p.v(lambda e, psv3=psv3, bvb3=bvb3, dst2=dst2: e.tensor_tensor(
                dst2, psv3, bvb3, op=ALU.add), inc="v", waits=[("t", tmm)])
        st_v = p.y(lambda e, ti=ti: e.dma_start(out=vP_h[ti * 128:(ti + 1) * 128, :], in_=vt_sb),
                   inc="st", waits=[("v", v_evac[0]), ("v", v_evac[1])])
        if ti % 2 == 0:
            drain_wstats(p, wbufs, stats_q, 1)
    ar.pop()
    ar.pop()
    ar.pop()
    p.barrier()

    if STOP_STAGE <= 2:
        return
    ar.push()
    qh_sb = [ar.alloc(BF16, [64, T]) for _ in range(2)]
    kh_sb = [ar.alloc(BF16, [64, T]) for _ in range(2)]
    vh_sb = [ar.alloc(BF16, [128, NT, 65]) for _ in range(2)]
    if rotary:
        qw_sb = ar.alloc(BF16, [64, T])
        rt1 = ar.alloc(BF16, [64, T])
        rA_sb = ar.alloc(BF16, [64, T])
        rB_sb = ar.alloc(BF16, [64, T])
        p.y(lambda e: e.dma_start(out=rA_sb, in_=rotA), inc="ld")
        p.y(lambda e: e.dma_start(out=rB_sb, in_=rotB), inc="ld")
    hT_sb = ar.alloc(BF16, [128, T])
    u_sb = [ar.alloc(BF16, [128, 1024]) for _ in range(2)]
    numsb = ar.alloc(F32, [64, 512])
    rd_sb = ar.alloc(F32, [1, 512])
    rdb_sb = ar.alloc(BF16, [1, 512])
    if causal:
        cm_sb = ar.alloc(BF16, [128, 4, 512])
        p.y(lambda e: e.dma_start(out=cm_sb, in_=cmask.rearrange("r p q -> p r q")), inc="ld")
    ps_sc = [ps_big[:, i * 512:(i + 1) * 512] for i in range(4)]
    ps_num = ps_big[:, 2048:2560]
    ps_bc = ps_big[:, 2560:3072]
    st_hT = 0
    buf_t_use = [0, 0]
    buf_v_use = [0, 0]
    ld_head = [0, 0]

    def emit_head_loads(h):
        hb = h % 2
        w = [("t", buf_t_use[hb]), ("v", buf_v_use[hb])]
        p.y(lambda e, h=h, hb=hb: e.dma_start(
            out=qh_sb[hb], in_=qT_h[h * 64:(h + 1) * 64, :]), inc="ld", waits=w)
        p.y(lambda e, h=h, hb=hb: e.dma_start(
            out=kh_sb[hb], in_=kT_h[h * 64:(h + 1) * 64, :]), inc="ld")
        lv = p.y(lambda e, h=h, hb=hb: e.dma_start(
            out=vh_sb[hb],
            in_=vP_h.rearrange("(t p) c -> p t c", p=128)[:, :, h * 65:(h + 1) * 65]),
            inc="ld")
        ld_head[hb] = lv
        return lv

    v_rot = 0
    emit_head_loads(0)
    for h in range(S3_HEADS):
        hb = h % 2
        lq = ld_head[hb]
        if rotary:
            for src_h, dst_sb in [(qT_h, qh_sb[hb]), (kT_h, kh_sb[hb])]:
                p.y(lambda e, h=h, src_h=src_h: e.dma_start(
                    out=qw_sb[0:32, :], in_=src_h[h * 64 + 32:(h + 1) * 64, :]),
                    inc="ld", waits=[("v", v_rot)])
                lsw = p.y(lambda e, h=h, src_h=src_h: e.dma_start(
                    out=qw_sb[32:64, :], in_=src_h[h * 64:h * 64 + 32, :]), inc="ld")
                p.v(lambda e, dst_sb=dst_sb: e.tensor_tensor(rt1, rA_sb, dst_sb, op=ALU.mult),
                    waits=[("ld", lsw), ("ld", lq)])
                p.v(lambda e: e.tensor_tensor(qw_sb, rB_sb, qw_sb, op=ALU.mult))
                v_rot = p.v(lambda e, dst_sb=dst_sb: e.tensor_tensor(
                    dst_sb, rt1, qw_sb, op=ALU.add), inc="v")
        v_norm = 0
        s_u = [0, 0]
        v_u = [0, 0]
        t_av_pair = [0, 0]
        for qb in range(8):
            kt_list = list(range(NT)) if not causal else list(range(min(NT, (qb + 1) * 4)))
            kt_first, kt_last = kt_list[0], kt_list[-1]
            pairs = [kt_list[i:i + 2] for i in range(0, len(kt_list), 2)]
            t_av = 0
            for pi, pr in enumerate(pairs):
                pb = pi % 2
                t_sc = 0
                for kj, kt in enumerate(pr):
                    bank = (2 * pi + kj) % 4
                    w = []
                    if kj == 0:
                        w = [("s", s_u[pb])]
                        if qb == 0 and pi < 2:
                            w.append(("ld", lq))
                        if rotary and qb == 0:
                            w.append(("v", v_rot))
                    t_sc = p.t(lambda e, kt=kt, qb=qb, bank=bank, hb=hb: e.matmul(
                        ps_sc[bank], kh_sb[hb][:, kt * 128:(kt + 1) * 128],
                        qh_sb[hb][:, qb * 512:(qb + 1) * 512], start=True, stop=True),
                        inc="t", waits=w)
                b0 = (2 * pi) % 4
                npair = len(pr)
                w = [("t", t_sc), ("t", t_av_pair[pb])]
                s_u[pb] = p.s(lambda e, b0=b0, pb=pb, npair=npair: e.activation(
                    u_sb[pb][:, 0:npair * 512], ps_big[:, b0 * 512:(b0 + npair) * 512],
                    AF.Exp, scale=scale), inc="s", waits=w)
                dep = ("s", s_u[pb])
                if causal:
                    for kj, kt in enumerate(pr):
                        if kt * 128 >= qb * 512:
                            r = (kt * 128 - qb * 512) // 128
                            v_u[pb] = p.v(lambda e, pb=pb, r=r, kj=kj: e.tensor_tensor(
                                u_sb[pb][:, kj * 512:(kj + 1) * 512], cm_sb[:, r, :],
                                u_sb[pb][:, kj * 512:(kj + 1) * 512], op=ALU.mult),
                                inc="v", waits=[("s", s_u[pb])])
                            dep = ("v", v_u[pb])
                for kj, kt in enumerate(pr):
                    w = [dep]
                    if kt == kt_first and v_norm:
                        w.append(("v", v_norm))
                    t_av = p.t(lambda e, kt=kt, pb=pb, kj=kj, hb=hb,
                               st=(kt == kt_first), sp=(kt == kt_last): e.matmul(
                        ps_num[0:65, :], vh_sb[hb][:, kt, :],
                        u_sb[pb][:, kj * 512:(kj + 1) * 512], start=st, stop=sp),
                        inc="t", waits=w)
                t_av_pair[pb] = t_av
            p.v(lambda e: e.reciprocal(rd_sb, ps_num[64:65, :]), waits=[("t", t_av)])
            p.v(lambda e: e.tensor_copy(rdb_sb, rd_sb))
            vv = p.v(lambda e: e.tensor_copy(numsb, ps_num[0:64, :]), inc="v")
            tbc = p.t(lambda e: e.matmul(ps_bc[0:64, :], ones64_sb, rdb_sb,
                                         start=True, stop=True), inc="t",
                      waits=[("v", vv)])
            w = [("t", tbc), ("st", st_hT)]
            v_norm = p.v(lambda e, qb=qb: e.tensor_tensor(
                hT_sb[0:64, qb * 512:(qb + 1) * 512], numsb, ps_bc[0:64, :], op=ALU.mult),
                inc="v", waits=w)
        buf_t_use[hb] = p.now()["t"]
        buf_v_use[hb] = v_rot
        if h + 1 < S3_HEADS:
            emit_head_loads(h + 1)
        st_hT = p.y(lambda e, h=h: e.dma_start(out=hT_h[h * 64:(h + 1) * 64, :],
                                               in_=hT_sb[0:64]),
                    inc="st", waits=[("v", v_norm)])
        drain_wstats(p, wbufs, stats_q, 2)
    ar.pop()
    p.barrier()

    if STOP_STAGE <= 3:
        return
    ar.push()
    wo_f = ar.alloc(F32, [128, NC * D // 4])
    wo_sb = ar.alloc(BF16, [128, NC, D])
    bob_sb = ar.alloc(F32, [128, D])
    hTt_sb = [ar.alloc(BF16, [128, NC, 128]) for _ in range(2)]
    s_sb = [ar.alloc(BF16, [128, D]) for _ in range(2)]
    xn_f = [ar.alloc(F32, [128, D]) for _ in range(2)]
    t14 = ar.alloc(F32, [128, 512])
    v_cast = 0
    wo_f3 = wo_f.rearrange("p (c e) -> p c e", e=D)
    for q4 in range(4):
        lw = p.y(lambda e, q4=q4: e.dma_start(
            out=wo_f3, in_=w_o.rearrange("(c p) e -> p c e", p=128)[:, q4 * 2:(q4 + 1) * 2, :]),
            inc="ld", waits=[("v", v_cast)])
        v_cast = p.v(lambda e, q4=q4: e.tensor_copy(
            wo_sb[:, q4 * 2:(q4 + 1) * 2, :], wo_f3),
            inc="v", waits=[("ld", lw)])
    p.y(lambda e: e.dma_start(out=bob_sb, in_=bo_bc), inc="ld")
    p.barrier()
    drain_wstats(p, wbufs, stats_q, 99)
    t_cons = [0, 0]
    st_s = [0, 0]
    v_ev_prev = [0, 0]
    for ti in range(NT):
        bi = ti % 2
        lv = p.y(lambda e, ti=ti, bi=bi: e.dma_start(
            out=hTt_sb[bi],
            in_=hT_h.rearrange("(c p) t -> p c t", p=128)[:, :, ti * 128:(ti + 1) * 128]),
            inc="ld", waits=[("t", t_cons[bi])])
        if add_xn:
            lv = p.y(lambda e, ti=ti, bi=bi: e.dma_start(out=xn_f[bi],
                                                         in_=xn_h[ti * 128:(ti + 1) * 128, :]),
                     inc="ld", waits=[("v", v_ev_prev[bi])])
        v_ev = 0
        for esl in range(2):
            pso = ps_big[:, esl * 512:(esl + 1) * 512]
            for c in range(NC):
                w = [("ld", lv), ("st", st_s[bi]), ("v", v_ev_prev[bi])] if c == 0 else []
                tmm = p.t(lambda e, c=c, esl=esl, bi=bi, pso=pso: e.matmul(
                    pso, hTt_sb[bi][:, c, :], wo_sb[:, c, esl * 512:(esl + 1) * 512],
                    start=(c == 0), stop=(c == NC - 1)),
                    inc="t" if c == NC - 1 else None, waits=w)
            t_cons[bi] = tmm
            sl = slice(esl * 512, (esl + 1) * 512)
            if add_xn:
                p.v(lambda e, pso=pso, sl=sl: e.tensor_tensor(t14, pso, bob_sb[:, sl],
                                                              op=ALU.add),
                    waits=[("t", tmm)])
                v_ev = p.v(lambda e, sl=sl, bi=bi: e.tensor_tensor(
                    s_sb[bi][:, sl], t14, xn_f[bi][:, sl], op=ALU.add), inc="v")
            else:
                v_ev = p.v(lambda e, pso=pso, sl=sl, bi=bi: e.tensor_tensor(
                    s_sb[bi][:, sl], pso, bob_sb[:, sl], op=ALU.add), inc="v",
                    waits=[("t", tmm)])
        v_ev_prev[bi] = v_ev
        st_s[bi] = p.y(lambda e, ti=ti, bi=bi: e.dma_start(
            out=s_h[ti * 128:(ti + 1) * 128, :], in_=s_sb[bi]),
            inc="st", waits=[("v", v_ev)])
    ar.pop()
    p.barrier()

    if STOP_STAGE <= 4:
        return
    ar.push()
    wgT_sb = ar.alloc(BF16, [128, NC, I])
    wuT_sb = ar.alloc(BF16, [128, NC, I])
    aqs = [{"A": wbufs["A"], "B": wbufs["B"], "BM": wbufs["BMr"][k], "SC": wbufs["SCr"][k]}
           for k in range(4)]
    aq = aqs[3]
    ld_w = 0
    for wsrc, wdst in [(wgq_h, wgT_sb), (wuq_h, wuT_sb)]:
        for c in range(NC):
            ld_w = p.y(lambda e, wsrc=wsrc, wdst=wdst, c=c: e.dma_start(
                out=wdst[:, c, :], in_=wsrc[:, c * 128:(c + 1) * 128], transpose=True),
                inc="ld", waits=[("g", wq_g_done), ("lg", wq_lg_done)] if c == 0 else [])
    A2 = ar.alloc(F32, [128, I])
    B2 = ar.alloc(F32, [128, I])
    BM2 = ar.alloc(F32, [128, I // 16])
    SC2 = ar.alloc(F32, [128, I // 16])
    aqh = {"A": A2, "B": B2, "BM": BM2, "SC": SC2}
    xnq_f = [ar.alloc(F32, [128, D]) for _ in range(2)]
    xq_sb = ar.alloc(BF16, [128, D])
    xqT_sb = [ar.alloc(BF16, [128, D]) for _ in range(2)]
    h_f = ar.alloc(F32, [128, I])
    hq_sb = ar.alloc(BF16, [128, I])
    hqT_sb = [ar.alloc(BF16, [128, NI, 128]) for _ in range(2)]
    GSL = [(i * 512, min(I, (i + 1) * 512)) for i in range(6)]
    v_xq_used = [0, 0]
    v_hq_prev = 0
    t_xq_tr = 0
    v_h_prev = 0
    s_h_prev = 0
    vq2_prev = 0
    t_hq_tr_prev = 0
    st_hq = [0, 0]
    for ti in range(NT):
        bi = ti % 2
        lv = p.y(lambda e, ti=ti, bi=bi: e.dma_start(out=xnq_f[bi],
                                                     in_=xn_h[ti * 128:(ti + 1) * 128, :]),
                 inc="ld", waits=[("v", v_xq_used[bi])])
        vq = emit_quant(p, "v", xnq_f[bi], xq_sb, aq, D,
                        waits_first=[("ld", lv), ("v", vq2_prev), ("t", t_xq_tr)] +
                        ([("g", wq_g_done)] if ti == 0 else []), inc_last="v")
        v_xq_used[bi] = vq
        for j in range(NC):
            w = [("v", vq), ("v", v_hq_prev)] if j == 0 else []
            ttr = p.t(lambda e, j=j: e.transpose(psT[:, j * 128:(j + 1) * 128],
                                                 xq_sb[:, j * 128:(j + 1) * 128], ident_sb),
                      inc="t" if j == NC - 1 else None,
                      waits=w + ([("ld", ld_w)] if ti == 0 and j == 0 else []))
        t_xq_tr = ttr
        vxq = p.v(lambda e, bi=bi: e.tensor_copy(xqT_sb[bi], psT[:]), inc="v",
                  waits=[("t", ttr)])
        s_ev = 0
        v_h = 0
        for phase in range(2):
            wsb = wgT_sb if phase == 0 else wuT_sb
            for c in range(NC):
                for isl, (a, b) in enumerate(GSL):
                    w = []
                    if c == 0 and isl == 0:
                        w = [("v", vxq)] if phase == 1 else [("v", vxq), ("v", v_h_prev)]
                        if phase == 1:
                            w.append(("s", s_ev))
                    tmm = p.t(lambda e, c=c, a=a, b=b, wsb=wsb, bi=bi: e.matmul(
                        ps_big[:, a:b], xqT_sb[bi][:, c * 128:(c + 1) * 128],
                        wsb[:, c, a:b], start=(c == 0), stop=(c == NC - 1)),
                        inc="t" if (c == NC - 1 and isl == len(GSL) - 1) else None, waits=w)
            if phase == 0:
                for isl, (a, b) in enumerate(GSL):
                    s_ev = p.s(lambda e, a=a, b=b: e.activation(
                        h_f[:, a:b], ps_big[:, a:b], AF.Silu),
                        inc="s" if isl == len(GSL) - 1 else None,
                        waits=[("t", tmm), ("v", vq2_prev)] if isl == 0 else [])
                s_h_prev = s_ev
            else:
                for isl, (a, b) in enumerate(GSL):
                    v_h = p.v(lambda e, a=a, b=b: e.tensor_tensor(
                        h_f[:, a:b], ps_big[:, a:b], h_f[:, a:b], op=ALU.mult),
                        inc="v" if isl == len(GSL) - 1 else None,
                        waits=[("t", tmm), ("s", s_ev)] if isl == 0 else [])
                v_h_prev = v_h
        vq2 = emit_quant(p, "v", h_f, hq_sb, aqh, I,
                         waits_first=[("v", v_h), ("t", t_hq_tr_prev)], inc_last="v")
        vq2_prev = vq2
        v_hq = 0
        for jb in range(0, NI, 8):
            jn = min(8, NI - jb)
            for j in range(jn):
                w = []
                if jb == 0 and j == 0:
                    w = [("v", vq2), ("v", vxq)]
                elif j == 0:
                    w = [("v", v_hq)]
                ttr2 = p.t(lambda e, j=j, jb=jb: e.transpose(
                    psT[:, j * 128:(j + 1) * 128],
                    hq_sb[:, (jb + j) * 128:(jb + j + 1) * 128], ident_sb),
                    inc="t" if j == jn - 1 else None, waits=w)
            dst = hqT_sb[bi][:, jb:jb + jn, :]
            src = psT[:].rearrange("p (c q) -> p c q", q=128)[:, 0:jn, :]
            v_hq = p.v(lambda e, dst=dst, src=src: e.tensor_copy(dst, src),
                       inc="v", waits=[("t", ttr2), ("st", st_hq[bi])] if jb == 0 else
                       [("t", ttr2)])
        t_hq_tr_prev = ttr2
        v_hq_prev = v_hq
        st_hq[bi] = p.y(lambda e, ti=ti, bi=bi: e.dma_start(
            out=hqT_h.rearrange("(c p) t -> p c t", p=128)[:, :, ti * 128:(ti + 1) * 128],
            in_=hqT_sb[bi]), inc="st", waits=[("v", v_hq)])
    ar.pop()
    p.barrier()

    if STOP_STAGE <= 5:
        return
    ar.push()
    wdT_sb = ar.alloc(BF16, [128, NI, D])
    ld_w = 0
    for c in range(NI):
        ld_w = p.y(lambda e, c=c: e.dma_start(
            out=wdT_sb[:, c, :], in_=wdq_h[:, c * 128:(c + 1) * 128], transpose=True),
            inc="ld")
    hqt_t = [ar.alloc(BF16, [128, NI, 128]) for _ in range(2)]
    g_sb = [ar.alloc(BF16, [128, D]) for _ in range(2)]
    s_tl = [ar.alloc(BF16, [128, D]) for _ in range(2)]
    x_tl = [ar.alloc(F32, [128, D]) for _ in range(2)]
    sg_f = ar.alloc(F32, [128, D])
    xo_f = [ar.alloc(F32, [128, D]) for _ in range(2)]
    t_cons = [0, 0]
    v_done = [0, 0]
    s_gd = [0, 0]
    st_x = [0, 0]
    for ti in range(NT):
        bi = ti % 2
        lv = p.y(lambda e, ti=ti, bi=bi: e.dma_start(
            out=hqt_t[bi],
            in_=hqT_h.rearrange("(c p) t -> p c t", p=128)[:, :, ti * 128:(ti + 1) * 128]),
            inc="ld", waits=[("t", t_cons[bi])])
        p.y(lambda e, ti=ti, bi=bi: e.dma_start(out=s_tl[bi],
                                                in_=s_h[ti * 128:(ti + 1) * 128, :]),
            inc="ld", waits=[("v", v_done[bi])])
        lv = p.y(lambda e, ti=ti, bi=bi: e.dma_start(out=x_tl[bi],
                                                     in_=x_in[ti * 128:(ti + 1) * 128, :]),
                 inc="ld")
        t_dn = 0
        for dsl in range(2):
            psd = ps_big[:, dsl * 512:(dsl + 1) * 512]
            for c in range(NI):
                w = []
                if c == 0:
                    w = [("ld", lv), ("s", s_gd[dsl])]
                    if ti == 0 and dsl == 0:
                        w.append(("ld", ld_w))
                t_dn = p.t(lambda e, c=c, dsl=dsl, psd=psd, bi=bi: e.matmul(
                    psd, hqt_t[bi][:, c, :], wdT_sb[:, c, dsl * 512:(dsl + 1) * 512],
                    start=(c == 0), stop=(c == NI - 1)),
                    inc="t" if c == NI - 1 else None, waits=w)
            t_cons[bi] = t_dn
            s_gd[dsl] = p.s(lambda e, dsl=dsl, psd=psd, bi=bi: e.activation(
                g_sb[bi][:, dsl * 512:(dsl + 1) * 512], psd, AF.Silu),
                inc="s", waits=[("t", t_dn), ("v", v_done[bi])] if dsl == 0 else [("t", t_dn)])
        if DEBUG:
            p.y(lambda e, ti=ti, bi=bi: e.dma_start(out=g_h[ti * 128:(ti + 1) * 128, :],
                                                    in_=g_sb[bi]),
                inc="st", waits=[("s", s_gd[1])])
        p.v(lambda e, bi=bi: e.tensor_tensor(sg_f, s_tl[bi], g_sb[bi], op=ALU.mult),
            waits=[("s", s_gd[1]), ("st", st_x[bi])])
        v_done[bi] = p.v(lambda e, bi=bi: e.tensor_tensor(xo_f[bi], x_tl[bi], sg_f, op=ALU.add),
                         inc="v")
        st_x[bi] = p.y(lambda e, ti=ti, bi=bi: e.dma_start(
            out=x_out[ti * 128:(ti + 1) * 128, :], in_=xo_f[bi]),
            inc="st", waits=[("v", v_done[bi])])
    ar.pop()
    ar.pop()
    LAST_PEAK.update({str(k): v for k, v in ar.peak.items()})


def emit_l3(nc, p, ar, ps_big, psT, ident_sb, pfx, x_in, x_out):
    HID = 4 * D
    NHID = HID // 128
    dt = lambda name, *a, **k: nc.dram_tensor(pfx + name, *a, **k)
    wq = dt("wq", [D, D], F32, kind="ExternalInput").ap()
    wk = dt("wk", [D, D], F32, kind="ExternalInput").ap()
    wv = dt("wv", [D, D], F32, kind="ExternalInput").ap()
    wo = dt("wo", [D, D], F32, kind="ExternalInput").ap()
    ln1g = dt("ln1g", [128, D], F32, kind="ExternalInput").ap()
    ln1b = dt("ln1b", [128, D], F32, kind="ExternalInput").ap()
    ln2g = dt("ln2g", [128, D], F32, kind="ExternalInput").ap()
    ln2b = dt("ln2b", [128, D], F32, kind="ExternalInput").ap()
    cw1 = dt("cw1", [3, D, HID], F32, kind="ExternalInput").ap()
    cb1 = dt("cb1", [3, 128, NHID], F32, kind="ExternalInput").ap()
    cw2 = dt("cw2", [3, HID, D], F32, kind="ExternalInput").ap()
    cb2 = dt("cb2", [3, 128, NC], F32, kind="ExternalInput").ap()
    wrT = dt("wrT", [D, D], F32, kind="ExternalInput").ap()
    wkT = dt("wkT", [D, D], F32, kind="ExternalInput").ap()
    wvT = dt("wvT", [D, D], F32, kind="ExternalInput").ap()
    mur = dt("mur", [128, D], F32, kind="ExternalInput").ap()
    muk = dt("muk", [128, D], F32, kind="ExternalInput").ap()
    trilm = dt("trilm", [128, 128], BF16, kind="ExternalInput").ap()

    kind_dbg = "ExternalOutput" if DEBUG else "Internal"
    xn_h = dt("xn_h", [T, D], F32, kind=kind_dbg).ap()
    qT_h = dt("qT_h", [D, T], BF16, kind=kind_dbg).ap()
    kT_h = dt("kT_h", [D, T], BF16, kind=kind_dbg).ap()
    ktok_h = dt("ktok_h", [T, D], BF16, kind=kind_dbg).ap()
    vtok_h = dt("vtok_h", [T, D], BF16, kind=kind_dbg).ap()
    yT_h = dt("yT_h", [D, T], BF16, kind=kind_dbg).ap()
    x1_h = dt("x1_h", [T, D], F32, kind=kind_dbg).ap()
    hT_hs = [dt(f"h{i}T_h", [D, T], BF16, kind=kind_dbg).ap() for i in range(4)]
    g3_h = dt("g3_h", [T, D], BF16, kind=kind_dbg).ap() if DEBUG else None

    if True:
        ar.push()
        tril_sb = ar.alloc(BF16, [128, 128])
        p.y(lambda e: e.dma_start(out=tril_sb, in_=trilm), inc="ld")

        def emit_ln_tokens(src_getter, dst_h, g_bc, b_bc, make_T, hT_dst, src_store_h=None):
            xfb = [ar.alloc(F32, [128, D]) for _ in range(2)]
            sqb = ar.alloc(F32, [128, D])
            on32 = [ar.alloc(F32, [128, D]) for _ in range(2)]
            onb = [ar.alloc(BF16, [128, D]) for _ in range(2)]
            st8 = ar.alloc(F32, [128, 8])
            nsum, sqs, negm, ex2, m2, var, lntmp, rstd = (st8[:, i:i + 1] for i in range(8))
            v_consumed = [0, 0]
            tr_done = [0, 0]
            st_n = [0, 0]
            st_s = [0, 0]
            v_evac_done = 0
            for i in range(NT):
                bi = i % 2
                fn, ws = src_getter(i, xfb[bi], [("v", v_consumed[bi]), ("st", st_s[bi])])
                lv = p.y(fn, inc="ld", waits=ws)
                p.v(lambda e, bi=bi: e.tensor_tensor(sqb, xfb[bi], xfb[bi], op=ALU.mult),
                    waits=[("ld", lv)])
                p.v(lambda e, bi=bi: e.tensor_reduce(nsum, xfb[bi], axis=AX, op=ALU.add,
                                                     negate=True))
                p.v(lambda e: e.tensor_reduce(sqs, sqb, axis=AX, op=ALU.add))
                p.v(lambda e: e.tensor_scalar_mul(negm, nsum, 1.0 / D))
                p.v(lambda e: e.tensor_scalar(ex2, sqs, 1.0 / D, 1e-5, op0=ALU.mult, op1=ALU.add))
                p.v(lambda e: e.tensor_tensor(m2, negm, negm, op=ALU.mult))
                p.spacer()
                vst = p.v(lambda e: e.tensor_tensor(var, ex2, m2, op=ALU.subtract), inc="v")
                p.s(lambda e: e.activation(lntmp, var, AF.Ln), waits=[("v", vst)])
                sst = p.s(lambda e: e.activation(rstd, lntmp, AF.Exp, scale=-0.5), inc="s")
                p.v(lambda e: e.tensor_tensor(negm, negm, rstd, op=ALU.mult), waits=[("s", sst)])
                p.spacer()
                p.v(lambda e, bi=bi: e.tensor_scalar(on32[bi], xfb[bi], rstd[:, 0:1], negm[:, 0:1],
                                                     op0=ALU.mult, op1=ALU.add),
                    waits=[("st", st_n[bi])])
                if g_bc is not None:
                    p.v(lambda e, bi=bi: e.tensor_tensor(on32[bi], on32[bi], g_bc, op=ALU.mult))
                    p.v(lambda e, bi=bi: e.tensor_tensor(on32[bi], on32[bi], b_bc, op=ALU.add))
                vdone = p.v(lambda e, bi=bi: e.tensor_copy(onb[bi], on32[bi]), inc="v",
                            waits=[("t", tr_done[bi])])
                v_consumed[bi] = vdone
                if src_store_h is not None:
                    st_s[bi] = p.y(lambda e, i=i, bi=bi: e.dma_start(
                        out=src_store_h[i * 128:(i + 1) * 128, :], in_=xfb[bi]),
                        inc="st", waits=[("v", vdone)])
                if dst_h is not None:
                    st_n[bi] = p.y(lambda e, i=i, bi=bi: e.dma_start(
                        out=dst_h[i * 128:(i + 1) * 128, :], in_=on32[bi]),
                        inc="st", waits=[("v", vdone)])
                if make_T:
                    for j in range(NC):
                        w = [("v", vdone), ("v", v_evac_done)] if j == 0 else []
                        last = p.t(lambda e, j=j, bi=bi: e.transpose(
                            psT[:, j * 128:(j + 1) * 128], onb[bi][:, j * 128:(j + 1) * 128],
                            ident_sb), inc="t" if j == NC - 1 else None, waits=w)
                    tr_done[bi] = last
                    dst = hT_dst[:, :, i * 128:(i + 1) * 128]
                    v_evac_done = p.v(lambda e, dst=dst: e.tensor_copy(
                        dst, psT[:].rearrange("p (c q) -> p c q", q=128)),
                        inc="v", waits=[("t", last)])

        ar.push()
        emit_ln_tokens(lambda i, dst, ws: ((lambda e, i=i, dst=dst: e.dma_start(
            out=dst, in_=x_in[i * 128:(i + 1) * 128, :])), ws), xn_h, None, None, False, None)
        ar.pop()
        p.barrier()

        ar.push()
        h1T_sb = ar.alloc(BF16, [128, NC, T])
        ar.push()
        l1g_sb = ar.alloc(F32, [128, D])
        l1b_sb = ar.alloc(F32, [128, D])
        p.y(lambda e: e.dma_start(out=l1g_sb, in_=ln1g), inc="ld")
        lvg = p.y(lambda e: e.dma_start(out=l1b_sb, in_=ln1b), inc="ld")
        p.pending["vector"].append(("ld", lvg))
        emit_ln_tokens(lambda i, dst, ws: ((lambda e, i=i, dst=dst: e.dma_start(
            out=dst, in_=xn_h[i * 128:(i + 1) * 128, :])), ws), None, l1g_sb, l1b_sb,
            True, h1T_sb)
        ar.pop()
        p.barrier()

        ar.push()
        wp_sb = ar.alloc(BF16, [128, NC, 3 * D])
        qe_sb = ar.alloc(BF16, [128, T])
        kv_sb = ar.alloc(BF16, [128, D])
        p.g(lambda e: e.dma_start(out=wp_sb[:, :, 0:D], in_=wq.rearrange("(c p) e -> p c e", p=128)),
            inc="lg")
        p.g(lambda e: e.dma_start(out=wp_sb[:, :, D:2 * D],
                                  in_=wk.rearrange("(c p) e -> p c e", p=128)), inc="lg")
        p.g(lambda e: e.dma_start(out=wp_sb[:, :, 2 * D:3 * D],
                                  in_=wv.rearrange("(c p) e -> p c e", p=128)), inc="lg")
        p.barrier()
        for et in range(16):
            off = (et % 8) * 128 + (et // 8) * D
            s_evac = [0, 0]
            for nsl in range(8):
                pb = nsl % 2
                psq = ps_big[:, pb * 512:(pb + 1) * 512]
                for c in range(NC):
                    w = [("s", s_evac[pb])] if c == 0 else []
                    tmm = p.t(lambda e, c=c, off=off, nsl=nsl, psq=psq: e.matmul(
                        psq, wp_sb[:, c, off:off + 128],
                        h1T_sb[:, c, nsl * 512:(nsl + 1) * 512],
                        start=(c == 0), stop=(c == NC - 1)),
                        inc="t" if c == NC - 1 else None, waits=w)
                s_evac[pb] = p.s(lambda e, nsl=nsl, psq=psq: e.activation(
                    qe_sb[:, nsl * 512:(nsl + 1) * 512], psq, AF.Identity),
                    inc="s", waits=[("t", tmm)])
            dst_h = qT_h if et < 8 else kT_h
            row = (et % 8) * 128
            stv = p.y(lambda e, dst_h=dst_h, row=row: e.dma_start(
                out=dst_h[row:row + 128, :], in_=qe_sb), inc="st",
                waits=[("s", s_evac[0]), ("s", s_evac[1])])
            p.pending["scalar"].append(("st", stv))
        st_v = 0
        v_evac = [0, 0]
        for part, dst_h in [(1, ktok_h), (2, vtok_h)]:
            for ti in range(NT):
                for vsl in range(2):
                    psv = ps_big[:, vsl * 512:(vsl + 1) * 512]
                    for c in range(NC):
                        w = [("v", v_evac[vsl]), ("st", st_v)] if c == 0 else []
                        tmm = p.t(lambda e, c=c, ti=ti, vsl=vsl, psv=psv, part=part: e.matmul(
                            psv, h1T_sb[:, c, ti * 128:(ti + 1) * 128],
                            wp_sb[:, c, part * D + vsl * 512:part * D + (vsl + 1) * 512],
                            start=(c == 0), stop=(c == NC - 1)),
                            inc="t" if c == NC - 1 else None, waits=w)
                    v_evac[vsl] = p.v(lambda e, psv=psv, vsl=vsl: e.tensor_copy(
                        kv_sb[:, vsl * 512:(vsl + 1) * 512], psv), inc="v", waits=[("t", tmm)])
                st_v = p.y(lambda e, ti=ti, dst_h=dst_h: e.dma_start(
                    out=dst_h[ti * 128:(ti + 1) * 128, :], in_=kv_sb),
                    inc="st", waits=[("v", v_evac[0]), ("v", v_evac[1])])
        ar.pop()
        ar.pop()
        p.barrier()

        ar.push()
        qh_sb = ar.alloc(BF16, [64, T])
        kh_sb = ar.alloc(BF16, [64, T])
        ktk_sb = ar.alloc(BF16, [128, NT, 64])
        vtk_sb = ar.alloc(BF16, [128, NT, 64])
        msb = ar.alloc(BF16, [64, 64])
        ssb = ar.alloc(BF16, [128, 128])
        ysb = ar.alloc(BF16, [128, 64])
        yT_sb = ar.alloc(BF16, [64, T])
        ps_y = ps_big[:, 0:512]
        ps_s = ps_big[:, 512:1024]
        ps_M = ps_big[:, 1024:1536]
        for h in range(NH):
            p.y(lambda e, h=h: e.dma_start(out=qh_sb, in_=qT_h[h * 64:(h + 1) * 64, :]), inc="ld")
            p.y(lambda e, h=h: e.dma_start(out=kh_sb, in_=kT_h[h * 64:(h + 1) * 64, :]), inc="ld")
            p.y(lambda e, h=h: e.dma_start(
                out=ktk_sb, in_=ktok_h.rearrange("(t p) c -> p t c", p=128)[:, :, h * 64:(h + 1) * 64]),
                inc="ld")
            lq = p.y(lambda e, h=h: e.dma_start(
                out=vtk_sb, in_=vtok_h.rearrange("(t p) c -> p t c", p=128)[:, :, h * 64:(h + 1) * 64]),
                inc="ld")
            t_mup = 0
            v_mcp = 0
            t_yev = 0
            v_yev = 0
            s_yev = 0
            v_ss = 0
            t_intra_prev = 0
            for ci in range(NT):
                w0 = [("ld", lq)] if ci == 0 else []
                if ci > 0:
                    v_mcp = p.v(lambda e: e.tensor_copy(msb, ps_M[0:64, 0:64]), inc="v",
                                waits=[("t", t_mup)])
                    p.t(lambda e, ci=ci: e.matmul(
                        ps_y[0:128, 0:64], qh_sb[:, ci * 128:(ci + 1) * 128], msb,
                        start=True, stop=False), inc="t",
                        waits=[("v", v_mcp), ("s", s_yev)])
                tss = p.t(lambda e, ci=ci: e.matmul(
                    ps_s[:, 0:128], kh_sb[:, ci * 128:(ci + 1) * 128],
                    qh_sb[:, ci * 128:(ci + 1) * 128], start=True, stop=True), inc="t",
                    waits=w0 + [("v", v_ss)])
                v_ss = p.v(lambda e: e.tensor_tensor(ssb, tril_sb, ps_s[:, 0:128], op=ALU.mult),
                           inc="v", waits=[("t", tss), ("t", t_intra_prev)])
                t_intra_prev = p.t(lambda e, ci=ci: e.matmul(
                    ps_y[0:128, 0:64], ssb, vtk_sb[:, ci, :],
                    start=(ci == 0), stop=True), inc="t",
                    waits=[("v", v_ss)] + ([("s", s_yev)] if ci == 0 else []))
                t_mup = p.t(lambda e, ci=ci: e.matmul(
                    ps_M[0:64, 0:64], ktk_sb[:, ci, :], vtk_sb[:, ci, :],
                    start=(ci == 0), stop=(ci == NT - 1)), inc="t",
                    waits=[("v", v_mcp)] if ci > 0 else [])
                s_yev = p.s(lambda e: e.activation(ysb, ps_y[0:128, 0:64], AF.Identity),
                            inc="s", waits=[("t", t_intra_prev), ("t", t_yev)])
                jj = ci % 8
                t_yev = p.t(lambda e, jj=jj: e.transpose(
                    psT[0:64, jj * 128:(jj + 1) * 128], ysb, ident_sb), inc="t",
                    waits=[("s", s_yev)] + ([("v", v_yev)] if jj == 0 else []))
                if jj == 7:
                    dst = yT_sb[:, (ci - 7) * 128:(ci + 1) * 128]
                    v_yev = p.v(lambda e, dst=dst: e.tensor_copy(dst, psT[0:64, :]),
                                inc="v", waits=[("t", t_yev)])
            st_yT = p.y(lambda e, h=h: e.dma_start(out=yT_h[h * 64:(h + 1) * 64, :], in_=yT_sb),
                        inc="st", waits=[("v", v_yev)])
            p.pending["sync"].append(("t", p.now()["t"]))
            p.pending["vector"].append(("st", st_yT))
        ar.pop()
        p.barrier()

        ar.push()
        h2T_sb = ar.alloc(BF16, [128, NC, T])
        ar.push()
        wo_sb = ar.alloc(BF16, [128, NC, D])
        yt_t = [ar.alloc(BF16, [128, NC, 128]) for _ in range(2)]
        l2g_sb = ar.alloc(F32, [128, D])
        l2b_sb = ar.alloc(F32, [128, D])
        xnl_sb = ar.alloc(F32, [128, D])
        p.g(lambda e: e.dma_start(out=wo_sb, in_=wo.rearrange("(c p) e -> p c e", p=128)), inc="lg")
        p.y(lambda e: e.dma_start(out=l2g_sb, in_=ln2g), inc="ld")
        lvg = p.y(lambda e: e.dma_start(out=l2b_sb, in_=ln2b), inc="ld")
        p.pending["vector"].append(("ld", lvg))
        p.barrier()
        t_cons = [0, 0]

        def src_x1(i, dst, ws):
            bi = i % 2
            p.y(lambda e, i=i, bi=bi: e.dma_start(
                out=yt_t[bi], in_=yT_h.rearrange("(c p) t -> p c t", p=128)[:, :, i * 128:(i + 1) * 128]),
                inc="ld", waits=[("t", t_cons[bi])])
            lv2 = p.y(lambda e, i=i: e.dma_start(out=xnl_sb, in_=xn_h[i * 128:(i + 1) * 128, :]),
                      inc="ld", waits=ws)
            v_ev = 0
            for esl in range(2):
                pso = ps_big[:, esl * 512:(esl + 1) * 512]
                for c in range(NC):
                    w = [("ld", lv2), ("v", v_ev)] if c == 0 else []
                    tmm = p.t(lambda e, c=c, esl=esl, bi=bi, pso=pso: e.matmul(
                        pso, yt_t[bi][:, c, :], wo_sb[:, c, esl * 512:(esl + 1) * 512],
                        start=(c == 0), stop=(c == NC - 1)),
                        inc="t" if c == NC - 1 else None, waits=w)
                t_cons[bi] = tmm
                sl = slice(esl * 512, (esl + 1) * 512)
                v_ev = p.v(lambda e, pso=pso, sl=sl, dst=dst: e.tensor_tensor(
                    dst[:, sl], pso, xnl_sb[:, sl], op=ALU.add), inc="v", waits=[("t", tmm)])
            return (lambda e: e.nop()), [("v", v_ev)]

        emit_ln_tokens(src_x1, None, l2g_sb, l2b_sb, True, h2T_sb, src_store_h=x1_h)
        ar.pop()
        p.barrier()
        p.y(lambda e: e.dma_start(out=hT_hs[0].rearrange("(c p) t -> p c t", p=128), in_=h2T_sb),
            inc="st")
        ar.pop()
        p.barrier()

        ar.push()
        w1_sb = ar.alloc(BF16, [128, NC, HID])
        w2_sb = ar.alloc(BF16, [128, NHID, D])
        b1_sb = ar.alloc(F32, [128, NHID])
        b2_sb = ar.alloc(F32, [128, NC])
        TB = 256
        NTB = T // TB
        hblk = ar.alloc(BF16, [128, NC, TB])
        asb = [ar.alloc(BF16, [128, TB]) for _ in range(2)]
        outev = ar.alloc(BF16, [128, NC, TB])
        st_o = 0
        v_oev = 0
        s_aev = [0, 0]
        t_out2 = 0
        for lvl in range(3):
            src_hT = hT_hs[lvl]
            dst_hT = hT_hs[lvl + 1]
            p.g(lambda e, lvl=lvl: e.dma_start(
                out=w1_sb, in_=cw1[lvl].rearrange("(c p) e -> p c e", p=128)), inc="lg",
                waits=[("t", p.now()["t"])])
            p.g(lambda e, lvl=lvl: e.dma_start(
                out=w2_sb, in_=cw2[lvl].rearrange("(c p) e -> p c e", p=128)), inc="lg")
            p.y(lambda e, lvl=lvl: e.dma_start(out=b1_sb, in_=cb1[lvl]), inc="ld")
            p.y(lambda e, lvl=lvl: e.dma_start(out=b2_sb, in_=cb2[lvl]), inc="ld")
            p.barrier()
            for tb in range(NTB):
                tsl = slice(tb * TB, (tb + 1) * TB)
                lv = p.y(lambda e, tsl=tsl, src_hT=src_hT: e.dma_start(
                    out=hblk, in_=src_hT.rearrange("(c p) t -> p c t", p=128)[:, :, tsl]),
                    inc="ld", waits=[("t", p.now()["t"])])
                for ht in range(NHID):
                    hb = ht % 2
                    pa = ps_big[:, 2048 + hb * 512: 2048 + hb * 512 + TB]
                    for c in range(NC):
                        w = []
                        if c == 0:
                            w = [("ld", lv), ("s", s_aev[hb])]
                        tmm = p.t(lambda e, c=c, ht=ht, pa=pa: e.matmul(
                            pa, w1_sb[:, c, ht * 128:(ht + 1) * 128], hblk[:, c, :],
                            start=(c == 0), stop=(c == NC - 1)),
                            inc="t" if c == NC - 1 else None, waits=w)
                    w = [("t", tmm)]
                    if t_out2:
                        w.append(("t", t_out2))
                    s_aev[hb] = p.s(lambda e, ht=ht, pa=pa, hb=hb: e.activation(
                        asb[hb], pa, AF.Gelu_apprx_tanh, bias=b1_sb[:, ht:ht + 1]),
                        inc="s", waits=w)
                    for dt in range(NC):
                        po = ps_big[:, dt * 256:(dt + 1) * 256]
                        w = []
                        if dt == 0:
                            w = [("s", s_aev[hb])]
                            if ht == 0:
                                w.append(("v", v_oev))
                        t_out2 = p.t(lambda e, ht=ht, dt=dt, po=po, hb=hb: e.matmul(
                            po, w2_sb[:, ht, dt * 128:(dt + 1) * 128], asb[hb],
                            start=(ht == 0 and dt % 2 == 0), stop=(ht == NHID - 1)),
                            inc="t" if dt == NC - 1 else None, waits=w)
                for dt in range(NC):
                    po = ps_big[:, dt * 256:(dt + 1) * 256]
                    v_oev = p.v(lambda e, dt=dt, po=po: e.tensor_scalar(
                        outev[:, dt, :], po, b2_sb[:, dt:dt + 1], None, op0=ALU.add),
                        inc="v" if dt == NC - 1 else None,
                        waits=[("t", t_out2), ("st", st_o)] if dt == 0 else [])
                st_o = p.y(lambda e, tsl=tsl, dst_hT=dst_hT: e.dma_start(
                    out=dst_hT.rearrange("(c p) t -> p c t", p=128)[:, :, tsl],
                    in_=outev), inc="st", waits=[("v", v_oev)])
            p.barrier()
        ar.pop()
        p.barrier()

        ar.push()
        wr_sb = ar.alloc(BF16, [128, NC, D])
        wk2_sb = ar.alloc(BF16, [128, NC, D])
        wv2_sb = ar.alloc(BF16, [128, NC, D])
        mur_sb = ar.alloc(F32, [128, D])
        muk_sb = ar.alloc(F32, [128, D])
        omur_sb = ar.alloc(F32, [128, D])
        omuk_sb = ar.alloc(F32, [128, D])
        xn_t = ar.alloc(F32, [128, D])
        xp_t = ar.alloc(F32, [128, D])
        xr_t = ar.alloc(F32, [128, D])
        xk_t = ar.alloc(F32, [128, D])
        tmp_t = ar.alloc(F32, [128, D])
        mixb = ar.alloc(BF16, [128, D])
        mixT = ar.alloc(BF16, [128, D])
        r_f = xr_t
        krel_f = xk_t
        kact = ar.alloc(BF16, [128, D])
        kaT = ar.alloc(BF16, [128, D])
        rk_f = tmp_t
        g_sb = ar.alloc(BF16, [128, D])
        h3t = [ar.alloc(BF16, [128, NC, 128]) for _ in range(2)]
        h3tok = ar.alloc(F32, [128, D])
        s_f = ar.alloc(F32, [128, D])
        s_bf = ar.alloc(BF16, [128, D])
        x_tl = ar.alloc(F32, [128, D])
        sg_f = xn_t
        xo_f = xp_t
        p.g(lambda e: e.dma_start(out=wr_sb, in_=wrT.rearrange("(c p) e -> p c e", p=128)), inc="lg")
        p.g(lambda e: e.dma_start(out=wk2_sb, in_=wkT.rearrange("(c p) e -> p c e", p=128)), inc="lg")
        p.g(lambda e: e.dma_start(out=wv2_sb, in_=wvT.rearrange("(c p) e -> p c e", p=128)), inc="lg")
        p.y(lambda e: e.dma_start(out=mur_sb, in_=mur), inc="ld")
        lvm = p.y(lambda e: e.dma_start(out=muk_sb, in_=muk), inc="ld")
        p.barrier()
        p.v(lambda e: e.tensor_scalar(omur_sb, mur_sb, -1.0, 1.0, op0=ALU.mult, op1=ALU.add),
            waits=[("ld", lvm)])
        p.v(lambda e: e.tensor_scalar(omuk_sb, muk_sb, -1.0, 1.0, op0=ALU.mult, op1=ALU.add))
        p.barrier()
        for ti in range(NT):
            p.barrier()
            p.y(lambda e, ti=ti: e.dma_start(out=xn_t, in_=xn_h[ti * 128:(ti + 1) * 128, :]),
                inc="ld")
            if ti == 0:
                p.v(lambda e: e.memset(xp_t[0:1, :], 0.0))
                p.y(lambda e: e.dma_start(out=xp_t[1:128, :], in_=xn_h[0:127, :]), inc="ld")
            else:
                p.y(lambda e, ti=ti: e.dma_start(
                    out=xp_t, in_=xn_h[ti * 128 - 1:(ti + 1) * 128 - 1, :]), inc="ld")
            p.y(lambda e, ti=ti: e.dma_start(
                out=h3t[ti % 2], in_=hT_hs[3].rearrange("(c p) t -> p c t", p=128)[:, :, ti * 128:(ti + 1) * 128]),
                inc="ld")
            p.y(lambda e, ti=ti: e.dma_start(out=x_tl, in_=x_in[ti * 128:(ti + 1) * 128, :]),
                inc="ld")
            lv = p.y(lambda e, ti=ti: e.dma_start(out=s_f, in_=x1_h[ti * 128:(ti + 1) * 128, :]),
                     inc="ld")
            p.pending["vector"].append(("ld", lv))
            p.pending["tensor"].append(("ld", lv))
            p.v(lambda e: e.tensor_tensor(xr_t, mur_sb, xn_t, op=ALU.mult))
            p.v(lambda e: e.tensor_tensor(tmp_t, omur_sb, xp_t, op=ALU.mult))
            p.v(lambda e: e.tensor_tensor(xr_t, xr_t, tmp_t, op=ALU.add))
            p.v(lambda e: e.tensor_tensor(xk_t, muk_sb, xn_t, op=ALU.mult))
            p.v(lambda e: e.tensor_tensor(tmp_t, omuk_sb, xp_t, op=ALU.mult))
            p.v(lambda e: e.tensor_tensor(xk_t, xk_t, tmp_t, op=ALU.add))
            s_used = 0
            for src, wsb, dst, act in [(xr_t, wr_sb, r_f, AF.Sigmoid),
                                       (xk_t, wk2_sb, krel_f, AF.Relu)]:
                vq = p.v(lambda e, src=src: e.tensor_copy(mixb, src), inc="v")
                for j in range(NC):
                    w = [("v", vq)] if j == 0 else []
                    ttr = p.t(lambda e, j=j: e.transpose(
                        psT[:, j * 128:(j + 1) * 128], mixb[:, j * 128:(j + 1) * 128], ident_sb),
                        inc="t" if j == NC - 1 else None, waits=w)
                vx = p.v(lambda e: e.tensor_copy(mixT, psT[:]), inc="v", waits=[("t", ttr)])
                for esl in range(2):
                    pso = ps_big[:, 1024 + esl * 512:1024 + (esl + 1) * 512]
                    for c in range(NC):
                        w = [("v", vx), ("s", s_used)] if c == 0 else []
                        tmm = p.t(lambda e, c=c, esl=esl, wsb=wsb, pso=pso: e.matmul(
                            pso, mixT[:, c * 128:(c + 1) * 128],
                            wsb[:, c, esl * 512:(esl + 1) * 512],
                            start=(c == 0), stop=(c == NC - 1)),
                            inc="t" if c == NC - 1 else None, waits=w)
                    s_used = p.s(lambda e, esl=esl, dst=dst, act=act, pso=pso: e.activation(
                        dst[:, esl * 512:(esl + 1) * 512], pso, act),
                        inc="s", waits=[("t", tmm)])
            vq = p.v(lambda e: e.tensor_tensor(kact, krel_f, krel_f, op=ALU.mult), inc="v",
                     waits=[("s", s_used)])
            for j in range(NC):
                w = [("v", vq)] if j == 0 else []
                ttr = p.t(lambda e, j=j: e.transpose(
                    psT[:, j * 128:(j + 1) * 128], kact[:, j * 128:(j + 1) * 128], ident_sb),
                    inc="t" if j == NC - 1 else None, waits=w)
            vx = p.v(lambda e: e.tensor_copy(kaT, psT[:]), inc="v", waits=[("t", ttr)])
            v_used = 0
            for esl in range(2):
                pso = ps_big[:, 1024 + esl * 512:1024 + (esl + 1) * 512]
                for c in range(NC):
                    w = [("v", vx), ("s", s_used)] if c == 0 else []
                    tmm = p.t(lambda e, c=c, esl=esl, pso=pso: e.matmul(
                        pso, kaT[:, c * 128:(c + 1) * 128], wv2_sb[:, c, esl * 512:(esl + 1) * 512],
                        start=(c == 0), stop=(c == NC - 1)),
                        inc="t" if c == NC - 1 else None, waits=w)
                v_used = p.v(lambda e, esl=esl, pso=pso: e.tensor_tensor(
                    rk_f[:, esl * 512:(esl + 1) * 512], pso, r_f[:, esl * 512:(esl + 1) * 512],
                    op=ALU.mult), inc="v", waits=[("t", tmm)])
            s_g = p.s(lambda e: e.activation(g_sb, rk_f, AF.Silu), inc="s", waits=[("v", v_used)])
            for j in range(NC):
                ttr = p.t(lambda e, j=j, ti=ti: e.transpose(
                    psT[:, j * 128:(j + 1) * 128], h3t[ti % 2][:, j, :], ident_sb),
                    inc="t" if j == NC - 1 else None)
            p.v(lambda e: e.tensor_copy(h3tok, psT[:]), waits=[("t", ttr)])
            p.v(lambda e: e.tensor_tensor(s_f, s_f, h3tok, op=ALU.add))
            p.v(lambda e: e.tensor_copy(s_bf, s_f))
            p.v(lambda e: e.tensor_tensor(sg_f, s_bf, g_sb, op=ALU.mult), waits=[("s", s_g)])
            v_done = p.v(lambda e: e.tensor_tensor(xo_f, x_tl, sg_f, op=ALU.add), inc="v")
            if DEBUG:
                p.y(lambda e, ti=ti: e.dma_start(out=g3_h[ti * 128:(ti + 1) * 128, :], in_=g_sb),
                    inc="st", waits=[("s", s_g)])
            p.y(lambda e, ti=ti: e.dma_start(out=x_out[ti * 128:(ti + 1) * 128, :], in_=xo_f),
                inc="st", waits=[("v", v_done)])
        ar.pop()
        ar.pop()
        LAST_PEAK.update({str(k): v for k, v in ar.peak.items()})


def _setup_ctx(nc, p, bf_cols=70500, f32_cols=15700):
    stk = ExitStack()
    BFA = stk.enter_context(nc.sbuf_tensor("bf_arena", [128, bf_cols], BF16))
    F32A = stk.enter_context(nc.sbuf_tensor("f32_arena", [128, f32_cols], F32))
    ar = Arena({BF16: BFA, F32: F32A})
    ps_big = stk.enter_context(nc.psum_tensor("ps_big", [128, 3072], F32))
    psT = stk.enter_context(nc.psum_tensor("psT", [128, 1024], BF16))
    ident_in = nc.dram_tensor("ident", [128, 128], BF16, kind="ExternalInput").ap()
    ones64_in = nc.dram_tensor("ones64", [1, 64], BF16, kind="ExternalInput").ap()
    ident_sb = ar.alloc(BF16, [128, 128])
    ones64_sb = ar.alloc(BF16, [1, 64])
    spc = ar.alloc(F32, [128, 128])
    p.v(lambda e: e.memset(spc, 0.0))
    p.spacer = lambda: p.v(lambda e: e.tensor_copy(spc[:, 0:64], spc[:, 64:128]))
    p.y(lambda e: e.dma_start(out=ident_sb, in_=ident_in), inc="ld")
    p.y(lambda e: e.dma_start(out=ones64_sb, in_=ones64_in), inc="ld")
    return stk, ar, ps_big, psT, ident_sb, ones64_sb


def build_megakernel():
    nc = bass.Bass("TRN2", target_bir_lowering=False)
    p = Prog()
    x0 = nc.dram_tensor("x", [T, D], F32, kind="ExternalInput").ap()
    xo = nc.dram_tensor("x_out", [T, D], F32, kind="ExternalOutput").ap()
    xa = nc.dram_tensor("xmid_a", [T, D], F32, kind="Internal").ap()
    xb = nc.dram_tensor("xmid_b", [T, D], F32, kind="Internal").ap()
    stk, ar, ps_big, psT, ident_sb, ones64_sb = _setup_ctx(nc, p)
    with stk:
        chain = [(0, x0, xa), (1, xa, xb), (2, xb, xa), (3, xa, xo)]
        for layer, xi, xout in chain:
            if layer < 3:
                emit_attn_layer(nc, p, ar, ps_big, psT, ident_sb, ones64_sb,
                                layer, f"l{layer}_", xi, xout)
            else:
                emit_l3(nc, p, ar, ps_big, psT, ident_sb, f"l{layer}_", xi, xout)
            p.barrier()
        p.y(lambda e: e.nop(), waits=[("st", p.now()["st"])])
        p.build(nc)
    return nc


def build_attn_layer(layer):
    nc = bass.Bass("TRN2", target_bir_lowering=False)
    p = Prog()
    x0 = nc.dram_tensor("x", [T, D], F32, kind="ExternalInput").ap()
    xo = nc.dram_tensor("x_out", [T, D], F32, kind="ExternalOutput").ap()
    stk, ar, ps_big, psT, ident_sb, ones64_sb = _setup_ctx(nc, p)
    with stk:
        emit_attn_layer(nc, p, ar, ps_big, psT, ident_sb, ones64_sb, layer, "", x0, xo)
        p.y(lambda e: e.nop(), waits=[("st", p.now()["st"])])
        p.build(nc)
    return nc


def build_l3():
    nc = bass.Bass("TRN2", target_bir_lowering=False)
    p = Prog()
    x0 = nc.dram_tensor("x", [T, D], F32, kind="ExternalInput").ap()
    xo = nc.dram_tensor("x_out", [T, D], F32, kind="ExternalOutput").ap()
    stk, ar, ps_big, psT, ident_sb, ones64_sb = _setup_ctx(nc, p)
    with stk:
        emit_l3(nc, p, ar, ps_big, psT, ident_sb, "", x0, xo)
        p.y(lambda e: e.nop(), waits=[("st", p.now()["st"])])
        p.build(nc)
    return nc


_CONST_CACHE = {}


def host_consts():
    if _CONST_CACHE:
        return _CONST_CACHE
    c = {}
    c["ident"] = np.eye(128, dtype=ml_dtypes.bfloat16)
    c["ones64"] = np.ones((1, 64), dtype=ml_dtypes.bfloat16)
    freqs = (np.float32(10000.0) ** (np.arange(0, HD, 2.0, dtype=np.float32) /
                                     np.float32(HD))).astype(np.float32)
    t = np.arange(T, dtype=np.float32)
    ang = (t[:, None] * freqs[None, :]).astype(np.float32)
    cos = np.cos(ang.astype(np.float64)).astype(np.float32)
    sin = np.sin(ang.astype(np.float64)).astype(np.float32)
    A = np.empty((64, T), np.float32)
    B = np.empty((64, T), np.float32)
    for pm in range(64):
        j = pm % 32
        A[pm] = cos[:, j]
        B[pm] = sin[:, j] if pm >= 32 else -sin[:, j]
    c["rotA64"] = A.astype(ml_dtypes.bfloat16)
    c["rotB64"] = B.astype(ml_dtypes.bfloat16)
    cm = np.zeros((4, 128, 512), np.float32)
    for r in range(4):
        for kk in range(128):
            cm[r, kk, kk + r * 128:] = 1.0
    c["cmask"] = cm.astype(ml_dtypes.bfloat16)
    tr = np.zeros((128, 128), np.float32)
    for e in range(128):
        tr[e, e:] = 1.0
    c["trilm"] = tr.astype(ml_dtypes.bfloat16)
    _CONST_CACHE.update(c)
    return c


def rotary_perm_idx():
    idx = np.arange(3 * D)
    for part in range(2):
        for h in range(NH):
            base = part * D + h * HD
            ev = base + np.arange(0, HD, 2)
            od = base + np.arange(1, HD, 2)
            idx[base:base + 32] = ev
            idx[base + 32:base + 64] = od
    return idx


def layer_inputs(layer, ins):
    c = host_consts()
    d = {}
    if layer == 0:
        idx = rotary_perm_idx()
        w = np.ascontiguousarray(ins["L0_attn_qkv_w"][:, idx], dtype=np.float32)
        b = np.ascontiguousarray(np.asarray(ins["L0_attn_qkv_b"], dtype=np.float32)[idx])
        d["w_o"] = np.ascontiguousarray(ins["L0_attn_out_w"], dtype=np.float32)
        bo = np.asarray(ins["L0_attn_out_b"], dtype=np.float32)
        d["rotA64"] = c["rotA64"]
        d["rotB64"] = c["rotB64"]
        pref = "L0"
    elif layer == 1:
        w = np.ascontiguousarray(np.asarray(ins["L1_attn_w_qkv"], dtype=np.float32).T)
        b = np.asarray(ins["L1_attn_b_qkv"], dtype=np.float32)
        d["w_o"] = np.ascontiguousarray(np.asarray(ins["L1_attn_w_o"], dtype=np.float32).T)
        bo = np.asarray(ins["L1_attn_b_o"], dtype=np.float32)
        pref = "L1"
    else:
        w = np.ascontiguousarray(np.asarray(ins["L2_attn_w_in"], dtype=np.float32).T)
        b = np.asarray(ins["L2_attn_b_in"], dtype=np.float32)
        d["w_o"] = np.ascontiguousarray(np.asarray(ins["L2_attn_w_out"], dtype=np.float32).T)
        bo = np.asarray(ins["L2_attn_b_out"], dtype=np.float32)
        d["cmask"] = c["cmask"]
        pref = "L2"
    d["w_qkv"] = w
    d["b_pc"] = np.ascontiguousarray(b.reshape(24, 128).T)
    d["bv_bc"] = np.ascontiguousarray(np.broadcast_to(b[2 * D:], (128, D)))
    d["bo_bc"] = np.ascontiguousarray(np.broadcast_to(bo, (128, D)))
    d["w_gate"] = np.ascontiguousarray(ins[f"{pref}_ffn_w_gate"], dtype=np.float32)
    d["w_up"] = np.ascontiguousarray(ins[f"{pref}_ffn_w_up"], dtype=np.float32)
    d["w_down"] = np.ascontiguousarray(ins[f"{pref}_ffn_w_down"], dtype=np.float32)
    return d


def l3_inputs(ins):
    c = host_consts()
    f = np.float32
    d = {"trilm": c["trilm"]}
    for k, src in [("wq", "L3_attn_wq"), ("wk", "L3_attn_wk"), ("wv", "L3_attn_wv"),
                   ("wo", "L3_attn_wo")]:
        d[k] = np.ascontiguousarray(np.asarray(ins[src], dtype=f))
    for k, src in [("ln1g", "L3_attn_ln1_g"), ("ln1b", "L3_attn_ln1_b"),
                   ("ln2g", "L3_attn_ln2_g"), ("ln2b", "L3_attn_ln2_b")]:
        d[k] = np.ascontiguousarray(np.broadcast_to(np.asarray(ins[src], dtype=f), (128, D)))
    d["cw1"] = np.ascontiguousarray(np.asarray(ins["L3_attn_cms_w1"], dtype=f))
    d["cw2"] = np.ascontiguousarray(np.asarray(ins["L3_attn_cms_w2"], dtype=f))
    b1 = np.asarray(ins["L3_attn_cms_b1"], dtype=f)
    b2 = np.asarray(ins["L3_attn_cms_b2"], dtype=f)
    d["cb1"] = np.ascontiguousarray(np.stack([b1[l].reshape(32, 128).T for l in range(3)]))
    d["cb2"] = np.ascontiguousarray(np.stack([b2[l].reshape(8, 128).T for l in range(3)]))
    for k, src in [("wrT", "L3_ffn_W_r"), ("wkT", "L3_ffn_W_k"), ("wvT", "L3_ffn_W_v")]:
        d[k] = np.ascontiguousarray(np.asarray(ins[src], dtype=f).T)
    d["mur"] = np.ascontiguousarray(np.broadcast_to(np.asarray(ins["L3_ffn_mu_r"], dtype=f), (128, D)))
    d["muk"] = np.ascontiguousarray(np.broadcast_to(np.asarray(ins["L3_ffn_mu_k"], dtype=f), (128, D)))
    return d


def kernel(**ins):
    x = np.ascontiguousarray(np.asarray(ins["L0_attn_x"], dtype=np.float32))
    B = x.shape[0]
    c = host_consts()
    nc = build_megakernel()
    base = {"ident": c["ident"], "ones64": c["ones64"]}
    for layer in range(4):
        lid = layer_inputs(layer, ins) if layer < 3 else l3_inputs(ins)
        for k, v in lid.items():
            base[f"l{layer}_{k}"] = v
    in_maps = [dict(base, x=np.ascontiguousarray(x[b])) for b in range(B)]
    res = run_bass_kernel_spmd(nc, in_maps, core_ids=list(range(B)))
    return np.stack([np.asarray(res.results[b]["x_out"]) for b in range(B)]).astype(np.float32)


# version 05
# speedup vs baseline: 1.5444x; 1.2006x over previous
import numpy as np
import ml_dtypes
import concourse.bass as bass
import concourse.mybir as mybir
from concourse.bass_utils import run_bass_kernel_spmd
from contextlib import ExitStack

F32 = mybir.dt.float32
BF16 = mybir.dt.bfloat16
AF = mybir.ActivationFunctionType
ALU = mybir.AluOpType
AX = mybir.AxisListType.X

T, D, I, NH, HD = 4096, 1024, 2816, 16, 64
NT = T // 128
NC = D // 128
NI = I // 128
MAGIC = 12582912.0
DEBUG = False
STOP_STAGE = 99
DRAIN_S1 = 3
S3_HEADS = NH if False else 16
LAST_PEAK = {}

ENGS = ("tensor", "vector", "scalar", "gpsimd", "sync")
DMA_KEYS = ("ld", "lg", "st")


class Prog:

    def __init__(self):
        self.streams = {e: [] for e in ENGS}
        self.cnt = {k: 0 for k in ("t", "v", "s", "g", "w", "ld", "lg", "st")}
        self.pending = {e: [] for e in ENGS}

    def _emit(self, eng, fn, inc=None, waits=()):
        ws = {}
        for k, val in list(self.pending[eng]) + list(waits):
            if val and val > 0:
                ws[k] = max(ws.get(k, 0), val)
        self.pending[eng] = []
        self.streams[eng].append((fn, inc, sorted(ws.items())))
        if inc is not None:
            self.cnt[inc] += 16 if inc in DMA_KEYS else 1
            return self.cnt[inc]
        return None

    def t(self, fn, inc=None, waits=()):
        return self._emit("tensor", fn, inc, waits)

    def v(self, fn, inc=None, waits=()):
        return self._emit("vector", fn, inc, waits)

    def s(self, fn, inc=None, waits=()):
        return self._emit("scalar", fn, inc, waits)

    def g(self, fn, inc=None, waits=()):
        return self._emit("gpsimd", fn, inc, waits)

    def y(self, fn, inc=None, waits=()):
        return self._emit("sync", fn, inc, waits)

    def now(self):
        return dict(self.cnt)

    def barrier(self, engines=ENGS):
        if hasattr(self, "vflush"):
            self.vflush()
        snap = [(k, v) for k, v in self.cnt.items() if v > 0]
        for e in engines:
            self.pending[e].extend(snap)

    def build(self, nc):
        with ExitStack() as stk:
            sems = {k: stk.enter_context(nc.semaphore(f"sem_{k}")) for k in self.cnt}
            block = stk.enter_context(nc.Block())
            for eng_name in ENGS:
                stream = self.streams[eng_name]
                if not stream:
                    continue

                def make(stream_):
                    def run(eng):
                        for fn, inc, ws in stream_:
                            for k, val in ws:
                                eng.wait_ge(sems[k], val)
                            inst = fn(eng)
                            if inc is not None:
                                inst.then_inc(sems[inc], 16 if inc in DMA_KEYS else 1)
                    return run

                getattr(block, eng_name)(make(stream))


class Arena:
    def __init__(self, tensors):
        self.tensors = tensors
        self.off = {d: 0 for d in tensors}
        self.stack = []
        self.peak = {d: 0 for d in tensors}

    def push(self):
        self.stack.append(dict(self.off))

    def pop(self):
        self.off = self.stack.pop()

    def alloc(self, dtype, shape):
        cols = 1
        for s in shape[1:]:
            cols *= s
        esz = 2 if dtype == BF16 else 4
        o = self.off[dtype]
        o += (-(o * esz)) % 32 // esz
        self.off[dtype] = o + cols
        self.peak[dtype] = max(self.peak[dtype], self.off[dtype])
        tsr = self.tensors[dtype]
        assert self.off[dtype] <= tsr.shape[1], (
            f"arena overflow {dtype}: {self.off[dtype]} > {tsr.shape[1]}")
        ap = tsr[:][:, o:o + cols]
        if len(shape) > 2:
            names = " ".join(f"d{i}" for i in range(len(shape) - 1))
            ap = ap.rearrange(f"p ({names}) -> p {names}",
                              **{f"d{i}": shape[i + 1] for i in range(len(shape) - 1)})
        if shape[0] != 128:
            ap = ap[0:shape[0]]
        return ap


def emit_quant_stats(p, x_ap, BM, SC, K, inc_key="v", waits_first=()):
    NB = K // 16
    BM, SC = BM[:, :NB], SC[:, :NB]
    xv = x_ap.rearrange("p (b s) -> p b s", s=16)
    p.v(lambda e: e.tensor_reduce(BM, xv, axis=AX, op=ALU.max, apply_absolute_value=True),
        waits=waits_first)
    p.v(lambda e: e.tensor_scalar(SC, BM, 1.0 / 6.0, 1e-12, op0=ALU.mult, op1=ALU.max))
    p.spacer()
    return p.v(lambda e: e.reciprocal(BM, SC), inc=inc_key)


def emit_quant_bulk(p, eng_key, x_ap, out_ap, A, B, BM, SC, K, stat_wait,
                    waits_first=(), inc_last=None):
    em = getattr(p, eng_key)
    NB = K // 16
    A, B = A[:, :K], B[:, :K]
    BM, SC = BM[:, :NB], SC[:, :NB]
    xv = x_ap.rearrange("p (b s) -> p b s", s=16)
    bmb = BM.broadcast_to([128, NB, 16])
    scb = SC.broadcast_to([128, NB, 16])
    em(lambda e: e.tensor_tensor(xv, xv, bmb, op=ALU.mult),
       waits=list(waits_first) + [stat_wait])
    em(lambda e: e.tensor_scalar(A, x_ap, 2.0, -2.0, op0=ALU.min, op1=ALU.max))
    em(lambda e: e.tensor_scalar(A, A, 2.0, MAGIC, op0=ALU.mult, op1=ALU.add))
    em(lambda e: e.tensor_scalar(A, A, MAGIC, 0.5, op0=ALU.subtract, op1=ALU.mult))
    em(lambda e: e.tensor_scalar(B, x_ap, 4.0, -4.0, op0=ALU.min, op1=ALU.max))
    em(lambda e: e.tensor_scalar(B, B, MAGIC, MAGIC, op0=ALU.add, op1=ALU.subtract))
    em(lambda e: e.tensor_tensor(A, A, B, op=ALU.add))
    em(lambda e: e.tensor_scalar(B, B, 2.0, -2.0, op0=ALU.min, op1=ALU.max))
    em(lambda e: e.tensor_tensor(A, A, B, op=ALU.subtract))
    em(lambda e: e.tensor_scalar(B, x_ap, 0.5, MAGIC, op0=ALU.mult, op1=ALU.add))
    em(lambda e: e.tensor_scalar(B, B, MAGIC, 2.0, op0=ALU.subtract, op1=ALU.mult))
    em(lambda e: e.tensor_tensor(A, A, B, op=ALU.add))
    em(lambda e: e.tensor_scalar(B, B, 4.0, -4.0, op0=ALU.min, op1=ALU.max))
    em(lambda e: e.tensor_tensor(A, A, B, op=ALU.subtract))
    av = A.rearrange("p (b s) -> p b s", s=16)
    ov = out_ap.rearrange("p (b s) -> p b s", s=16)
    return em(lambda e: e.tensor_tensor(ov, av, scb, op=ALU.mult), inc=inc_last)


def emit_quant(p, eng_key, x_ap, out_ap, tmps, K, waits_first=(), inc_last=None):
    emit_quant_stats(p, x_ap, tmps["BM"], tmps["SC"], K, inc_key="v",
                     waits_first=waits_first)
    if eng_key == "v":
        p.spacer()
        sw = ("v", 0)
    else:
        sw = ("v", p.now()["v"])
    return emit_quant_bulk(p, eng_key, x_ap, out_ap, tmps["A"], tmps["B"],
                           tmps["BM"], tmps["SC"], K, sw,
                           waits_first=(), inc_last=inc_last)


NRING = 4


def plan_wquant(nc, p, pfx, w_list, bufs):
    q = []
    dsts = []
    for w_src, n_rows, kk, dname in w_list:
        dst_h = nc.dram_tensor(pfx + dname, [n_rows, kk], BF16,
                               kind="ExternalOutput" if DEBUG else "Internal").ap()
        dsts.append(dst_h)
        for r in range(n_rows // 128):
            for c0 in range(0, kk, 1024):
                w = min(1024, kk - c0)
                q.append((w_src, dst_h, r, c0, w))
    st = {"i": 0, "lv": [0, 0], "v_done": [0, 0], "st_wq": [0, 0], "q": q}
    bufs["wstate"] = st

    def emit_load(idx):
        if idx >= len(q):
            return
        w_src, dst_h, r, c0, w = q[idx]
        bi = idx % 2
        st["lv"][bi] = p.y(lambda e, w_src=w_src, r=r, c0=c0, w=w, bi=bi: e.dma_start(
            out=bufs["wld"][bi][:, 0:w], in_=w_src[r * 128:(r + 1) * 128, c0:c0 + w]),
            inc="ld", waits=[("v", st["v_done"][bi])])

    st["emit_load"] = emit_load
    emit_load(0)
    emit_load(1)
    return st, dsts


def drain_wchunks(p, bufs, st, n):
    for _ in range(n):
        i = st["i"]
        if i >= len(st["q"]):
            return
        w_src, dst_h, r, c0, w = st["q"][i]
        bi = i % 2
        emit_quant_stats(p, bufs["wld"][bi][:, 0:w], bufs["BMr"][bi], bufs["SCr"][bi], w,
                         inc_key="v", waits_first=[("ld", st["lv"][bi]),
                                                   ("st", st["st_wq"][bi])])
        p.spacer()
        emit_quant_bulk(
            p, "v", bufs["wld"][bi][:, 0:w], bufs["wqt"][:, bi * 1024:bi * 1024 + w],
            bufs["A"], bufs["B"], bufs["BMr"][bi], bufs["SCr"][bi], w, ("v", 0))
        st["v_done"][bi] = p.vflush()
        st["st_wq"][bi] = p.y(lambda e, dst_h=dst_h, r=r, c0=c0, w=w, bi=bi: e.dma_start(
            out=dst_h[r * 128:(r + 1) * 128, c0:c0 + w],
            in_=bufs["wqt"][:, bi * 1024:bi * 1024 + w]),
            inc="st", waits=[("v", st["v_done"][bi])])
        st["i"] = i + 1
        st["emit_load"](i + 2)


def emit_attn_layer(nc, p, ar, ps_big, psT, ident_sb, ones64_sb, layer, pfx, x_in, x_out):
    scale = {0: 0.125, 1: 0.03125, 2: 0.125}[layer]
    causal = layer == 2
    rotary = layer == 0
    add_xn = layer == 1
    dt = lambda name, *a, **k: nc.dram_tensor(pfx + name, *a, **k)

    w_qkv = dt("w_qkv", [D, 3 * D], F32, kind="ExternalInput").ap()
    b_pc = dt("b_pc", [128, 24], F32, kind="ExternalInput").ap()
    bv_bc = dt("bv_bc", [128, D], F32, kind="ExternalInput").ap()
    w_o = dt("w_o", [D, D], F32, kind="ExternalInput").ap()
    bo_bc = dt("bo_bc", [128, D], F32, kind="ExternalInput").ap()
    w_gate = dt("w_gate", [I, D], F32, kind="ExternalInput").ap()
    w_up = dt("w_up", [I, D], F32, kind="ExternalInput").ap()
    w_down = dt("w_down", [D, I], F32, kind="ExternalInput").ap()
    if rotary:
        rotA = dt("rotA64", [64, T], BF16, kind="ExternalInput").ap()
        rotB = dt("rotB64", [64, T], BF16, kind="ExternalInput").ap()
    if causal:
        cmask = dt("cmask", [4, 128, 512], BF16, kind="ExternalInput").ap()

    kind_dbg = "ExternalOutput" if DEBUG else "Internal"
    xn_h = dt("xn_h", [T, D], F32, kind=kind_dbg).ap()
    qT_h = dt("qT_h", [D, T], BF16, kind=kind_dbg).ap()
    kT_h = dt("kT_h", [D, T], BF16, kind=kind_dbg).ap()
    vP_h = dt("vP_h", [T, NH * 65], BF16, kind=kind_dbg).ap()
    hT_h = dt("hT_h", [D, T], BF16, kind=kind_dbg).ap()
    s_h = dt("s_h", [T, D], BF16, kind=kind_dbg).ap()
    hqT_h = dt("hqT_h", [I, T], BF16, kind=kind_dbg).ap()
    g_h = dt("g_h", [T, D], BF16, kind=kind_dbg).ap() if DEBUG else None

    ar.push()
    wbufs = {
        "wld": [ar.alloc(F32, [128, 1024]) for _ in range(2)],
        "wqt": ar.alloc(BF16, [128, 2048]),
        "A": ar.alloc(F32, [128, 1024]),
        "B": ar.alloc(F32, [128, 1024]),
        "BMr": [ar.alloc(F32, [128, 64]) for _ in range(2)],
        "SCr": [ar.alloc(F32, [128, 64]) for _ in range(2)],
    }
    ar.push()
    wq_sb = ar.alloc(BF16, [128, NC, 3 * D])
    lg_wq = p.g(lambda e: e.dma_start(out=wq_sb, in_=w_qkv.rearrange("(c p) e -> p c e", p=128)),
                inc="lg")
    wstate, (wgq_h, wuq_h, wdq_h) = plan_wquant(
        nc, p, pfx, [(w_gate, I, D, "wgq"), (w_up, I, D, "wuq"), (w_down, D, I, "wdq")],
        wbufs)

    ar.push()
    xnT_sb = ar.alloc(BF16, [128, NC, T])
    ar.push()
    xf = [ar.alloc(F32, [128, D]) for _ in range(2)]
    sq = ar.alloc(F32, [128, D])
    xn32 = [ar.alloc(F32, [128, D]) for _ in range(2)]
    xnb = [ar.alloc(BF16, [128, D]) for _ in range(2)]
    st8 = ar.alloc(F32, [128, 8])
    nsum, sqs, negm, ex2, m2, var, lntmp, rstd = (st8[:, i:i + 1] for i in range(8))
    v_consumed = [0, 0]
    tr_done = [0, 0]
    st_xn = [0, 0]
    v_evac_done = 0
    for i in range(NT):
        bi = i % 2
        lv = p.y(lambda e, i=i, bi=bi: e.dma_start(out=xf[bi], in_=x_in[i * 128:(i + 1) * 128, :]),
                 inc="ld", waits=[("v", v_consumed[bi])])
        p.v(lambda e, bi=bi: e.tensor_tensor(sq, xf[bi], xf[bi], op=ALU.mult),
            waits=[("ld", lv)])
        p.v(lambda e, bi=bi: e.tensor_reduce(nsum, xf[bi], axis=AX, op=ALU.add, negate=True))
        p.v(lambda e: e.tensor_reduce(sqs, sq, axis=AX, op=ALU.add))
        p.v(lambda e: e.tensor_scalar_mul(negm, nsum, 1.0 / D))
        p.v(lambda e: e.tensor_scalar(ex2, sqs, 1.0 / D, 1e-5, op0=ALU.mult, op1=ALU.add))
        p.v(lambda e: e.tensor_tensor(m2, negm, negm, op=ALU.mult))
        p.spacer()
        vst = p.v(lambda e: e.tensor_tensor(var, ex2, m2, op=ALU.subtract), inc="v")
        p.s(lambda e: e.activation(lntmp, var, AF.Ln), waits=[("v", vst)])
        sst = p.s(lambda e: e.activation(rstd, lntmp, AF.Exp, scale=-0.5), inc="s")
        p.v(lambda e: e.tensor_tensor(negm, negm, rstd, op=ALU.mult), waits=[("s", sst)])
        p.spacer()
        p.v(lambda e, bi=bi: e.tensor_scalar(xn32[bi], xf[bi], rstd[:, 0:1], negm[:, 0:1],
                                             op0=ALU.mult, op1=ALU.add),
            waits=[("st", st_xn[bi])])
        vdone = p.v(lambda e, bi=bi: e.tensor_copy(xnb[bi], xn32[bi]), inc="v",
                    waits=[("t", tr_done[bi])])
        v_consumed[bi] = vdone
        vf = p.vflush()
        st_xn[bi] = p.y(lambda e, i=i, bi=bi: e.dma_start(
            out=xn_h[i * 128:(i + 1) * 128, :], in_=xn32[bi]),
            inc="st", waits=[("v", vf)])
        for j in range(NC):
            w = [("v", vdone), ("v", v_evac_done)] if j == 0 else []
            last = p.t(lambda e, j=j, bi=bi: e.transpose(psT[:, j * 128:(j + 1) * 128],
                                                         xnb[bi][:, j * 128:(j + 1) * 128],
                                                         ident_sb),
                       inc="t" if j == NC - 1 else None, waits=w)
        tr_done[bi] = last
        dst = xnT_sb[:, :, i * 128:(i + 1) * 128]
        v_evac_done = p.v(lambda e, dst=dst: e.tensor_copy(
            dst, psT[:].rearrange("p (c q) -> p c q", q=128)),
            inc="v", waits=[("t", last)])
        drain_wchunks(p, wbufs, wstate, DRAIN_S1)
    ar.pop()
    p.barrier()

    if STOP_STAGE <= 1:
        return
    ar.push()
    bq_sb = ar.alloc(F32, [128, 24])
    bvb_sb = ar.alloc(F32, [128, D])
    qe_sb = ar.alloc(BF16, [128, T])
    vt_sb = ar.alloc(BF16, [128, NH * 65])
    p.y(lambda e: e.dma_start(out=bq_sb, in_=b_pc), inc="ld")
    lv_b = p.y(lambda e: e.dma_start(out=bvb_sb, in_=bv_bc), inc="ld")
    p.pending["tensor"].append(("lg", lg_wq))
    p.pending["scalar"].append(("ld", lv_b))
    p.pending["vector"].append(("ld", lv_b))
    p.barrier(engines=("tensor",))
    for et in range(16):
        s_evac = [0, 0]
        for nsl in range(8):
            pb = nsl % 2
            psq = ps_big[:, pb * 512:(pb + 1) * 512]
            for c in range(NC):
                w = [("s", s_evac[pb])] if c == 0 else []
                tmm = p.t(lambda e, c=c, et=et, nsl=nsl, psq=psq: e.matmul(
                    psq, wq_sb[:, c, et * 128:(et + 1) * 128],
                    xnT_sb[:, c, nsl * 512:(nsl + 1) * 512],
                    start=(c == 0), stop=(c == NC - 1)),
                    inc="t" if c == NC - 1 else None, waits=w)
            s_evac[pb] = p.s(lambda e, et=et, nsl=nsl, psq=psq: e.activation(
                qe_sb[:, nsl * 512:(nsl + 1) * 512], psq, AF.Identity,
                bias=bq_sb[:, et:et + 1]), inc="s", waits=[("t", tmm)])
        dst_h = qT_h if et < 8 else kT_h
        row = (et % 8) * 128
        stv = p.y(lambda e, dst_h=dst_h, row=row: e.dma_start(
            out=dst_h[row:row + 128, :], in_=qe_sb), inc="st",
            waits=[("s", s_evac[0]), ("s", s_evac[1])])
        p.pending["scalar"].append(("st", stv))
        drain_wchunks(p, wbufs, wstate, 1)
    ovv = vt_sb.rearrange("p (h c) -> p h c", c=65)[:, :, 64:65]
    vmemset = p.v(lambda e: e.memset(ovv, 1.0), inc="v")
    p.pending["vector"].append(("v", vmemset))
    st_v = 0
    v_evac = [0, 0]
    for ti in range(NT):
        for vsl in range(2):
            psv = ps_big[:, vsl * 512:(vsl + 1) * 512]
            for c in range(NC):
                w = [("v", v_evac[vsl]), ("st", st_v)] if c == 0 else []
                tmm = p.t(lambda e, c=c, ti=ti, vsl=vsl, psv=psv: e.matmul(
                    psv, xnT_sb[:, c, ti * 128:(ti + 1) * 128],
                    wq_sb[:, c, 2 * D + vsl * 512:2 * D + (vsl + 1) * 512],
                    start=(c == 0), stop=(c == NC - 1)),
                    inc="t" if c == NC - 1 else None, waits=w)
            dst2 = vt_sb.rearrange("p (h c) -> p h c", c=65)[:, vsl * 8:(vsl + 1) * 8, 0:64]
            psv3 = psv.rearrange("p (h c) -> p h c", c=64)
            bvb3 = bvb_sb[:, vsl * 512:(vsl + 1) * 512].rearrange("p (h c) -> p h c", c=64)
            v_evac[vsl] = p.v(lambda e, psv3=psv3, bvb3=bvb3, dst2=dst2: e.tensor_tensor(
                dst2, psv3, bvb3, op=ALU.add), inc="v", waits=[("t", tmm)])
        vf = p.vflush()
        st_v = p.y(lambda e, ti=ti: e.dma_start(out=vP_h[ti * 128:(ti + 1) * 128, :], in_=vt_sb),
                   inc="st", waits=[("v", vf)])
        if ti % 2 == 0:
            drain_wchunks(p, wbufs, wstate, 1)
    ar.pop()
    ar.pop()
    ar.pop()
    p.barrier()

    if STOP_STAGE <= 2:
        return
    ar.push()
    qh_sb = [ar.alloc(BF16, [64, T]) for _ in range(2)]
    kh_sb = [ar.alloc(BF16, [64, T]) for _ in range(2)]
    vh_sb = [ar.alloc(BF16, [128, NT, 65]) for _ in range(2)]
    if rotary:
        qw_sb = ar.alloc(BF16, [64, T])
        rt1 = ar.alloc(BF16, [64, T])
        rA_sb = ar.alloc(BF16, [64, T])
        rB_sb = ar.alloc(BF16, [64, T])
        p.y(lambda e: e.dma_start(out=rA_sb, in_=rotA), inc="ld")
        p.y(lambda e: e.dma_start(out=rB_sb, in_=rotB), inc="ld")
    hT_sb = ar.alloc(BF16, [128, T])
    u_sb = [ar.alloc(BF16, [128, 1024]) for _ in range(2)]
    numsb = ar.alloc(F32, [64, 512])
    rd_sb = ar.alloc(F32, [1, 512])
    rdb_sb = ar.alloc(BF16, [1, 512])
    if causal:
        cm_sb = ar.alloc(BF16, [128, 4, 512])
        p.y(lambda e: e.dma_start(out=cm_sb, in_=cmask.rearrange("r p q -> p r q")), inc="ld")
    ps_sc = [ps_big[:, i * 512:(i + 1) * 512] for i in range(4)]
    ps_num = ps_big[:, 2048:2560]
    ps_bc = ps_big[:, 2560:3072]
    st_hT = 0
    buf_t_use = [0, 0]
    buf_v_use = [0, 0]
    ld_head = [0, 0]

    def emit_head_loads(h):
        hb = h % 2
        w = [("t", buf_t_use[hb]), ("v", buf_v_use[hb])]
        p.y(lambda e, h=h, hb=hb: e.dma_start(
            out=qh_sb[hb], in_=qT_h[h * 64:(h + 1) * 64, :]), inc="ld", waits=w)
        p.y(lambda e, h=h, hb=hb: e.dma_start(
            out=kh_sb[hb], in_=kT_h[h * 64:(h + 1) * 64, :]), inc="ld")
        lv = p.y(lambda e, h=h, hb=hb: e.dma_start(
            out=vh_sb[hb],
            in_=vP_h.rearrange("(t p) c -> p t c", p=128)[:, :, h * 65:(h + 1) * 65]),
            inc="ld")
        ld_head[hb] = lv
        return lv

    v_rot = 0
    emit_head_loads(0)
    for h in range(S3_HEADS):
        hb = h % 2
        lq = ld_head[hb]
        if rotary:
            for src_h, dst_sb in [(qT_h, qh_sb[hb]), (kT_h, kh_sb[hb])]:
                p.y(lambda e, h=h, src_h=src_h: e.dma_start(
                    out=qw_sb[0:32, :], in_=src_h[h * 64 + 32:(h + 1) * 64, :]),
                    inc="ld", waits=[("v", v_rot)])
                lsw = p.y(lambda e, h=h, src_h=src_h: e.dma_start(
                    out=qw_sb[32:64, :], in_=src_h[h * 64:h * 64 + 32, :]), inc="ld")
                p.v(lambda e, dst_sb=dst_sb: e.tensor_tensor(rt1, rA_sb, dst_sb, op=ALU.mult),
                    waits=[("ld", lsw), ("ld", lq)])
                p.v(lambda e: e.tensor_tensor(qw_sb, rB_sb, qw_sb, op=ALU.mult))
                v_rot = p.v(lambda e, dst_sb=dst_sb: e.tensor_tensor(
                    dst_sb, rt1, qw_sb, op=ALU.add), inc="v")
        v_norm = 0
        s_u = [0, 0]
        v_u = [0, 0]
        t_av_pair = [0, 0]
        for qb in range(8):
            kt_list = list(range(NT)) if not causal else list(range(min(NT, (qb + 1) * 4)))
            kt_first, kt_last = kt_list[0], kt_list[-1]
            pairs = [kt_list[i:i + 2] for i in range(0, len(kt_list), 2)]
            t_av = 0
            for pi, pr in enumerate(pairs):
                pb = pi % 2
                t_sc = 0
                for kj, kt in enumerate(pr):
                    bank = (2 * pi + kj) % 4
                    w = []
                    if kj == 0:
                        w = [("s", s_u[pb])]
                        if qb == 0 and pi < 2:
                            w.append(("ld", lq))
                        if rotary and qb == 0:
                            w.append(("v", v_rot))
                    t_sc = p.t(lambda e, kt=kt, qb=qb, bank=bank, hb=hb: e.matmul(
                        ps_sc[bank], kh_sb[hb][:, kt * 128:(kt + 1) * 128],
                        qh_sb[hb][:, qb * 512:(qb + 1) * 512], start=True, stop=True),
                        inc="t", waits=w)
                b0 = (2 * pi) % 4
                npair = len(pr)
                w = [("t", t_sc), ("t", t_av_pair[pb])]
                s_u[pb] = p.s(lambda e, b0=b0, pb=pb, npair=npair: e.activation(
                    u_sb[pb][:, 0:npair * 512], ps_big[:, b0 * 512:(b0 + npair) * 512],
                    AF.Exp, scale=scale), inc="s", waits=w)
                dep = ("s", s_u[pb])
                if causal:
                    for kj, kt in enumerate(pr):
                        if kt * 128 >= qb * 512:
                            r = (kt * 128 - qb * 512) // 128
                            v_u[pb] = p.v(lambda e, pb=pb, r=r, kj=kj: e.tensor_tensor(
                                u_sb[pb][:, kj * 512:(kj + 1) * 512], cm_sb[:, r, :],
                                u_sb[pb][:, kj * 512:(kj + 1) * 512], op=ALU.mult),
                                inc="v", waits=[("s", s_u[pb])])
                            dep = ("v", v_u[pb])
                for kj, kt in enumerate(pr):
                    w = [dep]
                    if kt == kt_first and v_norm:
                        w.append(("v", v_norm))
                    t_av = p.t(lambda e, kt=kt, pb=pb, kj=kj, hb=hb,
                               st=(kt == kt_first), sp=(kt == kt_last): e.matmul(
                        ps_num[0:65, :], vh_sb[hb][:, kt, :],
                        u_sb[pb][:, kj * 512:(kj + 1) * 512], start=st, stop=sp),
                        inc="t", waits=w)
                t_av_pair[pb] = t_av
            p.v(lambda e: e.reciprocal(rd_sb, ps_num[64:65, :]), waits=[("t", t_av)])
            p.v(lambda e: e.tensor_copy(rdb_sb, rd_sb))
            vv = p.v(lambda e: e.tensor_copy(numsb, ps_num[0:64, :]), inc="v")
            tbc = p.t(lambda e: e.matmul(ps_bc[0:64, :], ones64_sb, rdb_sb,
                                         start=True, stop=True), inc="t",
                      waits=[("v", vv)])
            w = [("t", tbc), ("st", st_hT)]
            v_norm = p.v(lambda e, qb=qb: e.tensor_tensor(
                hT_sb[0:64, qb * 512:(qb + 1) * 512], numsb, ps_bc[0:64, :], op=ALU.mult),
                inc="v", waits=w)
        buf_t_use[hb] = p.now()["t"]
        buf_v_use[hb] = v_rot
        if h + 1 < S3_HEADS:
            emit_head_loads(h + 1)
        vf = p.vflush()
        st_hT = p.y(lambda e, h=h: e.dma_start(out=hT_h[h * 64:(h + 1) * 64, :],
                                               in_=hT_sb[0:64]),
                    inc="st", waits=[("v", vf)])
        drain_wchunks(p, wbufs, wstate, 2)
    ar.pop()
    p.barrier()

    if STOP_STAGE <= 3:
        return
    ar.push()
    wo_f = ar.alloc(F32, [128, NC * D // 4])
    wo_sb = ar.alloc(BF16, [128, NC, D])
    bob_sb = ar.alloc(F32, [128, D])
    hTt_sb = [ar.alloc(BF16, [128, NC, 128]) for _ in range(2)]
    s_sb = [ar.alloc(BF16, [128, D]) for _ in range(2)]
    xn_f = [ar.alloc(F32, [128, D]) for _ in range(2)]
    t14 = ar.alloc(F32, [128, 512])
    v_cast = 0
    wo_f3 = wo_f.rearrange("p (c e) -> p c e", e=D)
    for q4 in range(4):
        lw = p.y(lambda e, q4=q4: e.dma_start(
            out=wo_f3, in_=w_o.rearrange("(c p) e -> p c e", p=128)[:, q4 * 2:(q4 + 1) * 2, :]),
            inc="ld", waits=[("v", v_cast)])
        v_cast = p.v(lambda e, q4=q4: e.tensor_copy(
            wo_sb[:, q4 * 2:(q4 + 1) * 2, :], wo_f3),
            inc="v", waits=[("ld", lw)])
    p.y(lambda e: e.dma_start(out=bob_sb, in_=bo_bc), inc="ld")
    p.barrier()
    drain_wchunks(p, wbufs, wstate, 99)
    t_cons = [0, 0]
    st_s = [0, 0]
    v_ev_prev = [0, 0]
    for ti in range(NT):
        bi = ti % 2
        lv = p.y(lambda e, ti=ti, bi=bi: e.dma_start(
            out=hTt_sb[bi],
            in_=hT_h.rearrange("(c p) t -> p c t", p=128)[:, :, ti * 128:(ti + 1) * 128]),
            inc="ld", waits=[("t", t_cons[bi])])
        if add_xn:
            lv = p.y(lambda e, ti=ti, bi=bi: e.dma_start(out=xn_f[bi],
                                                         in_=xn_h[ti * 128:(ti + 1) * 128, :]),
                     inc="ld", waits=[("v", v_ev_prev[bi])])
        v_ev = 0
        for esl in range(2):
            pso = ps_big[:, esl * 512:(esl + 1) * 512]
            for c in range(NC):
                w = [("ld", lv), ("st", st_s[bi]), ("v", v_ev_prev[bi])] if c == 0 else []
                tmm = p.t(lambda e, c=c, esl=esl, bi=bi, pso=pso: e.matmul(
                    pso, hTt_sb[bi][:, c, :], wo_sb[:, c, esl * 512:(esl + 1) * 512],
                    start=(c == 0), stop=(c == NC - 1)),
                    inc="t" if c == NC - 1 else None, waits=w)
            t_cons[bi] = tmm
            sl = slice(esl * 512, (esl + 1) * 512)
            if add_xn:
                p.v(lambda e, pso=pso, sl=sl: e.tensor_tensor(t14, pso, bob_sb[:, sl],
                                                              op=ALU.add),
                    waits=[("t", tmm)])
                v_ev = p.v(lambda e, sl=sl, bi=bi: e.tensor_tensor(
                    s_sb[bi][:, sl], t14, xn_f[bi][:, sl], op=ALU.add), inc="v")
            else:
                v_ev = p.v(lambda e, pso=pso, sl=sl, bi=bi: e.tensor_tensor(
                    s_sb[bi][:, sl], pso, bob_sb[:, sl], op=ALU.add), inc="v",
                    waits=[("t", tmm)])
        v_ev_prev[bi] = v_ev
        vf = p.vflush()
        st_s[bi] = p.y(lambda e, ti=ti, bi=bi: e.dma_start(
            out=s_h[ti * 128:(ti + 1) * 128, :], in_=s_sb[bi]),
            inc="st", waits=[("v", vf)])
    ar.pop()
    p.barrier()

    if STOP_STAGE <= 4:
        return
    ar.push()
    wgT_sb = ar.alloc(BF16, [128, NC, I])
    wuT_sb = ar.alloc(BF16, [128, NC, I])
    aq = {"A": wbufs["A"], "B": wbufs["B"], "BM": wbufs["BMr"][0], "SC": wbufs["SCr"][0]}
    ld_w = 0
    for wsrc, wdst in [(wgq_h, wgT_sb), (wuq_h, wuT_sb)]:
        for c in range(NC):
            ld_w = p.y(lambda e, wsrc=wsrc, wdst=wdst, c=c: e.dma_start(
                out=wdst[:, c, :], in_=wsrc[:, c * 128:(c + 1) * 128], transpose=True),
                inc="ld", waits=[("st", max(wstate["st_wq"]))] if c == 0 else [])
    A2 = ar.alloc(F32, [128, I])
    B2 = ar.alloc(F32, [128, I])
    BM2 = ar.alloc(F32, [128, I // 16])
    SC2 = ar.alloc(F32, [128, I // 16])
    aqh = {"A": A2, "B": B2, "BM": BM2, "SC": SC2}
    xnq_f = [ar.alloc(F32, [128, D]) for _ in range(2)]
    xq_sb = ar.alloc(BF16, [128, D])
    xqT_sb = [ar.alloc(BF16, [128, D]) for _ in range(2)]
    h_f = ar.alloc(F32, [128, I])
    hq_sb = ar.alloc(BF16, [128, I])
    hqT_sb = [ar.alloc(BF16, [128, NI, 128]) for _ in range(2)]
    GSL = [(i * 512, min(I, (i + 1) * 512)) for i in range(6)]
    v_xq_used = [0, 0]
    v_hq_prev = 0
    t_xq_tr = 0
    v_h_prev = 0
    s_h_prev = 0
    vq2_prev = 0
    t_hq_tr_prev = 0
    st_hq = [0, 0]
    for ti in range(NT):
        bi = ti % 2
        lv = p.y(lambda e, ti=ti, bi=bi: e.dma_start(out=xnq_f[bi],
                                                     in_=xn_h[ti * 128:(ti + 1) * 128, :]),
                 inc="ld", waits=[("v", v_xq_used[bi])])
        vq = emit_quant(p, "v", xnq_f[bi], xq_sb, aq, D,
                        waits_first=[("ld", lv), ("v", vq2_prev), ("t", t_xq_tr)],
                        inc_last="v")
        v_xq_used[bi] = vq
        for j in range(NC):
            w = [("v", vq), ("v", v_hq_prev)] if j == 0 else []
            ttr = p.t(lambda e, j=j: e.transpose(psT[:, j * 128:(j + 1) * 128],
                                                 xq_sb[:, j * 128:(j + 1) * 128], ident_sb),
                      inc="t" if j == NC - 1 else None,
                      waits=w + ([("ld", ld_w)] if ti == 0 and j == 0 else []))
        t_xq_tr = ttr
        vxq = p.v(lambda e, bi=bi: e.tensor_copy(xqT_sb[bi], psT[:]), inc="v",
                  waits=[("t", ttr)])
        s_ev = 0
        v_h = 0
        for phase in range(2):
            wsb = wgT_sb if phase == 0 else wuT_sb
            for c in range(NC):
                for isl, (a, b) in enumerate(GSL):
                    w = []
                    if c == 0 and isl == 0:
                        w = [("v", vxq)] if phase == 1 else [("v", vxq), ("v", v_h_prev)]
                        if phase == 1:
                            w.append(("s", s_ev))
                    tmm = p.t(lambda e, c=c, a=a, b=b, wsb=wsb, bi=bi: e.matmul(
                        ps_big[:, a:b], xqT_sb[bi][:, c * 128:(c + 1) * 128],
                        wsb[:, c, a:b], start=(c == 0), stop=(c == NC - 1)),
                        inc="t" if (c == NC - 1 and isl == len(GSL) - 1) else None, waits=w)
            if phase == 0:
                for isl, (a, b) in enumerate(GSL):
                    s_ev = p.s(lambda e, a=a, b=b: e.activation(
                        h_f[:, a:b], ps_big[:, a:b], AF.Silu),
                        inc="s" if isl == len(GSL) - 1 else None,
                        waits=[("t", tmm), ("v", vq2_prev)] if isl == 0 else [])
                s_h_prev = s_ev
            else:
                for isl, (a, b) in enumerate(GSL):
                    v_h = p.v(lambda e, a=a, b=b: e.tensor_tensor(
                        h_f[:, a:b], ps_big[:, a:b], h_f[:, a:b], op=ALU.mult),
                        inc="v" if isl == len(GSL) - 1 else None,
                        waits=[("t", tmm), ("s", s_ev)] if isl == 0 else [])
                v_h_prev = v_h
        vq2 = emit_quant(p, "v", h_f, hq_sb, aqh, I,
                         waits_first=[("v", v_h), ("t", t_hq_tr_prev)], inc_last="v")
        vq2_prev = vq2
        v_hq = 0
        for jb in range(0, NI, 8):
            jn = min(8, NI - jb)
            for j in range(jn):
                w = []
                if jb == 0 and j == 0:
                    w = [("v", vq2), ("v", vxq)]
                elif j == 0:
                    w = [("v", v_hq)]
                ttr2 = p.t(lambda e, j=j, jb=jb: e.transpose(
                    psT[:, j * 128:(j + 1) * 128],
                    hq_sb[:, (jb + j) * 128:(jb + j + 1) * 128], ident_sb),
                    inc="t" if j == jn - 1 else None, waits=w)
            dst = hqT_sb[bi][:, jb:jb + jn, :]
            src = psT[:].rearrange("p (c q) -> p c q", q=128)[:, 0:jn, :]
            v_hq = p.v(lambda e, dst=dst, src=src: e.tensor_copy(dst, src),
                       inc="v", waits=[("t", ttr2), ("st", st_hq[bi])] if jb == 0 else
                       [("t", ttr2)])
        t_hq_tr_prev = ttr2
        v_hq_prev = v_hq
        vf = p.vflush()
        st_hq[bi] = p.y(lambda e, ti=ti, bi=bi: e.dma_start(
            out=hqT_h.rearrange("(c p) t -> p c t", p=128)[:, :, ti * 128:(ti + 1) * 128],
            in_=hqT_sb[bi]), inc="st", waits=[("v", vf)])
    ar.pop()
    p.barrier()

    if STOP_STAGE <= 5:
        return
    ar.push()
    wdT_sb = ar.alloc(BF16, [128, NI, D])
    ld_w = 0
    for c in range(NI):
        ld_w = p.y(lambda e, c=c: e.dma_start(
            out=wdT_sb[:, c, :], in_=wdq_h[:, c * 128:(c + 1) * 128], transpose=True),
            inc="ld")
    hqt_t = [ar.alloc(BF16, [128, NI, 128]) for _ in range(2)]
    g_sb = [ar.alloc(BF16, [128, D]) for _ in range(2)]
    s_tl = [ar.alloc(BF16, [128, D]) for _ in range(2)]
    x_tl = [ar.alloc(F32, [128, D]) for _ in range(2)]
    sg_f = ar.alloc(F32, [128, D])
    xo_f = [ar.alloc(F32, [128, D]) for _ in range(2)]
    t_cons = [0, 0]
    v_done = [0, 0]
    s_gd = [0, 0]
    st_x = [0, 0]
    for ti in range(NT):
        bi = ti % 2
        lv = p.y(lambda e, ti=ti, bi=bi: e.dma_start(
            out=hqt_t[bi],
            in_=hqT_h.rearrange("(c p) t -> p c t", p=128)[:, :, ti * 128:(ti + 1) * 128]),
            inc="ld", waits=[("t", t_cons[bi])])
        p.y(lambda e, ti=ti, bi=bi: e.dma_start(out=s_tl[bi],
                                                in_=s_h[ti * 128:(ti + 1) * 128, :]),
            inc="ld", waits=[("v", v_done[bi])])
        lv = p.y(lambda e, ti=ti, bi=bi: e.dma_start(out=x_tl[bi],
                                                     in_=x_in[ti * 128:(ti + 1) * 128, :]),
                 inc="ld")
        t_dn = 0
        for dsl in range(2):
            psd = ps_big[:, dsl * 512:(dsl + 1) * 512]
            for c in range(NI):
                w = []
                if c == 0:
                    w = [("ld", lv), ("s", s_gd[dsl])]
                    if ti == 0 and dsl == 0:
                        w.append(("ld", ld_w))
                t_dn = p.t(lambda e, c=c, dsl=dsl, psd=psd, bi=bi: e.matmul(
                    psd, hqt_t[bi][:, c, :], wdT_sb[:, c, dsl * 512:(dsl + 1) * 512],
                    start=(c == 0), stop=(c == NI - 1)),
                    inc="t" if c == NI - 1 else None, waits=w)
            t_cons[bi] = t_dn
            s_gd[dsl] = p.s(lambda e, dsl=dsl, psd=psd, bi=bi: e.activation(
                g_sb[bi][:, dsl * 512:(dsl + 1) * 512], psd, AF.Silu),
                inc="s", waits=[("t", t_dn), ("v", v_done[bi])] if dsl == 0 else [("t", t_dn)])
        if DEBUG:
            p.y(lambda e, ti=ti, bi=bi: e.dma_start(out=g_h[ti * 128:(ti + 1) * 128, :],
                                                    in_=g_sb[bi]),
                inc="st", waits=[("s", s_gd[1])])
        p.v(lambda e, bi=bi: e.tensor_tensor(sg_f, s_tl[bi], g_sb[bi], op=ALU.mult),
            waits=[("s", s_gd[1]), ("st", st_x[bi])])
        v_done[bi] = p.v(lambda e, bi=bi: e.tensor_tensor(xo_f[bi], x_tl[bi], sg_f, op=ALU.add),
                         inc="v")
        vf = p.vflush()
        st_x[bi] = p.y(lambda e, ti=ti, bi=bi: e.dma_start(
            out=x_out[ti * 128:(ti + 1) * 128, :], in_=xo_f[bi]),
            inc="st", waits=[("v", vf)])
    ar.pop()
    ar.pop()
    LAST_PEAK.update({str(k): v for k, v in ar.peak.items()})


def emit_l3(nc, p, ar, ps_big, psT, ident_sb, pfx, x_in, x_out):
    HID = 4 * D
    NHID = HID // 128
    dt = lambda name, *a, **k: nc.dram_tensor(pfx + name, *a, **k)
    wq = dt("wq", [D, D], F32, kind="ExternalInput").ap()
    wk = dt("wk", [D, D], F32, kind="ExternalInput").ap()
    wv = dt("wv", [D, D], F32, kind="ExternalInput").ap()
    wo = dt("wo", [D, D], F32, kind="ExternalInput").ap()
    ln1g = dt("ln1g", [128, D], F32, kind="ExternalInput").ap()
    ln1b = dt("ln1b", [128, D], F32, kind="ExternalInput").ap()
    ln2g = dt("ln2g", [128, D], F32, kind="ExternalInput").ap()
    ln2b = dt("ln2b", [128, D], F32, kind="ExternalInput").ap()
    cw1 = dt("cw1", [3, D, HID], F32, kind="ExternalInput").ap()
    cb1 = dt("cb1", [3, 128, NHID], F32, kind="ExternalInput").ap()
    cw2 = dt("cw2", [3, HID, D], F32, kind="ExternalInput").ap()
    cb2 = dt("cb2", [3, 128, NC], F32, kind="ExternalInput").ap()
    wrT = dt("wrT", [D, D], F32, kind="ExternalInput").ap()
    wkT = dt("wkT", [D, D], F32, kind="ExternalInput").ap()
    wvT = dt("wvT", [D, D], F32, kind="ExternalInput").ap()
    mur = dt("mur", [128, D], F32, kind="ExternalInput").ap()
    muk = dt("muk", [128, D], F32, kind="ExternalInput").ap()
    trilm = dt("trilm", [128, 128], BF16, kind="ExternalInput").ap()

    kind_dbg = "ExternalOutput" if DEBUG else "Internal"
    xn_h = dt("xn_h", [T, D], F32, kind=kind_dbg).ap()
    qT_h = dt("qT_h", [D, T], BF16, kind=kind_dbg).ap()
    kT_h = dt("kT_h", [D, T], BF16, kind=kind_dbg).ap()
    ktok_h = dt("ktok_h", [T, D], BF16, kind=kind_dbg).ap()
    vtok_h = dt("vtok_h", [T, D], BF16, kind=kind_dbg).ap()
    yT_h = dt("yT_h", [D, T], BF16, kind=kind_dbg).ap()
    x1_h = dt("x1_h", [T, D], F32, kind=kind_dbg).ap()
    hT_hs = [dt(f"h{i}T_h", [D, T], BF16, kind=kind_dbg).ap() for i in range(4)]
    g3_h = dt("g3_h", [T, D], BF16, kind=kind_dbg).ap() if DEBUG else None

    if True:
        ar.push()
        tril_sb = ar.alloc(BF16, [128, 128])
        p.y(lambda e: e.dma_start(out=tril_sb, in_=trilm), inc="ld")

        def emit_ln_tokens(src_getter, dst_h, g_bc, b_bc, make_T, hT_dst, src_store_h=None):
            xfb = [ar.alloc(F32, [128, D]) for _ in range(2)]
            sqb = ar.alloc(F32, [128, D])
            on32 = [ar.alloc(F32, [128, D]) for _ in range(2)]
            onb = [ar.alloc(BF16, [128, D]) for _ in range(2)]
            st8 = ar.alloc(F32, [128, 8])
            nsum, sqs, negm, ex2, m2, var, lntmp, rstd = (st8[:, i:i + 1] for i in range(8))
            v_consumed = [0, 0]
            tr_done = [0, 0]
            st_n = [0, 0]
            st_s = [0, 0]
            v_evac_done = 0
            for i in range(NT):
                bi = i % 2
                fn, ws = src_getter(i, xfb[bi], [("v", v_consumed[bi]), ("st", st_s[bi])])
                lv = p.y(fn, inc="ld", waits=ws)
                p.v(lambda e, bi=bi: e.tensor_tensor(sqb, xfb[bi], xfb[bi], op=ALU.mult),
                    waits=[("ld", lv)])
                p.v(lambda e, bi=bi: e.tensor_reduce(nsum, xfb[bi], axis=AX, op=ALU.add,
                                                     negate=True))
                p.v(lambda e: e.tensor_reduce(sqs, sqb, axis=AX, op=ALU.add))
                p.v(lambda e: e.tensor_scalar_mul(negm, nsum, 1.0 / D))
                p.v(lambda e: e.tensor_scalar(ex2, sqs, 1.0 / D, 1e-5, op0=ALU.mult, op1=ALU.add))
                p.v(lambda e: e.tensor_tensor(m2, negm, negm, op=ALU.mult))
                p.spacer()
                vst = p.v(lambda e: e.tensor_tensor(var, ex2, m2, op=ALU.subtract), inc="v")
                p.s(lambda e: e.activation(lntmp, var, AF.Ln), waits=[("v", vst)])
                sst = p.s(lambda e: e.activation(rstd, lntmp, AF.Exp, scale=-0.5), inc="s")
                p.v(lambda e: e.tensor_tensor(negm, negm, rstd, op=ALU.mult), waits=[("s", sst)])
                p.spacer()
                p.v(lambda e, bi=bi: e.tensor_scalar(on32[bi], xfb[bi], rstd[:, 0:1], negm[:, 0:1],
                                                     op0=ALU.mult, op1=ALU.add),
                    waits=[("st", st_n[bi])])
                if g_bc is not None:
                    p.v(lambda e, bi=bi: e.tensor_tensor(on32[bi], on32[bi], g_bc, op=ALU.mult))
                    p.v(lambda e, bi=bi: e.tensor_tensor(on32[bi], on32[bi], b_bc, op=ALU.add))
                vdone = p.v(lambda e, bi=bi: e.tensor_copy(onb[bi], on32[bi]), inc="v",
                            waits=[("t", tr_done[bi])])
                v_consumed[bi] = vdone
                if src_store_h is not None or dst_h is not None:
                    vf = p.vflush()
                if src_store_h is not None:
                    st_s[bi] = p.y(lambda e, i=i, bi=bi: e.dma_start(
                        out=src_store_h[i * 128:(i + 1) * 128, :], in_=xfb[bi]),
                        inc="st", waits=[("v", vf)])
                if dst_h is not None:
                    st_n[bi] = p.y(lambda e, i=i, bi=bi: e.dma_start(
                        out=dst_h[i * 128:(i + 1) * 128, :], in_=on32[bi]),
                        inc="st", waits=[("v", vf)])
                if make_T:
                    for j in range(NC):
                        w = [("v", vdone), ("v", v_evac_done)] if j == 0 else []
                        last = p.t(lambda e, j=j, bi=bi: e.transpose(
                            psT[:, j * 128:(j + 1) * 128], onb[bi][:, j * 128:(j + 1) * 128],
                            ident_sb), inc="t" if j == NC - 1 else None, waits=w)
                    tr_done[bi] = last
                    dst = hT_dst[:, :, i * 128:(i + 1) * 128]
                    v_evac_done = p.v(lambda e, dst=dst: e.tensor_copy(
                        dst, psT[:].rearrange("p (c q) -> p c q", q=128)),
                        inc="v", waits=[("t", last)])

        ar.push()
        emit_ln_tokens(lambda i, dst, ws: ((lambda e, i=i, dst=dst: e.dma_start(
            out=dst, in_=x_in[i * 128:(i + 1) * 128, :])), ws), xn_h, None, None, False, None)
        ar.pop()
        p.barrier()

        ar.push()
        h1T_sb = ar.alloc(BF16, [128, NC, T])
        ar.push()
        l1g_sb = ar.alloc(F32, [128, D])
        l1b_sb = ar.alloc(F32, [128, D])
        p.y(lambda e: e.dma_start(out=l1g_sb, in_=ln1g), inc="ld")
        lvg = p.y(lambda e: e.dma_start(out=l1b_sb, in_=ln1b), inc="ld")
        p.pending["vector"].append(("ld", lvg))
        emit_ln_tokens(lambda i, dst, ws: ((lambda e, i=i, dst=dst: e.dma_start(
            out=dst, in_=xn_h[i * 128:(i + 1) * 128, :])), ws), None, l1g_sb, l1b_sb,
            True, h1T_sb)
        ar.pop()
        p.barrier()

        ar.push()
        wp_sb = ar.alloc(BF16, [128, NC, 3 * D])
        qe_sb = ar.alloc(BF16, [128, T])
        kv_sb = ar.alloc(BF16, [128, D])
        p.g(lambda e: e.dma_start(out=wp_sb[:, :, 0:D], in_=wq.rearrange("(c p) e -> p c e", p=128)),
            inc="lg")
        p.g(lambda e: e.dma_start(out=wp_sb[:, :, D:2 * D],
                                  in_=wk.rearrange("(c p) e -> p c e", p=128)), inc="lg")
        p.g(lambda e: e.dma_start(out=wp_sb[:, :, 2 * D:3 * D],
                                  in_=wv.rearrange("(c p) e -> p c e", p=128)), inc="lg")
        p.barrier()
        for et in range(16):
            off = (et % 8) * 128 + (et // 8) * D
            s_evac = [0, 0]
            for nsl in range(8):
                pb = nsl % 2
                psq = ps_big[:, pb * 512:(pb + 1) * 512]
                for c in range(NC):
                    w = [("s", s_evac[pb])] if c == 0 else []
                    tmm = p.t(lambda e, c=c, off=off, nsl=nsl, psq=psq: e.matmul(
                        psq, wp_sb[:, c, off:off + 128],
                        h1T_sb[:, c, nsl * 512:(nsl + 1) * 512],
                        start=(c == 0), stop=(c == NC - 1)),
                        inc="t" if c == NC - 1 else None, waits=w)
                s_evac[pb] = p.s(lambda e, nsl=nsl, psq=psq: e.activation(
                    qe_sb[:, nsl * 512:(nsl + 1) * 512], psq, AF.Identity),
                    inc="s", waits=[("t", tmm)])
            dst_h = qT_h if et < 8 else kT_h
            row = (et % 8) * 128
            stv = p.y(lambda e, dst_h=dst_h, row=row: e.dma_start(
                out=dst_h[row:row + 128, :], in_=qe_sb), inc="st",
                waits=[("s", s_evac[0]), ("s", s_evac[1])])
            p.pending["scalar"].append(("st", stv))
        st_v = 0
        v_evac = [0, 0]
        for part, dst_h in [(1, ktok_h), (2, vtok_h)]:
            for ti in range(NT):
                for vsl in range(2):
                    psv = ps_big[:, vsl * 512:(vsl + 1) * 512]
                    for c in range(NC):
                        w = [("v", v_evac[vsl]), ("st", st_v)] if c == 0 else []
                        tmm = p.t(lambda e, c=c, ti=ti, vsl=vsl, psv=psv, part=part: e.matmul(
                            psv, h1T_sb[:, c, ti * 128:(ti + 1) * 128],
                            wp_sb[:, c, part * D + vsl * 512:part * D + (vsl + 1) * 512],
                            start=(c == 0), stop=(c == NC - 1)),
                            inc="t" if c == NC - 1 else None, waits=w)
                    v_evac[vsl] = p.v(lambda e, psv=psv, vsl=vsl: e.tensor_copy(
                        kv_sb[:, vsl * 512:(vsl + 1) * 512], psv), inc="v", waits=[("t", tmm)])
                vf = p.vflush()
                st_v = p.y(lambda e, ti=ti, dst_h=dst_h: e.dma_start(
                    out=dst_h[ti * 128:(ti + 1) * 128, :], in_=kv_sb),
                    inc="st", waits=[("v", vf)])
        ar.pop()
        ar.pop()
        p.barrier()

        ar.push()
        qh_sb = ar.alloc(BF16, [64, T])
        kh_sb = ar.alloc(BF16, [64, T])
        ktk_sb = ar.alloc(BF16, [128, NT, 64])
        vtk_sb = ar.alloc(BF16, [128, NT, 64])
        msb = ar.alloc(BF16, [64, 64])
        ssb = ar.alloc(BF16, [128, 128])
        ysb = ar.alloc(BF16, [128, 64])
        yT_sb = ar.alloc(BF16, [64, T])
        ps_y = ps_big[:, 0:512]
        ps_s = ps_big[:, 512:1024]
        ps_M = ps_big[:, 1024:1536]
        for h in range(NH):
            p.y(lambda e, h=h: e.dma_start(out=qh_sb, in_=qT_h[h * 64:(h + 1) * 64, :]), inc="ld")
            p.y(lambda e, h=h: e.dma_start(out=kh_sb, in_=kT_h[h * 64:(h + 1) * 64, :]), inc="ld")
            p.y(lambda e, h=h: e.dma_start(
                out=ktk_sb, in_=ktok_h.rearrange("(t p) c -> p t c", p=128)[:, :, h * 64:(h + 1) * 64]),
                inc="ld")
            lq = p.y(lambda e, h=h: e.dma_start(
                out=vtk_sb, in_=vtok_h.rearrange("(t p) c -> p t c", p=128)[:, :, h * 64:(h + 1) * 64]),
                inc="ld")
            t_mup = 0
            v_mcp = 0
            t_yev = 0
            v_yev = 0
            s_yev = 0
            v_ss = 0
            t_intra_prev = 0
            for ci in range(NT):
                w0 = [("ld", lq)] if ci == 0 else []
                if ci > 0:
                    v_mcp = p.v(lambda e: e.tensor_copy(msb, ps_M[0:64, 0:64]), inc="v",
                                waits=[("t", t_mup)])
                    p.t(lambda e, ci=ci: e.matmul(
                        ps_y[0:128, 0:64], qh_sb[:, ci * 128:(ci + 1) * 128], msb,
                        start=True, stop=False), inc="t",
                        waits=[("v", v_mcp), ("s", s_yev)])
                tss = p.t(lambda e, ci=ci: e.matmul(
                    ps_s[:, 0:128], kh_sb[:, ci * 128:(ci + 1) * 128],
                    qh_sb[:, ci * 128:(ci + 1) * 128], start=True, stop=True), inc="t",
                    waits=w0 + [("v", v_ss)])
                v_ss = p.v(lambda e: e.tensor_tensor(ssb, tril_sb, ps_s[:, 0:128], op=ALU.mult),
                           inc="v", waits=[("t", tss), ("t", t_intra_prev)])
                t_intra_prev = p.t(lambda e, ci=ci: e.matmul(
                    ps_y[0:128, 0:64], ssb, vtk_sb[:, ci, :],
                    start=(ci == 0), stop=True), inc="t",
                    waits=[("v", v_ss)] + ([("s", s_yev)] if ci == 0 else []))
                t_mup = p.t(lambda e, ci=ci: e.matmul(
                    ps_M[0:64, 0:64], ktk_sb[:, ci, :], vtk_sb[:, ci, :],
                    start=(ci == 0), stop=(ci == NT - 1)), inc="t",
                    waits=[("v", v_mcp)] if ci > 0 else [])
                s_yev = p.s(lambda e: e.activation(ysb, ps_y[0:128, 0:64], AF.Identity),
                            inc="s", waits=[("t", t_intra_prev), ("t", t_yev)])
                jj = ci % 8
                t_yev = p.t(lambda e, jj=jj: e.transpose(
                    psT[0:64, jj * 128:(jj + 1) * 128], ysb, ident_sb), inc="t",
                    waits=[("s", s_yev)] + ([("v", v_yev)] if jj == 0 else []))
                if jj == 7:
                    dst = yT_sb[:, (ci - 7) * 128:(ci + 1) * 128]
                    v_yev = p.v(lambda e, dst=dst: e.tensor_copy(dst, psT[0:64, :]),
                                inc="v", waits=[("t", t_yev)])
            vf = p.vflush()
            st_yT = p.y(lambda e, h=h: e.dma_start(out=yT_h[h * 64:(h + 1) * 64, :], in_=yT_sb),
                        inc="st", waits=[("v", vf)])
            p.pending["sync"].append(("t", p.now()["t"]))
            p.pending["vector"].append(("st", st_yT))
        ar.pop()
        p.barrier()

        ar.push()
        h2T_sb = ar.alloc(BF16, [128, NC, T])
        ar.push()
        wo_sb = ar.alloc(BF16, [128, NC, D])
        yt_t = [ar.alloc(BF16, [128, NC, 128]) for _ in range(2)]
        l2g_sb = ar.alloc(F32, [128, D])
        l2b_sb = ar.alloc(F32, [128, D])
        xnl_sb = ar.alloc(F32, [128, D])
        p.g(lambda e: e.dma_start(out=wo_sb, in_=wo.rearrange("(c p) e -> p c e", p=128)), inc="lg")
        p.y(lambda e: e.dma_start(out=l2g_sb, in_=ln2g), inc="ld")
        lvg = p.y(lambda e: e.dma_start(out=l2b_sb, in_=ln2b), inc="ld")
        p.pending["vector"].append(("ld", lvg))
        p.barrier()
        t_cons = [0, 0]

        def src_x1(i, dst, ws):
            bi = i % 2
            p.y(lambda e, i=i, bi=bi: e.dma_start(
                out=yt_t[bi], in_=yT_h.rearrange("(c p) t -> p c t", p=128)[:, :, i * 128:(i + 1) * 128]),
                inc="ld", waits=[("t", t_cons[bi])])
            lv2 = p.y(lambda e, i=i: e.dma_start(out=xnl_sb, in_=xn_h[i * 128:(i + 1) * 128, :]),
                      inc="ld", waits=ws)
            v_ev = 0
            for esl in range(2):
                pso = ps_big[:, esl * 512:(esl + 1) * 512]
                for c in range(NC):
                    w = [("ld", lv2), ("v", v_ev)] if c == 0 else []
                    tmm = p.t(lambda e, c=c, esl=esl, bi=bi, pso=pso: e.matmul(
                        pso, yt_t[bi][:, c, :], wo_sb[:, c, esl * 512:(esl + 1) * 512],
                        start=(c == 0), stop=(c == NC - 1)),
                        inc="t" if c == NC - 1 else None, waits=w)
                t_cons[bi] = tmm
                sl = slice(esl * 512, (esl + 1) * 512)
                v_ev = p.v(lambda e, pso=pso, sl=sl, dst=dst: e.tensor_tensor(
                    dst[:, sl], pso, xnl_sb[:, sl], op=ALU.add), inc="v", waits=[("t", tmm)])
            return (lambda e: e.nop()), [("v", v_ev)]

        emit_ln_tokens(src_x1, None, l2g_sb, l2b_sb, True, h2T_sb, src_store_h=x1_h)
        ar.pop()
        p.barrier()
        p.y(lambda e: e.dma_start(out=hT_hs[0].rearrange("(c p) t -> p c t", p=128), in_=h2T_sb),
            inc="st")
        ar.pop()
        p.barrier()

        ar.push()
        w1_sb = ar.alloc(BF16, [128, NC, HID])
        w2_sb = ar.alloc(BF16, [128, NHID, D])
        b1_sb = ar.alloc(F32, [128, NHID])
        b2_sb = ar.alloc(F32, [128, NC])
        TB = 256
        NTB = T // TB
        hblk = ar.alloc(BF16, [128, NC, TB])
        asb = [ar.alloc(BF16, [128, TB]) for _ in range(2)]
        outev = ar.alloc(BF16, [128, NC, TB])
        st_o = 0
        v_oev = 0
        s_aev = [0, 0]
        t_out2 = 0
        for lvl in range(3):
            src_hT = hT_hs[lvl]
            dst_hT = hT_hs[lvl + 1]
            p.g(lambda e, lvl=lvl: e.dma_start(
                out=w1_sb, in_=cw1[lvl].rearrange("(c p) e -> p c e", p=128)), inc="lg",
                waits=[("t", p.now()["t"])])
            p.g(lambda e, lvl=lvl: e.dma_start(
                out=w2_sb, in_=cw2[lvl].rearrange("(c p) e -> p c e", p=128)), inc="lg")
            p.y(lambda e, lvl=lvl: e.dma_start(out=b1_sb, in_=cb1[lvl]), inc="ld")
            p.y(lambda e, lvl=lvl: e.dma_start(out=b2_sb, in_=cb2[lvl]), inc="ld")
            p.barrier()
            for tb in range(NTB):
                tsl = slice(tb * TB, (tb + 1) * TB)
                lv = p.y(lambda e, tsl=tsl, src_hT=src_hT: e.dma_start(
                    out=hblk, in_=src_hT.rearrange("(c p) t -> p c t", p=128)[:, :, tsl]),
                    inc="ld", waits=[("t", p.now()["t"])])
                for ht in range(NHID):
                    hb = ht % 2
                    pa = ps_big[:, 2048 + hb * 512: 2048 + hb * 512 + TB]
                    for c in range(NC):
                        w = []
                        if c == 0:
                            w = [("ld", lv), ("s", s_aev[hb])]
                        tmm = p.t(lambda e, c=c, ht=ht, pa=pa: e.matmul(
                            pa, w1_sb[:, c, ht * 128:(ht + 1) * 128], hblk[:, c, :],
                            start=(c == 0), stop=(c == NC - 1)),
                            inc="t" if c == NC - 1 else None, waits=w)
                    w = [("t", tmm)]
                    if t_out2:
                        w.append(("t", t_out2))
                    s_aev[hb] = p.s(lambda e, ht=ht, pa=pa, hb=hb: e.activation(
                        asb[hb], pa, AF.Gelu_apprx_tanh, bias=b1_sb[:, ht:ht + 1]),
                        inc="s", waits=w)
                    for dt in range(NC):
                        po = ps_big[:, dt * 256:(dt + 1) * 256]
                        w = []
                        if dt == 0:
                            w = [("s", s_aev[hb])]
                            if ht == 0:
                                w.append(("v", v_oev))
                        t_out2 = p.t(lambda e, ht=ht, dt=dt, po=po, hb=hb: e.matmul(
                            po, w2_sb[:, ht, dt * 128:(dt + 1) * 128], asb[hb],
                            start=(ht == 0 and dt % 2 == 0), stop=(ht == NHID - 1)),
                            inc="t" if dt == NC - 1 else None, waits=w)
                for dt in range(NC):
                    po = ps_big[:, dt * 256:(dt + 1) * 256]
                    v_oev = p.v(lambda e, dt=dt, po=po: e.tensor_scalar(
                        outev[:, dt, :], po, b2_sb[:, dt:dt + 1], None, op0=ALU.add),
                        inc="v" if dt == NC - 1 else None,
                        waits=[("t", t_out2), ("st", st_o)] if dt == 0 else [])
                vf = p.vflush()
                st_o = p.y(lambda e, tsl=tsl, dst_hT=dst_hT: e.dma_start(
                    out=dst_hT.rearrange("(c p) t -> p c t", p=128)[:, :, tsl],
                    in_=outev), inc="st", waits=[("v", vf)])
            p.barrier()
        ar.pop()
        p.barrier()

        ar.push()
        wr_sb = ar.alloc(BF16, [128, NC, D])
        wk2_sb = ar.alloc(BF16, [128, NC, D])
        wv2_sb = ar.alloc(BF16, [128, NC, D])
        mur_sb = ar.alloc(F32, [128, D])
        muk_sb = ar.alloc(F32, [128, D])
        omur_sb = ar.alloc(F32, [128, D])
        omuk_sb = ar.alloc(F32, [128, D])
        xn_t = ar.alloc(F32, [128, D])
        xp_t = ar.alloc(F32, [128, D])
        xr_t = ar.alloc(F32, [128, D])
        xk_t = ar.alloc(F32, [128, D])
        tmp_t = ar.alloc(F32, [128, D])
        mixb = ar.alloc(BF16, [128, D])
        mixT = ar.alloc(BF16, [128, D])
        r_f = xr_t
        krel_f = xk_t
        kact = ar.alloc(BF16, [128, D])
        kaT = ar.alloc(BF16, [128, D])
        rk_f = tmp_t
        g_sb = ar.alloc(BF16, [128, D])
        h3t = [ar.alloc(BF16, [128, NC, 128]) for _ in range(2)]
        h3tok = ar.alloc(F32, [128, D])
        s_f = ar.alloc(F32, [128, D])
        s_bf = ar.alloc(BF16, [128, D])
        x_tl = ar.alloc(F32, [128, D])
        sg_f = xn_t
        xo_f = xp_t
        p.g(lambda e: e.dma_start(out=wr_sb, in_=wrT.rearrange("(c p) e -> p c e", p=128)), inc="lg")
        p.g(lambda e: e.dma_start(out=wk2_sb, in_=wkT.rearrange("(c p) e -> p c e", p=128)), inc="lg")
        p.g(lambda e: e.dma_start(out=wv2_sb, in_=wvT.rearrange("(c p) e -> p c e", p=128)), inc="lg")
        p.y(lambda e: e.dma_start(out=mur_sb, in_=mur), inc="ld")
        lvm = p.y(lambda e: e.dma_start(out=muk_sb, in_=muk), inc="ld")
        p.barrier()
        p.v(lambda e: e.tensor_scalar(omur_sb, mur_sb, -1.0, 1.0, op0=ALU.mult, op1=ALU.add),
            waits=[("ld", lvm)])
        p.v(lambda e: e.tensor_scalar(omuk_sb, muk_sb, -1.0, 1.0, op0=ALU.mult, op1=ALU.add))
        p.barrier()
        for ti in range(NT):
            p.barrier()
            p.y(lambda e, ti=ti: e.dma_start(out=xn_t, in_=xn_h[ti * 128:(ti + 1) * 128, :]),
                inc="ld")
            if ti == 0:
                p.v(lambda e: e.memset(xp_t[0:1, :], 0.0))
                p.y(lambda e: e.dma_start(out=xp_t[1:128, :], in_=xn_h[0:127, :]), inc="ld")
            else:
                p.y(lambda e, ti=ti: e.dma_start(
                    out=xp_t, in_=xn_h[ti * 128 - 1:(ti + 1) * 128 - 1, :]), inc="ld")
            p.y(lambda e, ti=ti: e.dma_start(
                out=h3t[ti % 2], in_=hT_hs[3].rearrange("(c p) t -> p c t", p=128)[:, :, ti * 128:(ti + 1) * 128]),
                inc="ld")
            p.y(lambda e, ti=ti: e.dma_start(out=x_tl, in_=x_in[ti * 128:(ti + 1) * 128, :]),
                inc="ld")
            lv = p.y(lambda e, ti=ti: e.dma_start(out=s_f, in_=x1_h[ti * 128:(ti + 1) * 128, :]),
                     inc="ld")
            p.pending["vector"].append(("ld", lv))
            p.pending["tensor"].append(("ld", lv))
            p.v(lambda e: e.tensor_tensor(xr_t, mur_sb, xn_t, op=ALU.mult))
            p.v(lambda e: e.tensor_tensor(tmp_t, omur_sb, xp_t, op=ALU.mult))
            p.v(lambda e: e.tensor_tensor(xr_t, xr_t, tmp_t, op=ALU.add))
            p.v(lambda e: e.tensor_tensor(xk_t, muk_sb, xn_t, op=ALU.mult))
            p.v(lambda e: e.tensor_tensor(tmp_t, omuk_sb, xp_t, op=ALU.mult))
            p.v(lambda e: e.tensor_tensor(xk_t, xk_t, tmp_t, op=ALU.add))
            s_used = 0
            for src, wsb, dst, act in [(xr_t, wr_sb, r_f, AF.Sigmoid),
                                       (xk_t, wk2_sb, krel_f, AF.Relu)]:
                vq = p.v(lambda e, src=src: e.tensor_copy(mixb, src), inc="v")
                for j in range(NC):
                    w = [("v", vq)] if j == 0 else []
                    ttr = p.t(lambda e, j=j: e.transpose(
                        psT[:, j * 128:(j + 1) * 128], mixb[:, j * 128:(j + 1) * 128], ident_sb),
                        inc="t" if j == NC - 1 else None, waits=w)
                vx = p.v(lambda e: e.tensor_copy(mixT, psT[:]), inc="v", waits=[("t", ttr)])
                for esl in range(2):
                    pso = ps_big[:, 1024 + esl * 512:1024 + (esl + 1) * 512]
                    for c in range(NC):
                        w = [("v", vx), ("s", s_used)] if c == 0 else []
                        tmm = p.t(lambda e, c=c, esl=esl, wsb=wsb, pso=pso: e.matmul(
                            pso, mixT[:, c * 128:(c + 1) * 128],
                            wsb[:, c, esl * 512:(esl + 1) * 512],
                            start=(c == 0), stop=(c == NC - 1)),
                            inc="t" if c == NC - 1 else None, waits=w)
                    s_used = p.s(lambda e, esl=esl, dst=dst, act=act, pso=pso: e.activation(
                        dst[:, esl * 512:(esl + 1) * 512], pso, act),
                        inc="s", waits=[("t", tmm)])
            vq = p.v(lambda e: e.tensor_tensor(kact, krel_f, krel_f, op=ALU.mult), inc="v",
                     waits=[("s", s_used)])
            for j in range(NC):
                w = [("v", vq)] if j == 0 else []
                ttr = p.t(lambda e, j=j: e.transpose(
                    psT[:, j * 128:(j + 1) * 128], kact[:, j * 128:(j + 1) * 128], ident_sb),
                    inc="t" if j == NC - 1 else None, waits=w)
            vx = p.v(lambda e: e.tensor_copy(kaT, psT[:]), inc="v", waits=[("t", ttr)])
            v_used = 0
            for esl in range(2):
                pso = ps_big[:, 1024 + esl * 512:1024 + (esl + 1) * 512]
                for c in range(NC):
                    w = [("v", vx), ("s", s_used)] if c == 0 else []
                    tmm = p.t(lambda e, c=c, esl=esl, pso=pso: e.matmul(
                        pso, kaT[:, c * 128:(c + 1) * 128], wv2_sb[:, c, esl * 512:(esl + 1) * 512],
                        start=(c == 0), stop=(c == NC - 1)),
                        inc="t" if c == NC - 1 else None, waits=w)
                v_used = p.v(lambda e, esl=esl, pso=pso: e.tensor_tensor(
                    rk_f[:, esl * 512:(esl + 1) * 512], pso, r_f[:, esl * 512:(esl + 1) * 512],
                    op=ALU.mult), inc="v", waits=[("t", tmm)])
            s_g = p.s(lambda e: e.activation(g_sb, rk_f, AF.Silu), inc="s", waits=[("v", v_used)])
            for j in range(NC):
                ttr = p.t(lambda e, j=j, ti=ti: e.transpose(
                    psT[:, j * 128:(j + 1) * 128], h3t[ti % 2][:, j, :], ident_sb),
                    inc="t" if j == NC - 1 else None)
            p.v(lambda e: e.tensor_copy(h3tok, psT[:]), waits=[("t", ttr)])
            p.v(lambda e: e.tensor_tensor(s_f, s_f, h3tok, op=ALU.add))
            p.v(lambda e: e.tensor_copy(s_bf, s_f))
            p.v(lambda e: e.tensor_tensor(sg_f, s_bf, g_sb, op=ALU.mult), waits=[("s", s_g)])
            v_done = p.v(lambda e: e.tensor_tensor(xo_f, x_tl, sg_f, op=ALU.add), inc="v")
            if DEBUG:
                p.y(lambda e, ti=ti: e.dma_start(out=g3_h[ti * 128:(ti + 1) * 128, :], in_=g_sb),
                    inc="st", waits=[("s", s_g)])
            vf = p.vflush()
            p.y(lambda e, ti=ti: e.dma_start(out=x_out[ti * 128:(ti + 1) * 128, :], in_=xo_f),
                inc="st", waits=[("v", vf)])
        ar.pop()
        ar.pop()
        LAST_PEAK.update({str(k): v for k, v in ar.peak.items()})


def _setup_ctx(nc, p, bf_cols=70500, f32_cols=15700):
    stk = ExitStack()
    BFA = stk.enter_context(nc.sbuf_tensor("bf_arena", [128, bf_cols], BF16))
    F32A = stk.enter_context(nc.sbuf_tensor("f32_arena", [128, f32_cols], F32))
    ar = Arena({BF16: BFA, F32: F32A})
    ps_big = stk.enter_context(nc.psum_tensor("ps_big", [128, 3072], F32))
    psT = stk.enter_context(nc.psum_tensor("psT", [128, 1024], BF16))
    ident_in = nc.dram_tensor("ident", [128, 128], BF16, kind="ExternalInput").ap()
    ones64_in = nc.dram_tensor("ones64", [1, 64], BF16, kind="ExternalInput").ap()
    ident_sb = ar.alloc(BF16, [128, 128])
    ones64_sb = ar.alloc(BF16, [1, 64])
    spc = ar.alloc(F32, [128, 128])
    p.v(lambda e: e.memset(spc, 0.0))
    p.spacer = lambda: p.v(lambda e: e.tensor_copy(spc[:, 0:64], spc[:, 64:128]))

    def vflush():
        p.v(lambda e: e.tensor_copy(spc[:, 0:64], spc[:, 64:128]))
        return p.v(lambda e: e.tensor_copy(spc[:, 0:64], spc[:, 64:128]), inc="v")
    p.vflush = vflush
    p.y(lambda e: e.dma_start(out=ident_sb, in_=ident_in), inc="ld")
    p.y(lambda e: e.dma_start(out=ones64_sb, in_=ones64_in), inc="ld")
    return stk, ar, ps_big, psT, ident_sb, ones64_sb


def build_megakernel():
    nc = bass.Bass("TRN2", target_bir_lowering=False)
    p = Prog()
    x0 = nc.dram_tensor("x", [T, D], F32, kind="ExternalInput").ap()
    xo = nc.dram_tensor("x_out", [T, D], F32, kind="ExternalOutput").ap()
    xa = nc.dram_tensor("xmid_a", [T, D], F32, kind="Internal").ap()
    xb = nc.dram_tensor("xmid_b", [T, D], F32, kind="Internal").ap()
    stk, ar, ps_big, psT, ident_sb, ones64_sb = _setup_ctx(nc, p)
    with stk:
        chain = [(0, x0, xa), (1, xa, xb), (2, xb, xa), (3, xa, xo)]
        for layer, xi, xout in chain:
            if layer < 3:
                emit_attn_layer(nc, p, ar, ps_big, psT, ident_sb, ones64_sb,
                                layer, f"l{layer}_", xi, xout)
            else:
                emit_l3(nc, p, ar, ps_big, psT, ident_sb, f"l{layer}_", xi, xout)
            p.barrier()
        p.y(lambda e: e.nop(), waits=[("st", p.now()["st"])])
        p.build(nc)
    return nc


def build_attn_layer(layer):
    nc = bass.Bass("TRN2", target_bir_lowering=False)
    p = Prog()
    x0 = nc.dram_tensor("x", [T, D], F32, kind="ExternalInput").ap()
    xo = nc.dram_tensor("x_out", [T, D], F32, kind="ExternalOutput").ap()
    stk, ar, ps_big, psT, ident_sb, ones64_sb = _setup_ctx(nc, p)
    with stk:
        emit_attn_layer(nc, p, ar, ps_big, psT, ident_sb, ones64_sb, layer, "", x0, xo)
        p.y(lambda e: e.nop(), waits=[("st", p.now()["st"])])
        p.build(nc)
    return nc


def build_l3():
    nc = bass.Bass("TRN2", target_bir_lowering=False)
    p = Prog()
    x0 = nc.dram_tensor("x", [T, D], F32, kind="ExternalInput").ap()
    xo = nc.dram_tensor("x_out", [T, D], F32, kind="ExternalOutput").ap()
    stk, ar, ps_big, psT, ident_sb, ones64_sb = _setup_ctx(nc, p)
    with stk:
        emit_l3(nc, p, ar, ps_big, psT, ident_sb, "", x0, xo)
        p.y(lambda e: e.nop(), waits=[("st", p.now()["st"])])
        p.build(nc)
    return nc


_CONST_CACHE = {}


def host_consts():
    if _CONST_CACHE:
        return _CONST_CACHE
    c = {}
    c["ident"] = np.eye(128, dtype=ml_dtypes.bfloat16)
    c["ones64"] = np.ones((1, 64), dtype=ml_dtypes.bfloat16)
    freqs = (np.float32(10000.0) ** (np.arange(0, HD, 2.0, dtype=np.float32) /
                                     np.float32(HD))).astype(np.float32)
    t = np.arange(T, dtype=np.float32)
    ang = (t[:, None] * freqs[None, :]).astype(np.float32)
    cos = np.cos(ang.astype(np.float64)).astype(np.float32)
    sin = np.sin(ang.astype(np.float64)).astype(np.float32)
    A = np.empty((64, T), np.float32)
    B = np.empty((64, T), np.float32)
    for pm in range(64):
        j = pm % 32
        A[pm] = cos[:, j]
        B[pm] = sin[:, j] if pm >= 32 else -sin[:, j]
    c["rotA64"] = A.astype(ml_dtypes.bfloat16)
    c["rotB64"] = B.astype(ml_dtypes.bfloat16)
    cm = np.zeros((4, 128, 512), np.float32)
    for r in range(4):
        for kk in range(128):
            cm[r, kk, kk + r * 128:] = 1.0
    c["cmask"] = cm.astype(ml_dtypes.bfloat16)
    tr = np.zeros((128, 128), np.float32)
    for e in range(128):
        tr[e, e:] = 1.0
    c["trilm"] = tr.astype(ml_dtypes.bfloat16)
    _CONST_CACHE.update(c)
    return c


def rotary_perm_idx():
    idx = np.arange(3 * D)
    for part in range(2):
        for h in range(NH):
            base = part * D + h * HD
            ev = base + np.arange(0, HD, 2)
            od = base + np.arange(1, HD, 2)
            idx[base:base + 32] = ev
            idx[base + 32:base + 64] = od
    return idx


def layer_inputs(layer, ins):
    c = host_consts()
    d = {}
    if layer == 0:
        idx = rotary_perm_idx()
        w = np.ascontiguousarray(ins["L0_attn_qkv_w"][:, idx], dtype=np.float32)
        b = np.ascontiguousarray(np.asarray(ins["L0_attn_qkv_b"], dtype=np.float32)[idx])
        d["w_o"] = np.ascontiguousarray(ins["L0_attn_out_w"], dtype=np.float32)
        bo = np.asarray(ins["L0_attn_out_b"], dtype=np.float32)
        d["rotA64"] = c["rotA64"]
        d["rotB64"] = c["rotB64"]
        pref = "L0"
    elif layer == 1:
        w = np.ascontiguousarray(np.asarray(ins["L1_attn_w_qkv"], dtype=np.float32).T)
        b = np.asarray(ins["L1_attn_b_qkv"], dtype=np.float32)
        d["w_o"] = np.ascontiguousarray(np.asarray(ins["L1_attn_w_o"], dtype=np.float32).T)
        bo = np.asarray(ins["L1_attn_b_o"], dtype=np.float32)
        pref = "L1"
    else:
        w = np.ascontiguousarray(np.asarray(ins["L2_attn_w_in"], dtype=np.float32).T)
        b = np.asarray(ins["L2_attn_b_in"], dtype=np.float32)
        d["w_o"] = np.ascontiguousarray(np.asarray(ins["L2_attn_w_out"], dtype=np.float32).T)
        bo = np.asarray(ins["L2_attn_b_out"], dtype=np.float32)
        d["cmask"] = c["cmask"]
        pref = "L2"
    d["w_qkv"] = w
    d["b_pc"] = np.ascontiguousarray(b.reshape(24, 128).T)
    d["bv_bc"] = np.ascontiguousarray(np.broadcast_to(b[2 * D:], (128, D)))
    d["bo_bc"] = np.ascontiguousarray(np.broadcast_to(bo, (128, D)))
    d["w_gate"] = np.ascontiguousarray(ins[f"{pref}_ffn_w_gate"], dtype=np.float32)
    d["w_up"] = np.ascontiguousarray(ins[f"{pref}_ffn_w_up"], dtype=np.float32)
    d["w_down"] = np.ascontiguousarray(ins[f"{pref}_ffn_w_down"], dtype=np.float32)
    return d


def l3_inputs(ins):
    c = host_consts()
    f = np.float32
    d = {"trilm": c["trilm"]}
    for k, src in [("wq", "L3_attn_wq"), ("wk", "L3_attn_wk"), ("wv", "L3_attn_wv"),
                   ("wo", "L3_attn_wo")]:
        d[k] = np.ascontiguousarray(np.asarray(ins[src], dtype=f))
    for k, src in [("ln1g", "L3_attn_ln1_g"), ("ln1b", "L3_attn_ln1_b"),
                   ("ln2g", "L3_attn_ln2_g"), ("ln2b", "L3_attn_ln2_b")]:
        d[k] = np.ascontiguousarray(np.broadcast_to(np.asarray(ins[src], dtype=f), (128, D)))
    d["cw1"] = np.ascontiguousarray(np.asarray(ins["L3_attn_cms_w1"], dtype=f))
    d["cw2"] = np.ascontiguousarray(np.asarray(ins["L3_attn_cms_w2"], dtype=f))
    b1 = np.asarray(ins["L3_attn_cms_b1"], dtype=f)
    b2 = np.asarray(ins["L3_attn_cms_b2"], dtype=f)
    d["cb1"] = np.ascontiguousarray(np.stack([b1[l].reshape(32, 128).T for l in range(3)]))
    d["cb2"] = np.ascontiguousarray(np.stack([b2[l].reshape(8, 128).T for l in range(3)]))
    for k, src in [("wrT", "L3_ffn_W_r"), ("wkT", "L3_ffn_W_k"), ("wvT", "L3_ffn_W_v")]:
        d[k] = np.ascontiguousarray(np.asarray(ins[src], dtype=f).T)
    d["mur"] = np.ascontiguousarray(np.broadcast_to(np.asarray(ins["L3_ffn_mu_r"], dtype=f), (128, D)))
    d["muk"] = np.ascontiguousarray(np.broadcast_to(np.asarray(ins["L3_ffn_mu_k"], dtype=f), (128, D)))
    return d


def kernel(**ins):
    x = np.ascontiguousarray(np.asarray(ins["L0_attn_x"], dtype=np.float32))
    B = x.shape[0]
    c = host_consts()
    nc = build_megakernel()
    base = {"ident": c["ident"], "ones64": c["ones64"]}
    for layer in range(4):
        lid = layer_inputs(layer, ins) if layer < 3 else l3_inputs(ins)
        for k, v in lid.items():
            base[f"l{layer}_{k}"] = v
    in_maps = [dict(base, x=np.ascontiguousarray(x[b])) for b in range(B)]
    res = run_bass_kernel_spmd(nc, in_maps, core_ids=list(range(B)))
    return np.stack([np.asarray(res.results[b]["x_out"]) for b in range(B)]).astype(np.float32)
